# version 12
# baseline (speedup 1.0000x reference)
import contextlib
import numpy as np
import ml_dtypes
import concourse.bass as bass
import concourse.mybir as mybir
from concourse.ap import AP
from concourse.bass_utils import run_bass_kernel_spmd

F32 = mybir.dt.float32
BF16 = mybir.dt.bfloat16
AF = mybir.ActivationFunctionType
ALU = mybir.AluOpType
AX = mybir.AxisListType
bfloat16 = ml_dtypes.bfloat16

DIM = 2048
NCORES = 8
LAM_INIT = 0.8 - 0.6 * float(np.exp(-3.6))
OML = 1.0 - LAM_INIT
LN_OML = float(np.log(OML))
EPS = 1e-5


def _tchunks(T):
    return [(c, min(512, T - c)) for c in range(0, T, 512)]


def build(nt=16, taps=()):
    T = nt * 128
    TCH = _tchunks(T)
    nc = bass.Bass("TRN2", target_bir_lowering=False, debug=False,
                   detect_race_conditions=False)

    x_d = nc.dram_tensor("x", [T, DIM], F32, kind="ExternalInput").ap()
    wqT_d = nc.dram_tensor("wqT", [DIM, DIM], F32, kind="ExternalInput").ap()
    wkT_d = nc.dram_tensor("wkT", [DIM, DIM], F32, kind="ExternalInput").ap()
    wvT_d = nc.dram_tensor("wvT", [DIM, DIM], F32, kind="ExternalInput").ap()
    woT_d = nc.dram_tensor("woT", [DIM, DIM], F32, kind="ExternalInput").ap()
    gwT_d = nc.dram_tensor("gwT", [DIM, 9], F32, kind="ExternalInput").ap()
    upg_d = nc.dram_tensor("upg", [10, DIM, 1024], F32, kind="ExternalInput").ap()
    upu_d = nc.dram_tensor("upu", [10, DIM, 1024], F32, kind="ExternalInput").ap()
    dnw_d = nc.dram_tensor("dnw", [10, 1024, DIM], F32, kind="ExternalInput").ap()
    lama_d = nc.dram_tensor("lama", [1, 128], F32, kind="ExternalInput").ap()
    lamb_d = nc.dram_tensor("lamb", [1, 128], F32, kind="ExternalInput").ap()
    bo_d = nc.dram_tensor("bo", [1, DIM], F32, kind="ExternalInput").ap()
    id_d = nc.dram_tensor("ident", [128, 128], F32, kind="ExternalInput").ap()
    c64_d = nc.dram_tensor("c64", [1, 4], F32, kind="ExternalInput").ap()
    y_d = nc.dram_tensor("y", [T, DIM], F32, kind="ExternalOutput").ap()

    h_dram = nc.dram_tensor("h_dram", [T, DIM], BF16, kind="Internal").ap()
    q_dram = nc.dram_tensor("q_dram", [T, DIM], BF16, kind="Internal").ap()
    k_dram = nc.dram_tensor("k_dram", [T, DIM], BF16, kind="Internal").ap()
    v_dram = nc.dram_tensor("v_dram", [T, DIM], BF16, kind="Internal").ap()
    ltd = [nc.dram_tensor(f"ltd{i}", [2048, 128], BF16, kind="Internal").ap()
           for i in range(4)]
    v2_dram = nc.dram_tensor("v2_dram", [T, DIM], BF16, kind="Internal").ap()
    o2p_dram = nc.dram_tensor("o2p_dram", [T, DIM], BF16, kind="Internal").ap()
    o2_dram = nc.dram_tensor("o2_dram", [T, DIM], BF16, kind="Internal").ap()
    x1_dram = nc.dram_tensor("x1_dram", [T, DIM], F32, kind="Internal").ap()
    h2_dram = nc.dram_tensor("h2_dram", [T, DIM], BF16, kind="Internal").ap()
    acc_dram = nc.dram_tensor("acc_dram", [T, DIM], F32, kind="Internal").ap()
    scratch = {"h_dram": h_dram, "q_dram": q_dram, "k_dram": k_dram,
               "v_dram": v_dram, "o2_dram": o2_dram, "x1_dram": x1_dram,
               "h2_dram": h2_dram, "acc_dram": acc_dram,
               "ltd0": ltd[0], "ltd1": ltd[1]}
    tap_outs = {}
    for name in taps:
        src = scratch[name]
        tap_outs[name] = nc.dram_tensor(
            "tap_" + name, list(src.shape), src.dtype, kind="ExternalOutput").ap()

    es = contextlib.ExitStack()
    sems = {}
    C = {}

    def mksem(name):
        sems[name] = es.enter_context(nc.semaphore(name))
        C[name] = 0

    def inc(instr, name, k=1):
        instr.then_inc(sems[name], k)
        C[name] += k

    def W(eng, name, val=None):
        eng.wait_ge(sems[name], C[name] if val is None else val)

    def fence(eng, instr):
        inc(instr, "sFEN")
        eng.wait_ge(sems["sFEN"], C["sFEN"])
        return instr

    for s in ["sXL0", "sXL1", "sLN", "sHD0", "sHD1", "sHFM", "sW0", "sW1",
              "sQMM", "sEV", "sQKVD0", "sQKVD1",
              "sRP0", "sRP1", "sRPK0", "sRPK1",
              "sATT", "sEXP", "sSMX", "sD", "sDD0", "sDD1",
              "sLT20", "sLT21", "sLT22", "sLT23",
              "sVST", "sM2", "sRMS", "sO2", "sO2D", "sO2P",
              "sVSTL0", "sVSTL1", "sVSTL2", "sVSTL3",
              "sWOT", "sOFM", "sX1", "sX1D0", "sX1D1",
              "sH2", "sH2D0", "sH2D1", "sH2FM", "sGW", "sSCM", "sGTV", "sCMB",
              "sCR", "sCB", "sCBE", "sUW0", "sUW1", "sUW2", "sUW3",
              "sUWF0", "sUWF1", "sUWF2", "sUWF3", "sUCP",
              "sUMM", "sUPE", "sHID", "sDW0", "sDW1", "sDW2", "sDW3", "sDMM",
              "sDNE", "sACC0", "sACC1", "sFB0", "sFB1", "sFIN", "sY0", "sY1",
              "sMISC", "sLAM", "sFEN"]:
        mksem(s)
    for g_ in range(4):
        mksem(f"sHFM{g_}")
    for j_ in range(nt):
        mksem(f"sHF{j_}")
    ngrp = 4 if nt % 4 == 0 and nt >= 8 else 2
    gsz = nt // ngrp

    hist = {}

    def snap(key, idx, semname):
        hist.setdefault(key, {})[idx] = C[semname]

    def hget(key, idx, default=0):
        return hist.get(key, {}).get(idx, default)

    with es:
        blk = es.enter_context(nc.Block())

        ones_f = es.enter_context(nc.sbuf_tensor("ones_f", [1, 128], F32))
        ones_b = es.enter_context(nc.sbuf_tensor("ones_b", [1, 128], BF16))
        ident = es.enter_context(nc.sbuf_tensor("ident_sb", [128, 128], F32))
        lam_sb = es.enter_context(nc.sbuf_tensor("lam_sb", [1, 256], F32))
        lamt = es.enter_context(nc.sbuf_tensor("lamt", [1, 256], F32))
        neglam_b = es.enter_context(nc.sbuf_tensor("neglam_b", [128, 1], F32))
        lnoml_b = es.enter_context(nc.sbuf_tensor("lnoml_b", [128, 1], F32))
        cst = es.enter_context(nc.sbuf_tensor("cst", [128, 2], F32))
        nc.const_aps.register(F32, EPS, cst[:, 0:1])
        nc.const_aps.register(F32, LN_OML, cst[:, 1:2])
        scrB = es.enter_context(nc.sbuf_tensor("scrB", [128, 512], F32))
        stat = es.enter_context(nc.sbuf_tensor("stat", [128, 16], F32))
        stg = [es.enter_context(nc.sbuf_tensor(f"stg{i}", [128, 512], BF16))
               for i in range(2)]
        psb = [es.enter_context(nc.psum_tensor(f"psb{i}", [128, 512], F32))
               for i in range(4)]
        psE1 = es.enter_context(nc.psum_tensor("psE1", [128, 512], F32))
        psE2 = es.enter_context(nc.psum_tensor("psE2", [128, 512], F32))
        psM = [es.enter_context(nc.psum_tensor(f"psM{i}", [128, 512], F32))
               for i in range(2)]

        z0 = contextlib.ExitStack()
        with z0:
            zz = z0.enter_context(nc.sbuf_tensor("zz", [128, 512], BF16))

            @blk.gpsimd
            def _(g):
                inc(g.dma_start(lam_sb[:, 0:128], lama_d[:, :]), "sMISC", 16)
                inc(g.dma_start(lam_sb[:, 128:256], lamb_d[:, :]), "sMISC", 16)
                inc(g.dma_start(ident[:], id_d[:, :]), "sMISC", 16)
                inc(g.dma_start(lamt[:, 140:144], c64_d[:, :]), "sMISC", 16)

            @blk.vector
            def _(v):
                inc(v.memset(ones_f[:], 1.0), "sLAM")
                inc(v.memset(ones_b[:], 1.0), "sLAM")
                inc(v.memset(zz[:], 0.0), "sLAM")
                inc(v.memset(cst[:, 0:1], EPS), "sLAM")
                inc(v.memset(cst[:, 1:2], LN_OML), "sLAM")
                W(v, "sMISC")
                fence(v, v.tensor_tensor(lamt[:, 0:128], lam_sb[:, 0:128],
                                         lam_sb[:, 128:256], op=ALU.mult))
                inc(v.reduce_sum(lamt[:, 128:130],
                                 lamt[:, 0:128].rearrange("p (a b) -> p a b", a=2),
                                 axis=AX.X), "sLAM")

            @blk.scalar
            def _(sc):
                W(sc, "sLAM")
                inc(sc.activation(lamt[:, 130:132], lamt[:, 128:130], AF.Exp),
                    "sLAM")

            @blk.vector
            def _(v):
                W(v, "sLAM")
                fence(v, v.tensor_tensor(lamt[:, 132:133], lamt[:, 130:131],
                                         lamt[:, 131:132], op=ALU.subtract))
                fence(v, v.tensor_tensor(lamt[:, 134:135], lamt[:, 140:141],
                                         lamt[:, 141:142], op=ALU.add))
                fence(v, v.tensor_tensor(lamt[:, 135:136], lamt[:, 142:143],
                                         lamt[:, 143:144], op=ALU.add))
                fence(v, v.tensor_tensor(lamt[:, 133:134], lamt[:, 132:133],
                                         lamt[:, 134:135], op=ALU.add))
                inc(v.tensor_scalar(lamt[:, 133:134], lamt[:, 133:134], -1.0,
                                    None, op0=ALU.mult), "sLAM")

            @blk.tensor
            def _(t):
                W(t, "sLAM")
                t.matmul(psM[0][:, 0:1], ones_f[:], lamt[:, 133:134],
                         start=True, stop=True)
                inc(t.matmul(psM[0][:, 4:5], ones_f[:], lamt[:, 135:136],
                             start=True, stop=True), "sLAM")

            @blk.vector
            def _(v):
                W(v, "sLAM")
                fence(v, v.tensor_copy(neglam_b[:], psM[0][:, 0:1]))
                inc(v.tensor_copy(lnoml_b[:], psM[0][:, 4:5]), "sLAM")

            @blk.gpsimd
            def _(g):
                W(g, "sLAM")
                for i in range(4):
                    for qtr in range(4):
                        inc(g.dma_start(
                            AP(ltd[i].tensor, 512 * qtr,
                               [[2048, 128], [1, 512]]), zz[:]),
                            "sMISC", 16)
            ltd_init_done = C["sMISC"]

            pa = contextlib.ExitStack()
            paWo = contextlib.ExitStack()
            with pa:
                paW = contextlib.ExitStack()
                scr = paW.enter_context(nc.sbuf_tensor("scr", [128, 2048], F32))
                h_fm = paW.enter_context(
                    nc.sbuf_tensor("h_fm", [128, 16 * T], BF16))
                wbuf = [paW.enter_context(
                    nc.sbuf_tensor(f"wbuf{i}", [128, 8192], BF16))
                    for i in range(2)]

                a1 = contextlib.ExitStack()
                with a1:
                    x_sb = [a1.enter_context(
                        nc.sbuf_tensor(f"x{i}", [128, 2048], F32)) for i in range(2)]
                    h_sb = [a1.enter_context(
                        nc.sbuf_tensor(f"h{i}", [128, 2048], BF16)) for i in range(2)]

                    for j in range(nt):
                        b = j % 2
                        so = 8 * b

                        @blk.gpsimd
                        def _(g, j=j, b=b):
                            if j >= 2:
                                g.wait_ge(sems["sLN"], hget("hact", j - 2))
                            inc(g.dma_start(x_sb[b][:],
                                            x_d[128 * j:128 * j + 128, :]),
                                f"sXL{b}", 16)

                        @blk.vector
                        def _(v, j=j, b=b, so=so):
                            v.wait_ge(sems[f"sXL{b}"], 16 * (j // 2 + 1))
                            if j >= 2:
                                v.wait_ge(sems["sLN"], hget("hact", j - 2))
                            fence(v, v.reduce_sum(stat[:, so:so + 1], x_sb[b][:],
                                                  axis=AX.X))
                            fence(v, v.tensor_scalar(stat[:, so:so + 1],
                                                     stat[:, so:so + 1],
                                                     1.0 / DIM, None,
                                                     op0=ALU.mult))
                            inc(v.scalar_tensor_tensor(
                                scr[:], x_sb[b][:], stat[:, so:so + 1], x_sb[b][:],
                                op0=ALU.subtract, op1=ALU.mult), "sLN")
                            inc(v.reduce_sum(stat[:, so + 1:so + 2], scr[:],
                                             axis=AX.X), "sLN")
                        snap("ln_stats", j, "sLN")

                        @blk.scalar
                        def _(sc, j=j, so=so):
                            sc.wait_ge(sems["sLN"], hget("ln_stats", j))
                            fence(sc, sc.activation(stat[:, so + 2:so + 3],
                                                    stat[:, so + 1:so + 2], AF.Ln,
                                                    bias=EPS, scale=1.0 / DIM))
                            inc(sc.activation(stat[:, so + 2:so + 3],
                                              stat[:, so + 2:so + 3], AF.Exp,
                                              scale=-0.5), "sLN")
                        snap("ln_rstd", j, "sLN")

                        @blk.vector
                        def _(v, j=j, b=b, so=so):
                            v.wait_ge(sems["sLN"], hget("ln_rstd", j))
                            if j >= 2:
                                v.wait_ge(sems[f"sHF{j - 2}"], 16)
                            inc(v.tensor_scalar(h_sb[b][:], x_sb[b][:],
                                                stat[:, so:so + 1],
                                                stat[:, so + 2:so + 3],
                                                op0=ALU.subtract, op1=ALU.mult),
                                "sLN")
                        snap("hact", j, "sLN")

                        @blk.sync
                        def _(sy, j=j, b=b):
                            sy.wait_ge(sems["sLN"], hget("hact", j))
                            inc(sy.dma_start(h_dram[128 * j:128 * j + 128, :],
                                             h_sb[b][:]), f"sHF{j}", 16)

                        if (j + 1) % gsz == 0:
                            grp_ = (j + 1) // gsz - 1
                            r0 = 128 * gsz * grp_
                            nr = 128 * gsz
                            hsem = f"sHFM{grp_}"

                            @blk.sync
                            def _(sy, grp_=grp_, r0=r0, nr=nr, hsem=hsem):
                                for jd in range(grp_ * gsz,
                                                (grp_ + 1) * gsz):
                                    sy.wait_ge(sems[f"sHF{jd}"], 16)
                                if grp_ == 0:
                                    W(sy, "sMISC", ltd_init_done)
                                for kk in range(16):
                                    inc(sy.dma_start(
                                        h_fm[:, T * kk + r0:T * kk + r0 + nr],
                                        h_dram[r0:r0 + nr,
                                               128 * kk:128 * kk + 128],
                                        transpose=True), hsem, 16)

                        if j == min(2, nt - 1):
                            @blk.gpsimd
                            def _(g):
                                src = AP(wqT_d.tensor, 0,
                                         [[DIM, 128], [128 * DIM, 16],
                                          [1, 512]])
                                dst = AP(wbuf[0], 0,
                                         [[8192, 128], [512, 16], [1, 512]])
                                inc(g.dma_start(dst, src), "sW0", 16)
                        if j == min(6, nt - 1):
                            @blk.gpsimd
                            def _(g):
                                src = AP(wqT_d.tensor, 512,
                                         [[DIM, 128], [128 * DIM, 16],
                                          [1, 512]])
                                dst = AP(wbuf[1], 0,
                                         [[8192, 128], [512, 16], [1, 512]])
                                inc(g.dma_start(dst, src), "sW1", 16)

                wts = [wqT_d, wkT_d, wvT_d]
                drams = [q_dram, k_dram, v_dram]
                for wc in range(12):
                    tsr, ch = wc // 4, wc % 4

                    if wc >= 2:
                        @blk.gpsimd
                        def _(g, wc=wc, tsr=tsr, ch=ch):
                            g.wait_ge(sems["sQMM"], (wc - 1) * nt)
                            src = AP(wts[tsr].tensor, 512 * ch,
                                     [[DIM, 128], [128 * DIM, 16], [1, 512]])
                            dst = AP(wbuf[wc % 2], 0,
                                     [[8192, 128], [512, 16], [1, 512]])
                            inc(g.dma_start(dst, src), f"sW{wc % 2}", 16)

                    for j in range(nt):
                        cidx = wc * nt + j

                        @blk.tensor
                        def _(t, wc=wc, j=j, cidx=cidx):
                            t.wait_ge(sems[f"sHFM{min(j // gsz, ngrp - 1)}"],
                                      256)
                            t.wait_ge(sems[f"sW{wc % 2}"], 16 * (wc // 2 + 1))
                            if cidx >= 4:
                                t.wait_ge(sems["sEV"], cidx - 3)
                            pb = psb[cidx % 4]
                            mm = None
                            for kk in range(16):
                                mm = t.matmul(
                                    pb[:],
                                    h_fm[:, T * kk + 128 * j:T * kk + 128 * j + 128],
                                    wbuf[wc % 2][:, 512 * kk:512 * kk + 512],
                                    start=(kk == 0), stop=(kk == 15))
                            inc(mm, "sQMM")

                        @blk.vector
                        def _(v, cidx=cidx):
                            v.wait_ge(sems["sQMM"], cidx + 1)
                            if cidx >= 2:
                                v.wait_ge(sems[f"sQKVD{cidx % 2}"],
                                          16 * ((cidx - 2) // 2 + 1))
                            inc(v.tensor_copy(stg[cidx % 2][:], psb[cidx % 4][:]),
                                "sEV")

                        @blk.sync
                        def _(sy, tsr=tsr, ch=ch, j=j, cidx=cidx):
                            sy.wait_ge(sems["sEV"], cidx + 1)
                            inc(sy.dma_start(
                                drams[tsr][128 * j:128 * j + 128,
                                           512 * ch:512 * ch + 512],
                                stg[cidx % 2][:]), f"sQKVD{cidx % 2}", 16)

                paW.close()
                woT_sb = paWo.enter_context(
                    nc.sbuf_tensor("woT_sb", [128, 16 * DIM], BF16))
                bo_b = paWo.enter_context(nc.sbuf_tensor("bo_b", [1, DIM], BF16))

                lhsT1 = [[[pa.enter_context(
                    nc.sbuf_tensor(f"lhsT1_{p_}_{d}_{hh}", [128, 2048], BF16))
                    for hh in range(2)] for d in range(2)] for p_ in range(2)]
                rhs1 = [[pa.enter_context(
                    nc.sbuf_tensor(f"rhs1_{p_}_{d}", [128, 1024], BF16))
                    for d in range(2)] for p_ in range(2)]
                vst = [pa.enter_context(
                    nc.sbuf_tensor(f"vst{i}", [128, 2048], BF16))
                    for i in range(4)]
                lhsT2 = [pa.enter_context(
                    nc.sbuf_tensor(f"lhsT2_{i}", [128, 2048], BF16))
                    for i in range(4)]
                E1f = pa.enter_context(nc.sbuf_tensor("E1f", [128, 512], F32))
                E2f = pa.enter_context(nc.sbuf_tensor("E2f", [128, 512], F32))
                Ssum = pa.enter_context(nc.sbuf_tensor("Ssum", [128, 64], F32))
                D_sb = [pa.enter_context(
                    nc.sbuf_tensor(f"D_sb{i}", [128, 256], BF16))
                    for i in range(2)]
                o2_sb = pa.enter_context(nc.sbuf_tensor("o2_sb", [128, 2048], BF16))
                rmsd = pa.enter_context(nc.sbuf_tensor("rmsd", [128, 32], F32))
                Rb = pa.enter_context(nc.sbuf_tensor("Rb", [128, 1024], F32))
                QKVD_TOT = (C["sQKVD0"], C["sQKVD1"])
                psEsets = [(psE1, psE2), (psb[2], psb[3])]
                banks4 = [psM[0], psM[1], psb[0], psb[1]]

                @blk.vector
                def _(v):
                    v.wait_ge(sems["sQMM"], 12 * nt)
                    v.wait_ge(sems[f"sHF{nt - 1}"], 16)
                    for p_ in range(2):
                        for dd in range(2):
                            for hh in range(2):
                                inc(v.memset(lhsT1[p_][dd][hh][:], 0.0), "sLAM")
                            inc(v.memset(rhs1[p_][dd][:], 0.0), "sLAM")
                    for i in range(4):
                        inc(v.memset(vst[i][:], 0.0), "sLAM")
                        inc(v.memset(lhsT2[i][:], 0.0), "sLAM")
                lhsT1_zeroed = C["sLAM"]

                def em_repack(t):
                    tp = t % 2

                    @blk.sync
                    def _(sy, t=t, tp=tp):
                        sy.wait_ge(sems["sQKVD0"], QKVD_TOT[0])
                        sy.wait_ge(sems["sQKVD1"], QKVD_TOT[1])
                        if t < 2:
                            sy.wait_ge(sems["sLAM"], lhsT1_zeroed)
                        if t >= 2:
                            sy.wait_ge(sems["sATT"], t - 1)
                        tj = 128 * t
                        for dd in range(2):
                            for hh in range(2):
                                for s in range(4):
                                    dst = AP(lhsT1[tp][dd][hh],
                                             (32 * s) * 2048 + 16 * s,
                                             [[2048, 32], [64, 32], [1, 16]])
                                    src = AP(q_dram.tensor,
                                             (tj + s) * DIM + 1024 * dd + 16 * hh,
                                             [[32, 32], [4 * DIM, 32], [1, 16]])
                                    inc(sy.dma_start(dst, src), f"sRP{tp}", 16)
                    snap("rp", t, f"sRP{t % 2}")

                    @blk.scalar
                    def _(sc, t=t, tp=tp):
                        sc.wait_ge(sems["sQKVD0"], QKVD_TOT[0])
                        sc.wait_ge(sems["sQKVD1"], QKVD_TOT[1])
                        if t < 2:
                            sc.wait_ge(sems["sLAM"], lhsT1_zeroed)
                        if t >= 2:
                            sc.wait_ge(sems["sATT"], t - 1)
                        tj = 128 * t
                        for dd in range(2):
                            for s in range(4):
                                dst = AP(rhs1[tp][dd], (32 * s) * 1024,
                                         [[1024, 32], [32, 32], [1, 32]])
                                src = AP(k_dram.tensor,
                                         (tj + s) * DIM + 1024 * dd,
                                         [[32, 32], [4 * DIM, 32], [1, 32]])
                                inc(sc.dma_start(dst, src), f"sRPK{tp}", 16)
                    snap("rpk", t, f"sRPK{t % 2}")

                def em_front(j):
                    jp = j % 2
                    j4 = j % 4
                    tj = 128 * j

                    @blk.sync
                    def _(sy, j=j, tj=tj, j4=j4):
                        sy.wait_ge(sems["sQKVD0"], QKVD_TOT[0])
                        sy.wait_ge(sems["sQKVD1"], QKVD_TOT[1])
                        if j < 4:
                            sy.wait_ge(sems["sLAM"], lhsT1_zeroed)
                        for H in range(2):
                            for s in range(4):
                                for G in range(2):
                                    q2 = 8 * H + 2 * s + G
                                    dst = AP(v2_dram.tensor,
                                             128 * j * DIM + 128 * q2,
                                             [[16 * DIM, 8], [DIM, 16], [1, 128]])
                                    src = AP(v_dram.tensor,
                                             (tj + 64 * H + 32 * G + s) * DIM,
                                             [[4 * DIM, 8], [128, 16], [1, 128]])
                                    inc(sy.dma_start(dst, src), "sVST", 16)
                        sy.wait_ge(sems["sVST"], C["sVST"])
                        if j >= 4:
                            sy.wait_ge(sems["sM2"], 4 * (j - 3))
                        inc(sy.dma_start(vst[j4][:],
                                         v2_dram[128 * j:128 * j + 128, :]),
                            f"sVSTL{j4}", 16)
                    snap("vst", j, f"sVSTL{j % 4}")

                    @blk.tensor
                    def _(t, j=j, jp=jp):
                        t.wait_ge(sems[f"sRP{jp}"], hget("rp", j))
                        t.wait_ge(sems[f"sRPK{jp}"], hget("rpk", j))
                        if j >= 2:
                            t.wait_ge(sems["sEXP"], 2 * (j - 1))
                        eb1, eb2 = psEsets[jp]
                        mm = None
                        for p in range(32):
                            H, g16 = p // 16, p % 16
                            for hh, bank in ((0, eb1), (1, eb2)):
                                t.matmul(bank[64 * H:64 * H + 64,
                                              32 * g16:32 * g16 + 32],
                                         lhsT1[jp][0][hh][:, 64 * p:64 * p + 64],
                                         rhs1[jp][0][:, 32 * p:32 * p + 32],
                                         start=True, stop=False)
                                mm = t.matmul(
                                    bank[64 * H:64 * H + 64,
                                         32 * g16:32 * g16 + 32],
                                    lhsT1[jp][1][hh][:, 64 * p:64 * p + 64],
                                    rhs1[jp][1][:, 32 * p:32 * p + 32],
                                    start=False, stop=True)
                        inc(mm, "sATT")

                    @blk.scalar
                    def _(sc, j=j, jp=jp):
                        sc.wait_ge(sems["sATT"], j + 1)
                        if j >= 1:
                            sc.wait_ge(sems["sD"], j)
                        eb1, eb2 = psEsets[jp]
                        inc(sc.activation(E1f[:], eb1[:], AF.Exp, scale=0.125),
                            "sEXP")
                        inc(sc.activation(E2f[:], eb2[:], AF.Exp, scale=0.125),
                            "sEXP")

                    @blk.vector
                    def _(v, j=j, jp=jp):
                        v.wait_ge(sems["sEXP"], 2 * (j + 1))
                        if j >= 2:
                            v.wait_ge(sems[f"sDD{jp}"],
                                      256 * ((j - 2) // 2 + 1))
                        fence(v, v.reduce_sum(
                            Ssum[:, 0:16],
                            E1f[:].rearrange("p (g j) -> p g j", g=16),
                            axis=AX.X))
                        fence(v, v.reduce_sum(
                            Ssum[:, 32:48],
                            E2f[:].rearrange("p (g j) -> p g j", g=16),
                            axis=AX.X))
                        fence(v, v.reciprocal(Ssum[:, 16:32], Ssum[:, 0:16]))
                        fence(v, v.reciprocal(Ssum[:, 48:64], Ssum[:, 32:48]))
                        v.tensor_copy(
                            AP(Rb, 0, [[1024, 128], [32, 16], [1, 32]]),
                            AP(Ssum, 16, [[64, 128], [1, 16], [0, 32]]))
                        v.tensor_copy(
                            AP(Rb, 512, [[1024, 128], [32, 16], [1, 32]]),
                            AP(Ssum, 48, [[64, 128], [1, 16], [0, 32]]))
                        inc(v.tensor_tensor(E1f[:], E1f[:], Rb[:, 0:512],
                                            op=ALU.mult), "sSMX")
                        inc(v.tensor_tensor(E2f[:], E2f[:], Rb[:, 512:1024],
                                            op=ALU.mult), "sSMX")
                        inc(v.scalar_tensor_tensor(
                            AP(D_sb[jp], 0, [[256, 128], [16, 16], [1, 16]]),
                            AP(E2f, 16, [[512, 128], [32, 16], [1, 16]]),
                            neglam_b[:],
                            AP(E1f, 0, [[512, 128], [32, 16], [1, 16]]),
                            op0=ALU.mult, op1=ALU.add), "sD")

                    @blk.gpsimd
                    def _(g, j=j, jp=jp, j4=j4):
                        g.wait_ge(sems["sD"], j + 1)
                        if j >= 4:
                            g.wait_ge(sems[f"sLT2{j4}"],
                                      16 * ((j - 4) // 4 + 1))
                        if j < 4:
                            g.wait_ge(sems["sMISC"], ltd_init_done)
                        for H in range(2):
                            for s in range(4):
                                for G in range(2):
                                    q2 = 8 * H + 2 * s + G
                                    src = AP(D_sb[jp],
                                             (64 * H + 16 * s) * 256 + 128 * G,
                                             [[256, 16], [16, 8], [1, 16]])
                                    dst = AP(ltd[j4].tensor, 128 * 128 * q2,
                                             [[128, 16], [2064, 8], [1, 16]])
                                    inc(g.dma_start(dst, src), f"sDD{jp}", 16)

                def em_load(t):
                    t4 = t % 4
                    tp = t % 2

                    @blk.scalar
                    def _(sc, t=t, t4=t4, tp=tp):
                        sc.wait_ge(sems[f"sDD{tp}"], 256 * (t // 2 + 1))
                        if t < 4:
                            sc.wait_ge(sems["sLAM"], lhsT1_zeroed)
                        if t >= 4:
                            sc.wait_ge(sems["sM2"], 4 * (t - 3))
                        inc(sc.dma_start(lhsT2[t4][:], ltd[t4][:, :],
                                         transpose=True), f"sLT2{t4}", 16)

                def em_back(jj):
                    j4 = jj % 4
                    tjj = 128 * jj
                    for grp in range(4):
                        @blk.tensor
                        def _(t, jj=jj, j4=j4, grp=grp):
                            t.wait_ge(sems[f"sLT2{j4}"], 16 * (jj // 4 + 1))
                            t.wait_ge(sems[f"sVSTL{j4}"], hget("vst", jj))
                            if jj >= 1:
                                t.wait_ge(sems["sO2"],
                                          16 * (jj - 1) + 4 * (grp + 1))
                            bank = banks4[grp]
                            mm = None
                            for qq in range(4):
                                q2 = grp * 4 + qq
                                mm = t.matmul(
                                    bank[:, 128 * qq:128 * qq + 128],
                                    lhsT2[j4][:, 128 * q2:128 * q2 + 128],
                                    vst[j4][:, 128 * q2:128 * q2 + 128],
                                    start=True, stop=True)
                            inc(mm, "sM2")

                        @blk.scalar
                        def _(sc, jj=jj, grp=grp):
                            sc.wait_ge(sems["sM2"], jj * 4 + grp + 1)
                            W(sc, "sRMS")
                            inc(sc.activation(scrB[:], banks4[grp][:],
                                              AF.Square), "sRMS")

                        @blk.vector
                        def _(v, grp=grp):
                            W(v, "sRMS")
                            inc(v.reduce_sum(rmsd[:, 4 * grp:4 * grp + 4],
                                             scrB[:].rearrange(
                                                 "p (a b) -> p a b", a=4),
                                             axis=AX.X), "sRMS")

                    @blk.scalar
                    def _(sc):
                        W(sc, "sRMS")
                        fence(sc, sc.activation(rmsd[:, 16:32],
                                                rmsd[:, 0:16], AF.Ln,
                                                bias=EPS, scale=1.0 / 128))
                        inc(sc.activation(rmsd[:, 16:32],
                                          rmsd[:, 16:32], AF.Exp,
                                          bias=lnoml_b[:, 0:1], scale=-0.5),
                            "sRMS")

                    @blk.vector
                    def _(v, jj=jj):
                        W(v, "sRMS")
                        if jj >= 1:
                            v.wait_ge(sems["sO2D"], 256 * jj)
                        for q2 in range(16):
                            inc(v.tensor_scalar(
                                o2_sb[:, 128 * q2:128 * q2 + 128],
                                banks4[q2 // 4][:, 128 * (q2 % 4):
                                                128 * (q2 % 4) + 128],
                                rmsd[:, 16 + q2:17 + q2], None,
                                op0=ALU.mult), "sO2")

                    @blk.gpsimd
                    def _(g, jj=jj, tjj=tjj):
                        g.wait_ge(sems["sO2"], 16 * (jj + 1))
                        inc(g.dma_start(o2p_dram[128 * jj:128 * jj + 128, :],
                                        o2_sb[:]), "sO2P", 16)
                        g.wait_ge(sems["sO2P"], C["sO2P"])
                        for H in range(2):
                            for s in range(4):
                                for G in range(2):
                                    q2 = 8 * H + 2 * s + G
                                    src = AP(o2p_dram.tensor,
                                             128 * jj * DIM + 128 * q2,
                                             [[16 * DIM, 8], [DIM, 16], [1, 128]])
                                    dst = AP(o2_dram.tensor,
                                             (tjj + 64 * H + 32 * G + s) * DIM,
                                             [[4 * DIM, 8], [128, 16], [1, 128]])
                                    inc(g.dma_start(dst, src), "sO2D", 16)

                def em_wot_load():
                    @blk.gpsimd
                    def _(g):
                        g.wait_ge(sems["sLAM"], C["sLAM"])
                        inc(g.dma_start(bo_b[:], bo_d[:, :]), "sWOT", 16)
                        for a in range(16):
                            src = AP(woT_d.tensor, 128 * a * DIM,
                                     [[DIM, 128], [1, DIM]])
                            dst = AP(woT_sb, DIM * a,
                                     [[16 * DIM, 128], [1, DIM]])
                            inc(g.dma_start(dst, src), "sWOT", 16)

                for it in range(nt + 2):
                    if it == 0:
                        em_repack(0)
                        if nt > 1:
                            em_repack(1)
                    elif it + 1 <= nt - 1:
                        em_repack(it + 1)
                    if it < nt:
                        em_front(it)
                    if 1 <= it <= nt:
                        em_load(it - 1)
                    if it >= 2:
                        em_back(it - 2)
                    if nt >= 12:
                        if 2 <= it <= 9:
                            @blk.vector
                            def _(v, it=it):
                                if it == 2:
                                    v.wait_ge(sems["sQMM"], 12 * nt)
                                    inc(v.memset(bo_b[:], 0.0), "sLAM")
                                c0 = 4096 * (it - 2)
                                inc(v.memset(woT_sb[:, c0:c0 + 4096], 0.0),
                                    "sLAM")
                        elif it == 10:
                            em_wot_load()
                    elif it == 2:
                        @blk.vector
                        def _(v):
                            v.wait_ge(sems["sQMM"], 12 * nt)
                            inc(v.memset(bo_b[:], 0.0), "sLAM")
                            inc(v.memset(woT_sb[:], 0.0), "sLAM")
                        em_wot_load()

            A4_GATE = dict(sO2D=256 * nt, sM2=4 * nt, sATT=nt, sD=nt,
                           sO2=16 * nt, sEXP=2 * nt, sRMS=C["sRMS"],
                           sDD0=C["sDD0"], sDD1=C["sDD1"],
                           sLT20=C["sLT20"], sLT21=C["sLT21"],
                           sLT22=C["sLT22"], sLT23=C["sLT23"],
                           sVSTL0=C["sVSTL0"], sVSTL1=C["sVSTL1"],
                           sVSTL2=C["sVSTL2"], sVSTL3=C["sVSTL3"],
                           sRP0=C["sRP0"], sRP1=C["sRP1"],
                           sRPK0=C["sRPK0"], sRPK1=C["sRPK1"])

            def a4gate(eng):
                for k, v in A4_GATE.items():
                    eng.wait_ge(sems[k], v)

            pa5 = contextlib.ExitStack()
            with pa5:
                o2_fm = pa5.enter_context(
                    nc.sbuf_tensor("o2_fm", [128, 16 * T], BF16))
                x_sb5 = [pa5.enter_context(
                    nc.sbuf_tensor(f"x5{i}", [128, 2048], F32)) for i in range(2)]
                h2_sb = [pa5.enter_context(
                    nc.sbuf_tensor(f"h2{i}", [128, 2048], BF16)) for i in range(2)]
                scr5 = pa5.enter_context(nc.sbuf_tensor("scr5", [128, 2048], F32))

                @blk.vector
                def _(v):
                    a4gate(v)
                    for i in range(2):
                        inc(v.memset(x_sb5[i][:], 0.0), "sLAM")
                        inc(v.memset(h2_sb[i][:], 0.0), "sLAM")
                pa5_init = C["sLAM"]

                @blk.sync
                def _(sy):
                    a4gate(sy)
                    sy.wait_ge(sems["sO2D"], 256 * nt)
                    sy.wait_ge(sems["sLAM"], pa5_init)
                    for a in range(16):
                        inc(sy.dma_start(o2_fm[:, T * a:T * a + T],
                                         o2_dram[:, 128 * a:128 * a + 128],
                                         transpose=True), "sOFM", 16)

                base_qmm = C["sQMM"]
                for j in range(nt):
                    b = j % 2
                    so = 8 * b

                    @blk.gpsimd
                    def _(g, j=j, b=b):
                        a4gate(g)
                        if j < 2:
                            g.wait_ge(sems["sLAM"], pa5_init)
                        if j >= 2:
                            g.wait_ge(sems[f"sX1D{b}"], 16 * ((j - 2) // 2 + 1))
                        inc(g.dma_start(x_sb5[b][:], x_d[128 * j:128 * j + 128, :]),
                            f"sXL{b}", 16)
                    snap("x5", j, f"sXL{b}")

                    for ch in range(4):
                        cidx = j * 4 + ch

                        @blk.tensor
                        def _(t, j=j, ch=ch, cidx=cidx):
                            if cidx < 4:
                                t.wait_ge(sems["sEV"], 12 * nt)
                            t.wait_ge(sems["sWOT"], 272)
                            t.wait_ge(sems["sOFM"], 256)
                            if cidx >= 4:
                                t.wait_ge(sems["sX1"], cidx - 3)
                            pb = psb[cidx % 4]
                            for a in range(16):
                                t.matmul(
                                    pb[:],
                                    o2_fm[:, T * a + 128 * j:T * a + 128 * j + 128],
                                    woT_sb[:, DIM * a + 512 * ch:
                                           DIM * a + 512 * ch + 512],
                                    start=(a == 0), stop=False)
                            mm = t.matmul(pb[:], ones_b[:],
                                          bo_b[:, 512 * ch:512 * ch + 512],
                                          start=False, stop=True)
                            inc(mm, "sQMM")

                        @blk.vector
                        def _(v, j=j, ch=ch, b=b, cidx=cidx):
                            v.wait_ge(sems["sQMM"], base_qmm + cidx + 1)
                            v.wait_ge(sems[f"sXL{b}"], hget("x5", j))
                            inc(v.tensor_tensor(
                                x_sb5[b][:, 512 * ch:512 * ch + 512],
                                x_sb5[b][:, 512 * ch:512 * ch + 512],
                                psb[cidx % 4][:], op=ALU.add), "sX1")

                    @blk.vector
                    def _(v, j=j, b=b, so=so):
                        v.wait_ge(sems["sX1"], 4 * (j + 1))
                        if j >= 2:
                            v.wait_ge(sems["sH2"], j - 1)
                        fence(v, v.reduce_sum(stat[:, so:so + 1], x_sb5[b][:],
                                              axis=AX.X))
                        fence(v, v.tensor_scalar(stat[:, so:so + 1],
                                                 stat[:, so:so + 1], 1.0 / DIM,
                                                 None, op0=ALU.mult))
                        inc(v.scalar_tensor_tensor(
                            scr5[:], x_sb5[b][:], stat[:, so:so + 1], x_sb5[b][:],
                            op0=ALU.subtract, op1=ALU.mult), "sLN")
                        inc(v.reduce_sum(stat[:, so + 1:so + 2], scr5[:],
                                         axis=AX.X), "sLN")
                    snap("ln2_stats", j, "sLN")

                    @blk.scalar
                    def _(sc, j=j, so=so):
                        sc.wait_ge(sems["sLN"], hget("ln2_stats", j))
                        fence(sc, sc.activation(stat[:, so + 2:so + 3],
                                                stat[:, so + 1:so + 2], AF.Ln,
                                                bias=EPS, scale=1.0 / DIM))
                        inc(sc.activation(stat[:, so + 2:so + 3],
                                          stat[:, so + 2:so + 3], AF.Exp,
                                          scale=-0.5), "sLN")
                    snap("ln2_rstd", j, "sLN")

                    @blk.vector
                    def _(v, j=j, b=b, so=so):
                        v.wait_ge(sems["sLN"], hget("ln2_rstd", j))
                        if j >= 2:
                            v.wait_ge(sems[f"sH2D{b}"],
                                      16 * ((j - 2) // 2 + 1))
                        inc(v.tensor_scalar(h2_sb[b][:], x_sb5[b][:],
                                            stat[:, so:so + 1],
                                            stat[:, so + 2:so + 3],
                                            op0=ALU.subtract, op1=ALU.mult),
                            "sH2")

                    @blk.sync
                    def _(sy, j=j, b=b):
                        sy.wait_ge(sems["sH2"], j + 1)
                        inc(sy.dma_start(h2_dram[128 * j:128 * j + 128, :],
                                         h2_sb[b][:]), f"sH2D{b}", 16)
                        inc(sy.dma_start(x1_dram[128 * j:128 * j + 128, :],
                                         x_sb5[b][:]), f"sX1D{b}", 16)

            paWo.close()
            A5_GATE = dict(sQMM=C["sQMM"], sH2D0=C["sH2D0"], sH2D1=C["sH2D1"],
                           sX1D0=C["sX1D0"], sX1D1=C["sX1D1"])

            def a5gate(eng):
                for k, v in A5_GATE.items():
                    eng.wait_ge(sems[k], v)

            pbO = contextlib.ExitStack()
            with pbO:
                h2_fm = pbO.enter_context(
                    nc.sbuf_tensor("h2_fm", [128, 16 * T], BF16))
                gw_sb = pbO.enter_context(nc.sbuf_tensor("gw_sb", [128, 144], BF16))
                comb9 = pbO.enter_context(
                    nc.sbuf_tensor("comb9", [128, 16 * nt], F32))
                gstat = pbO.enter_context(nc.sbuf_tensor("gstat", [128, 32], F32))

                @blk.vector
                def _(v):
                    a5gate(v)
                    inc(v.memset(h2_fm[:], 0.0), "sLAM")
                    inc(v.memset(gw_sb[:], 0.0), "sLAM")
                pb0_init = C["sLAM"]

                @blk.sync
                def _(sy):
                    a5gate(sy)
                    sy.wait_ge(sems["sLAM"], pb0_init)
                    for kk in range(16):
                        inc(sy.dma_start(h2_fm[:, T * kk:T * kk + T],
                                         h2_dram[:, 128 * kk:128 * kk + 128],
                                         transpose=True), "sH2FM", 16)

                @blk.gpsimd
                def _(g):
                    a5gate(g)
                    g.wait_ge(sems["sLAM"], pb0_init)
                    src = AP(gwT_d.tensor, 0, [[9, 128], [128 * 9, 16], [1, 9]])
                    dst = AP(gw_sb, 0, [[144, 128], [9, 16], [1, 9]])
                    inc(g.dma_start(dst, src), "sGW", 16)

                for j in range(nt):
                    go = 16 * (j % 2)
                    sbo = 16 * (j % 2)

                    @blk.tensor
                    def _(t, j=j):
                        t.wait_ge(sems["sH2FM"], 256)
                        t.wait_ge(sems["sGW"], 16)
                        if j >= 2:
                            t.wait_ge(sems["sGTV"], hget("gt2", j - 2))
                        bank = psM[j % 2]
                        mm = None
                        for kk in range(16):
                            mm = t.matmul(bank[:, 0:9],
                                          h2_fm[:, T * kk + 128 * j:
                                                T * kk + 128 * j + 128],
                                          gw_sb[:, 9 * kk:9 * kk + 9],
                                          start=(kk == 0), stop=(kk == 15))
                        inc(mm, "sSCM")

                    @blk.vector
                    def _(v, j=j, go=go, sbo=sbo):
                        v.wait_ge(sems["sSCM"], j + 1)
                        if j >= 2:
                            v.wait_ge(sems["sCMB"], hget("cmb", j - 2))
                        sc_ = psM[j % 2][:, 0:8]
                        fence(v, v.reduce_max(gstat[:, go:go + 1], sc_,
                                              axis=AX.X))
                        fence(v, v.tensor_scalar(gstat[:, go + 2:go + 10], sc_,
                                                 gstat[:, go:go + 1], None,
                                                 op0=ALU.is_ge))
                        fence(v, v.scalar_tensor_tensor(
                            scrB[:, sbo:sbo + 8], gstat[:, go + 2:go + 10],
                            -1e30, sc_, op0=ALU.mult, op1=ALU.add))
                        fence(v, v.reduce_max(gstat[:, go + 1:go + 2],
                                              scrB[:, sbo:sbo + 8], axis=AX.X))
                        fence(v, v.tensor_tensor(gstat[:, go + 10:go + 11],
                                                 gstat[:, go:go + 1],
                                                 gstat[:, go + 1:go + 2],
                                                 op=ALU.subtract))
                        fence(v, v.tensor_scalar(scrB[:, sbo + 8:sbo + 16], sc_,
                                                 gstat[:, go + 1:go + 2], None,
                                                 op0=ALU.is_ge))
                        inc(v.tensor_tensor(scrB[:, sbo + 8:sbo + 16],
                                            scrB[:, sbo + 8:sbo + 16],
                                            gstat[:, go + 2:go + 10],
                                            op=ALU.subtract), "sGTV")
                    snap("gt1", j, "sGTV")

                    @blk.scalar
                    def _(sc, j=j, go=go):
                        sc.wait_ge(sems["sGTV"], hget("gt1", j))
                        sc.wait_ge(sems["sFEN"], C["sFEN"])
                        inc(sc.activation(gstat[:, go + 12:go + 13],
                                          gstat[:, go + 10:go + 11], AF.Sigmoid),
                            "sGTV")
                        inc(sc.activation(gstat[:, go + 13:go + 14],
                                          gstat[:, go + 10:go + 11], AF.Sigmoid,
                                          scale=-1.0), "sGTV")
                        inc(sc.activation(comb9[:, 16 * j + 8:16 * j + 9],
                                          psM[j % 2][:, 8:9], AF.Sigmoid), "sGTV")
                    snap("gt2", j, "sGTV")

                    @blk.vector
                    def _(v, j=j, go=go, sbo=sbo):
                        v.wait_ge(sems["sGTV"], hget("gt2", j))
                        fence(v, v.tensor_scalar(gstat[:, go + 2:go + 10],
                                                 gstat[:, go + 2:go + 10],
                                                 gstat[:, go + 12:go + 13], None,
                                                 op0=ALU.mult))
                        inc(v.scalar_tensor_tensor(
                            comb9[:, 16 * j:16 * j + 8],
                            scrB[:, sbo + 8:sbo + 16],
                            gstat[:, go + 13:go + 14],
                            gstat[:, go + 2:go + 10],
                            op0=ALU.mult, op1=ALU.add), "sCMB")
                    snap("cmb", j, "sCMB")

                pbW = contextlib.ExitStack()
                with pbW:
                    hid = [pbW.enter_context(
                        nc.sbuf_tensor(f"hid{i}", [128, 8 * T], BF16))
                        for i in range(2)]
                    wup = [[pbW.enter_context(
                        nc.sbuf_tensor(f"wup{gu}{i}", [128, 2048], BF16))
                        for i in range(2)] for gu in range(2)]
                    wdn = [pbW.enter_context(
                        nc.sbuf_tensor(f"wdn{i}", [128, 4096], BF16))
                        for i in range(4)]
                    comb_bc = pbW.enter_context(
                        nc.sbuf_tensor("comb_bc", [128, T], BF16))
                    sg_sb = pbW.enter_context(
                        nc.sbuf_tensor("sg_sb", [128, 512], BF16))
                    us_sb = pbW.enter_context(
                        nc.sbuf_tensor("us_sb", [128, 512], BF16))
                    stg_dn = [pbW.enter_context(
                        nc.sbuf_tensor(f"stgdn{i}", [128, 512], F32))
                        for i in range(2)]
                    comb_row = pbW.enter_context(
                        nc.sbuf_tensor("comb_row", [1, T], BF16))
                    @blk.vector
                    def _(v):
                        a5gate(v)
                        for gu in range(2):
                            for i in range(2):
                                inc(v.memset(wup[gu][i][:], 0.0), "sLAM")
                        for i in range(4):
                            inc(v.memset(wdn[i][:], 0.0), "sLAM")
                        inc(v.memset(comb_row[:], 0.0), "sLAM")
                    pbw_init = C["sLAM"]

                    up_emitted = set()

                    def em_upload(E, ch, gu):
                        up_emitted.add((E, ch, gu))
                        wsrc = upg_d if gu == 0 else upu_d

                        @blk.gpsimd
                        def _(g, E=E, ch=ch, gu=gu, wsrc=wsrc):
                            li = (E * UPC + ch) * 2 + gu
                            if li < 4:
                                a5gate(g)
                                g.wait_ge(sems["sLAM"], pbw_init)
                            if li >= 4:
                                g.wait_ge(sems["sUMM"],
                                          hget("upw_free", li - 4))
                            src = AP(wsrc.tensor,
                                     E * DIM * 1024 + 128 * ch,
                                     [[1024, 128], [128 * 1024, 16],
                                      [1, 128]])
                            dst = AP(wup[gu][ch % 2], 0,
                                     [[2048, 128], [128, 16], [1, 128]])
                            inc(g.dma_start(dst, src),
                                f"sUW{gu * 2 + ch % 2}", 16)
                        snap("uw", (E * UPC + ch) * 2 + gu,
                             f"sUW{gu * 2 + ch % 2}")

                    NE = 10
                    UPC = 8
                    base_x1 = C["sX1"]

                    up_chain = 0
                    dn_chain = 0
                    for E in range(NE):
                        eb = E % 2
                        rE = E if E < 8 else 8

                        for ci, (t0, tl) in enumerate(TCH):
                            ntile = tl // 128

                            @blk.tensor
                            def _(t, E=E, rE=rE, ci=ci, t0=t0, ntile=ntile):
                                t.wait_ge(sems["sCMB"], C["sCMB"])
                                t.wait_ge(sems["sGTV"], C["sGTV"])
                                if ci >= 2 or E >= 1:
                                    t.wait_ge(sems["sCBE"],
                                              max(0, C["sCBE"] - 1))
                                mm = None
                                for jj in range(ntile):
                                    jt = (t0 // 128) + jj
                                    mm = t.matmul(
                                        psM[ci % 2][0:1, 128 * jj:128 * jj + 128],
                                        comb9[:, 16 * jt + rE:16 * jt + rE + 1],
                                        ident[:], start=True, stop=True)
                                inc(mm, "sCB")

                            @blk.vector
                            def _(v, E=E, ci=ci, t0=t0, tl=tl):
                                W(v, "sCB")
                                if E >= 1:
                                    v.wait_ge(sems["sCB"], hget("bcast", E - 1))
                                inc(v.tensor_copy(comb_row[0:1, t0:t0 + tl],
                                                  psM[ci % 2][0:1, 0:tl]), "sCR")

                            @blk.tensor
                            def _(t, E=E, ci=ci, t0=t0, tl=tl):
                                t.wait_ge(sems["sCR"], C["sCR"])
                                inc(t.matmul(psM[ci % 2][:, 0:tl], ones_b[:],
                                             comb_row[0:1, t0:t0 + tl],
                                             start=True, stop=True), "sCB")

                            @blk.vector
                            def _(v, E=E, ci=ci, t0=t0, tl=tl):
                                W(v, "sCB")
                                inc(v.tensor_copy(comb_bc[:, t0:t0 + tl],
                                                  psM[ci % 2][:, 0:tl]), "sCBE")
                        snap("bcast", E, "sCB")

                        for ch in range(UPC):
                            for gu in range(2):
                                if (E, ch, gu) in up_emitted:
                                    continue
                                em_upload(E, ch, gu)

                            for ci, (t0, tl) in enumerate(TCH):
                                @blk.tensor
                                def _(t, E=E, ch=ch, ci=ci, t0=t0, tl=tl, eb=eb,
                                      up_chain=up_chain):
                                    li = (E * UPC + ch) * 2
                                    t.wait_ge(sems[f"sUW{0 * 2 + ch % 2}"],
                                              hget("uw", li))
                                    t.wait_ge(sems[f"sUW{1 * 2 + ch % 2}"],
                                              hget("uw", li + 1))
                                    t.wait_ge(sems["sCBE"], C["sCBE"])
                                    if E >= 2 and ch == 0 and ci == 0:
                                        t.wait_ge(sems["sDMM"],
                                                  hget("dn_done", E - 2))
                                    if up_chain == 0 and ci == 0:
                                        t.wait_ge(sems["sX1"], base_x1)
                                    if E >= 1 and E % 2 == 0 and ch == 0 and ci == 0:
                                        t.wait_ge(sems["sDNE"], C["sDNE"])
                                    for gu in range(2):
                                        cno = up_chain + (ci * 2 + gu)
                                        if cno >= 4:
                                            t.wait_ge(sems["sUPE"],
                                                      hget("up_cons", cno - 4))
                                        pb = psb[cno % 4]
                                        mm = None
                                        for kk in range(16):
                                            mm = t.matmul(
                                                pb[:, 0:tl],
                                                wup[gu][ch % 2][:,
                                                                128 * kk:128 * kk + 128],
                                                h2_fm[:, T * kk + t0:
                                                      T * kk + t0 + tl],
                                                start=(kk == 0), stop=(kk == 15))
                                        inc(mm, "sUMM")

                                @blk.scalar
                                def _(sc, E=E, ch=ch, ci=ci, tl=tl,
                                      up_chain=up_chain):
                                    sc.wait_ge(sems["sUMM"], up_chain + ci * 2 + 1)
                                    W(sc, "sUPE")
                                    gbank = psb[(up_chain + ci * 2) % 4]
                                    inc(sc.activation(sg_sb[:, 0:tl],
                                                      gbank[:, 0:tl], AF.Silu),
                                        "sUPE")
                                    snap("up_cons",
                                         up_chain + ci * 2, "sUPE")

                                @blk.vector
                                def _(v, E=E, ch=ch, ci=ci, t0=t0, tl=tl, eb=eb,
                                      up_chain=up_chain):
                                    v.wait_ge(sems["sUMM"], up_chain + ci * 2 + 2)
                                    W(v, "sUPE")
                                    ubank = psb[(up_chain + ci * 2 + 1) % 4]
                                    inc(v.tensor_tensor(us_sb[:, 0:tl],
                                                        ubank[:, 0:tl],
                                                        comb_bc[:, t0:t0 + tl],
                                                        op=ALU.mult), "sUPE")
                                    snap("up_cons",
                                         up_chain + ci * 2 + 1, "sUPE")
                                    inc(v.tensor_tensor(
                                        AP(hid[eb], T * ch + t0,
                                           [[8 * T, 128], [1, tl]]),
                                        sg_sb[:, 0:tl], us_sb[:, 0:tl],
                                        op=ALU.mult), "sHID")

                            up_chain += 2 * len(TCH)
                            snap("upw_free", (E * UPC + ch) * 2, "sUMM")
                            snap("upw_free", (E * UPC + ch) * 2 + 1, "sUMM")

                        if E % 2 == 1:
                            pair = E // 2
                            for cc in range(4):
                                for ei in range(2):
                                    @blk.gpsimd
                                    def _(g, E=E, pair=pair, cc=cc, ei=ei):
                                        li = (pair * 4 + cc) * 2 + ei
                                        if li < 4:
                                            g.wait_ge(sems["sLAM"], pbw_init)
                                        if li >= 4:
                                            g.wait_ge(sems["sDMM"],
                                                      hget("dnw_free", li - 4))
                                        Ex = 2 * pair + ei
                                        src = AP(dnw_d.tensor,
                                                 Ex * 1024 * DIM + 512 * cc,
                                                 [[DIM, 128], [128 * DIM, 8],
                                                  [1, 512]])
                                        dst = AP(wdn[(cc % 2) * 2 + ei], 0,
                                                 [[4096, 128], [512, 8], [1, 512]])
                                        inc(g.dma_start(dst, src),
                                            f"sDW{(cc % 2) * 2 + ei}", 16)
                                        snap("dw", (pair * 4 + cc) * 2 + ei,
                                             f"sDW{(cc % 2) * 2 + ei}")

                                if cc == 0 and E + 1 < NE:
                                    for ch_ in range(2):
                                        for gu_ in range(2):
                                            em_upload(E + 1, ch_, gu_)

                                for j in range(nt):
                                    @blk.tensor
                                    def _(t, pair=pair, cc=cc, j=j,
                                          dn_chain=dn_chain, upc=up_chain):
                                        li = (pair * 4 + cc) * 2
                                        t.wait_ge(sems[f"sDW{(cc % 2) * 2}"],
                                                  hget("dw", li))
                                        t.wait_ge(sems[f"sDW{(cc % 2) * 2 + 1}"],
                                                  hget("dw", li + 1))
                                        t.wait_ge(sems["sHID"], C["sHID"])
                                        cno = dn_chain + j
                                        if cno < dn_chain + 4 and cc == 0 and j < 4:
                                            t.wait_ge(sems["sUPE"], C["sUPE"])
                                        if cno >= 4:
                                            t.wait_ge(sems["sDNE"], cno - 3)
                                        pb = psb[cno % 4]
                                        mm = None
                                        for ei in range(2):
                                            for ht in range(8):
                                                mm = t.matmul(
                                                    pb[:],
                                                    AP(hid[ei], T * ht + 128 * j,
                                                       [[8 * T, 128], [1, 128]]),
                                                    wdn[(cc % 2) * 2 + ei][
                                                        :, 512 * ht:512 * ht + 512],
                                                    start=(ei == 0 and ht == 0),
                                                    stop=(ei == 1 and ht == 7))
                                        inc(mm, "sDMM")

                                    @blk.scalar
                                    def _(sc, cc=cc, j=j, dn_chain=dn_chain):
                                        cno = dn_chain + j
                                        sc.wait_ge(sems["sDMM"], cno + 1)
                                        if cno >= 2:
                                            sc.wait_ge(sems[f"sACC{cno % 2}"],
                                                       16 * ((cno - 2) // 2 + 1))
                                        inc(sc.copy(stg_dn[cno % 2][:],
                                                    psb[cno % 4][:]), "sDNE")

                                    @blk.gpsimd
                                    def _(g, pair=pair, cc=cc, j=j,
                                          dn_chain=dn_chain):
                                        cno = dn_chain + j
                                        g.wait_ge(sems["sDNE"], cno + 1)
                                        op = ALU.bypass if pair == 0 else ALU.add
                                        inc(g.dma_start(
                                            acc_dram[128 * j:128 * j + 128,
                                                     512 * cc:512 * cc + 512],
                                            stg_dn[cno % 2][:], accum_op=op),
                                            f"sACC{cno % 2}", 16)

                                dn_chain += nt
                                snap("dnw_free", (pair * 4 + cc) * 2, "sDMM")
                                snap("dnw_free", (pair * 4 + cc) * 2 + 1, "sDMM")
                            snap("dn_done", E, "sDMM")

                pb4 = contextlib.ExitStack()
                with pb4:
                    x1b = [pb4.enter_context(
                        nc.sbuf_tensor(f"x1b{i}", [128, 2048], F32))
                        for i in range(2)]
                    accb = [pb4.enter_context(
                        nc.sbuf_tensor(f"accb{i}", [128, 2048], F32))
                        for i in range(2)]
                    ACC_TOTAL = (C["sACC0"], C["sACC1"])

                    @blk.vector
                    def _(v):
                        v.wait_ge(sems["sACC0"], ACC_TOTAL[0])
                        v.wait_ge(sems["sACC1"], ACC_TOTAL[1])
                        for i in range(2):
                            inc(v.memset(x1b[i][:], 0.0), "sLAM")
                            inc(v.memset(accb[i][:], 0.0), "sLAM")
                    pb4_init = C["sLAM"]

                    for j in range(nt):
                        b = j % 2

                        @blk.gpsimd
                        def _(g, j=j, b=b):
                            g.wait_ge(sems["sACC0"], ACC_TOTAL[0])
                            g.wait_ge(sems["sACC1"], ACC_TOTAL[1])
                            g.wait_ge(sems["sLAM"], pb4_init)
                            if j >= 2:
                                g.wait_ge(sems[f"sY{b}"],
                                          16 * ((j - 2) // 2 + 1))
                            inc(g.dma_start(x1b[b][:],
                                            x1_dram[128 * j:128 * j + 128, :]),
                                f"sFB{b}", 16)
                            inc(g.dma_start(accb[b][:],
                                            acc_dram[128 * j:128 * j + 128, :]),
                                f"sFB{b}", 16)

                        @blk.vector
                        def _(v, j=j, b=b):
                            v.wait_ge(sems[f"sFB{b}"], 32 * (j // 2 + 1))
                            inc(v.tensor_tensor(x1b[b][:], x1b[b][:], accb[b][:],
                                                op=ALU.add), "sFIN")

                        @blk.sync
                        def _(sy, j=j, b=b):
                            sy.wait_ge(sems["sFIN"], j + 1)
                            inc(sy.dma_start(y_d[128 * j:128 * j + 128, :],
                                             x1b[b][:]), f"sY{b}", 16)

                    if taps:
                        @blk.sync
                        def _(sy):
                            sy.wait_ge(sems["sY0"], C["sY0"])
                            sy.wait_ge(sems["sY1"], C["sY1"])
                            for name, out in tap_outs.items():
                                src = scratch[name]
                                inc(sy.dma_start(out, src[:, :]), "sY0", 16)

                    @blk.gpsimd
                    def _(g):
                        g.wait_ge(sems["sY0"], C["sY0"])
                        g.wait_ge(sems["sY1"], C["sY1"])

    return nc


def _prep(inputs):
    f32 = np.float32
    g = lambda k: np.asarray(inputs[k], dtype=f32)
    cp = np.arange(DIM)
    i_ = cp % 32
    d_ = cp // 32
    pidx = (i_ % 16) * 128 + 2 * d_ + (i_ // 16)

    wq = g("L0_attn_Wq")[pidx, :]
    wk = g("L0_attn_Wk")[pidx, :]
    shg = g("L0_ffn_sh_gate")
    shu = g("L0_ffn_sh_up")
    shd = g("L0_ffn_sh_down")
    wg = g("L0_ffn_w_gate")
    wu = g("L0_ffn_w_up")
    wd = g("L0_ffn_w_down")

    upg = np.concatenate(
        [np.transpose(wg, (0, 2, 1)),
         np.transpose(shg, (1, 0)).reshape(DIM, 2, 1024).transpose(1, 0, 2)],
        axis=0)
    upu = np.concatenate(
        [np.transpose(wu, (0, 2, 1)),
         np.transpose(shu, (1, 0)).reshape(DIM, 2, 1024).transpose(1, 0, 2)],
        axis=0)
    dnw = np.concatenate(
        [np.transpose(wd, (0, 2, 1)),
         np.transpose(shd, (1, 0)).reshape(2, 1024, DIM)],
        axis=0)

    wm = {
        "wqT": np.ascontiguousarray(wq.T),
        "wkT": np.ascontiguousarray(wk.T),
        "wvT": np.ascontiguousarray(g("L0_attn_Wv").T),
        "woT": np.ascontiguousarray(g("L0_attn_Wo").T),
        "gwT": np.ascontiguousarray(
            np.concatenate([g("L0_ffn_gate_w"), g("L0_ffn_se_gate_w")], 0).T),
        "upg": np.ascontiguousarray(upg),
        "upu": np.ascontiguousarray(upu),
        "dnw": np.ascontiguousarray(dnw),
        "lama": np.ascontiguousarray(
            np.concatenate([g("L0_attn_lq1"), g("L0_attn_lq2")])[None, :]),
        "lamb": np.ascontiguousarray(
            np.concatenate([g("L0_attn_lk1"), g("L0_attn_lk2")])[None, :]),
        "bo": np.ascontiguousarray(g("L0_attn_bo")[None, :]),
        "ident": np.eye(128, dtype=f32),
    }
    c64 = np.zeros((1, 4), dtype=f32)
    for i, c in enumerate([LAM_INIT, LN_OML]):
        hi = np.float32(c)
        lo = np.float32(c - np.float64(hi))
        c64[0, 2 * i] = hi
        c64[0, 2 * i + 1] = lo
    wm["c64"] = c64
    x = np.ascontiguousarray(g("L0_attn_x"))
    return x, wm


def kernel(**inputs):
    x, wm = _prep(inputs)
    nc = build(nt=16)
    in_maps = []
    for c in range(NCORES):
        m = {"x": np.ascontiguousarray(x[c])}
        m.update(wm)
        in_maps.append(m)
    res = run_bass_kernel_spmd(nc, in_maps, core_ids=list(range(NCORES)))
    out = np.stack([np.asarray(r["y"], dtype=np.float32)
                    for r in res.results], axis=0)
    return out


# version 13
# speedup vs baseline: 1.0715x; 1.0715x over previous
import contextlib
import numpy as np
import ml_dtypes
import concourse.bass as bass
import concourse.mybir as mybir
from concourse.ap import AP
from concourse.bass_utils import run_bass_kernel_spmd

F32 = mybir.dt.float32
BF16 = mybir.dt.bfloat16
AF = mybir.ActivationFunctionType
ALU = mybir.AluOpType
AX = mybir.AxisListType
bfloat16 = ml_dtypes.bfloat16

DIM = 2048
NCORES = 8
LAM_INIT = 0.8 - 0.6 * float(np.exp(-3.6))
OML = 1.0 - LAM_INIT
LN_OML = float(np.log(OML))
EPS = 1e-5


def _tchunks(T):
    return [(c, min(512, T - c)) for c in range(0, T, 512)]


def build(nt=16, taps=()):
    T = nt * 128
    TCH = _tchunks(T)
    nc = bass.Bass("TRN2", target_bir_lowering=False, debug=False,
                   detect_race_conditions=False)

    x_d = nc.dram_tensor("x", [T, DIM], F32, kind="ExternalInput").ap()
    wqT_d = nc.dram_tensor("wqT", [DIM, DIM], F32, kind="ExternalInput").ap()
    wkT_d = nc.dram_tensor("wkT", [DIM, DIM], F32, kind="ExternalInput").ap()
    wvT_d = nc.dram_tensor("wvT", [DIM, DIM], F32, kind="ExternalInput").ap()
    woT_d = nc.dram_tensor("woT", [DIM, DIM], F32, kind="ExternalInput").ap()
    gwT_d = nc.dram_tensor("gwT", [DIM, 9], F32, kind="ExternalInput").ap()
    upg_d = nc.dram_tensor("upg", [10, DIM, 1024], F32, kind="ExternalInput").ap()
    upu_d = nc.dram_tensor("upu", [10, DIM, 1024], F32, kind="ExternalInput").ap()
    dnw_d = nc.dram_tensor("dnw", [10, 1024, DIM], F32, kind="ExternalInput").ap()
    lama_d = nc.dram_tensor("lama", [1, 128], F32, kind="ExternalInput").ap()
    lamb_d = nc.dram_tensor("lamb", [1, 128], F32, kind="ExternalInput").ap()
    bo_d = nc.dram_tensor("bo", [1, DIM], F32, kind="ExternalInput").ap()
    id_d = nc.dram_tensor("ident", [128, 128], F32, kind="ExternalInput").ap()
    c64_d = nc.dram_tensor("c64", [1, 4], F32, kind="ExternalInput").ap()
    y_d = nc.dram_tensor("y", [T, DIM], F32, kind="ExternalOutput").ap()

    h_dram = nc.dram_tensor("h_dram", [T, DIM], BF16, kind="Internal").ap()
    q_dram = nc.dram_tensor("q_dram", [T, DIM], BF16, kind="Internal").ap()
    k_dram = nc.dram_tensor("k_dram", [T, DIM], BF16, kind="Internal").ap()
    v_dram = nc.dram_tensor("v_dram", [T, DIM], BF16, kind="Internal").ap()
    ltd = [nc.dram_tensor(f"ltd{i}", [2048, 128], BF16, kind="Internal").ap()
           for i in range(4)]
    v2_dram = nc.dram_tensor("v2_dram", [T, DIM], BF16, kind="Internal").ap()
    o2p_dram = nc.dram_tensor("o2p_dram", [T, DIM], BF16, kind="Internal").ap()
    o2_dram = nc.dram_tensor("o2_dram", [T, DIM], BF16, kind="Internal").ap()
    x1_dram = nc.dram_tensor("x1_dram", [T, DIM], F32, kind="Internal").ap()
    h2_dram = nc.dram_tensor("h2_dram", [T, DIM], BF16, kind="Internal").ap()
    acc_dram = nc.dram_tensor("acc_dram", [T, DIM], F32, kind="Internal").ap()
    scratch = {"h_dram": h_dram, "q_dram": q_dram, "k_dram": k_dram,
               "v_dram": v_dram, "o2_dram": o2_dram, "x1_dram": x1_dram,
               "h2_dram": h2_dram, "acc_dram": acc_dram,
               "ltd0": ltd[0], "ltd1": ltd[1]}
    tap_outs = {}
    for name in taps:
        src = scratch[name]
        tap_outs[name] = nc.dram_tensor(
            "tap_" + name, list(src.shape), src.dtype, kind="ExternalOutput").ap()

    es = contextlib.ExitStack()
    sems = {}
    C = {}

    def mksem(name):
        sems[name] = es.enter_context(nc.semaphore(name))
        C[name] = 0

    def inc(instr, name, k=1):
        instr.then_inc(sems[name], k)
        C[name] += k

    def W(eng, name, val=None):
        eng.wait_ge(sems[name], C[name] if val is None else val)

    def fence(eng, instr):
        inc(instr, "sFEN")
        eng.wait_ge(sems["sFEN"], C["sFEN"])
        return instr

    for s in ["sXL0", "sXL1", "sLN", "sHD0", "sHD1", "sHFM", "sW0", "sW1",
              "sQMM", "sEV", "sQKVD0", "sQKVD1",
              "sRP0", "sRP1", "sRPK0", "sRPK1",
              "sATT", "sEXP", "sSMX", "sD", "sDD0", "sDD1",
              "sLT20", "sLT21", "sLT22", "sLT23",
              "sVST", "sM2", "sRMS", "sO2", "sO2D", "sO2P",
              "sVSTL0", "sVSTL1", "sVSTL2", "sVSTL3",
              "sWOT", "sOFM", "sX1", "sX1D0", "sX1D1",
              "sH2", "sH2D0", "sH2D1", "sH2FM", "sGW", "sSCM", "sGTV", "sCMB",
              "sCR", "sCB", "sCBE", "sUW0", "sUW1", "sUW2", "sUW3",
              "sUWF0", "sUWF1", "sUWF2", "sUWF3", "sUCP",
              "sUMM", "sUPE", "sHID", "sDW0", "sDW1", "sDW2", "sDW3", "sDMM",
              "sDNE", "sACC0", "sACC1", "sFB0", "sFB1", "sFIN", "sY0", "sY1",
              "sMISC", "sLAM", "sFEN"]:
        mksem(s)
    for g_ in range(4):
        mksem(f"sHFM{g_}")
    for j_ in range(nt):
        mksem(f"sHF{j_}")
    ngrp = 4 if nt % 4 == 0 and nt >= 8 else 2
    gsz = nt // ngrp

    hist = {}

    def snap(key, idx, semname):
        hist.setdefault(key, {})[idx] = C[semname]

    def hget(key, idx, default=0):
        return hist.get(key, {}).get(idx, default)

    with es:
        blk = es.enter_context(nc.Block())

        ones_f = es.enter_context(nc.sbuf_tensor("ones_f", [1, 128], F32))
        ones_b = es.enter_context(nc.sbuf_tensor("ones_b", [1, 128], BF16))
        ident = es.enter_context(nc.sbuf_tensor("ident_sb", [128, 128], F32))
        lam_sb = es.enter_context(nc.sbuf_tensor("lam_sb", [1, 256], F32))
        lamt = es.enter_context(nc.sbuf_tensor("lamt", [1, 256], F32))
        neglam_b = es.enter_context(nc.sbuf_tensor("neglam_b", [128, 1], F32))
        lnoml_b = es.enter_context(nc.sbuf_tensor("lnoml_b", [128, 1], F32))
        cst = es.enter_context(nc.sbuf_tensor("cst", [128, 2], F32))
        nc.const_aps.register(F32, EPS, cst[:, 0:1])
        nc.const_aps.register(F32, LN_OML, cst[:, 1:2])
        scrB = es.enter_context(nc.sbuf_tensor("scrB", [128, 512], F32))
        stat = es.enter_context(nc.sbuf_tensor("stat", [128, 16], F32))
        stg = [es.enter_context(nc.sbuf_tensor(f"stg{i}", [128, 512], BF16))
               for i in range(2)]
        psb = [es.enter_context(nc.psum_tensor(f"psb{i}", [128, 512], F32))
               for i in range(4)]
        psE1 = es.enter_context(nc.psum_tensor("psE1", [128, 512], F32))
        psE2 = es.enter_context(nc.psum_tensor("psE2", [128, 512], F32))
        psM = [es.enter_context(nc.psum_tensor(f"psM{i}", [128, 512], F32))
               for i in range(2)]

        z0 = contextlib.ExitStack()
        with z0:
            zz = z0.enter_context(nc.sbuf_tensor("zz", [128, 512], BF16))

            @blk.gpsimd
            def _(g):
                inc(g.dma_start(lam_sb[:, 0:128], lama_d[:, :]), "sMISC", 16)
                inc(g.dma_start(lam_sb[:, 128:256], lamb_d[:, :]), "sMISC", 16)
                inc(g.dma_start(ident[:], id_d[:, :]), "sMISC", 16)
                inc(g.dma_start(lamt[:, 140:144], c64_d[:, :]), "sMISC", 16)

            @blk.vector
            def _(v):
                inc(v.memset(ones_f[:], 1.0), "sLAM")
                inc(v.memset(ones_b[:], 1.0), "sLAM")
                inc(v.memset(zz[:], 0.0), "sLAM")
                inc(v.memset(cst[:, 0:1], EPS), "sLAM")
                inc(v.memset(cst[:, 1:2], LN_OML), "sLAM")
                W(v, "sMISC")
                fence(v, v.tensor_tensor(lamt[:, 0:128], lam_sb[:, 0:128],
                                         lam_sb[:, 128:256], op=ALU.mult))
                inc(v.reduce_sum(lamt[:, 128:130],
                                 lamt[:, 0:128].rearrange("p (a b) -> p a b", a=2),
                                 axis=AX.X), "sLAM")

            @blk.scalar
            def _(sc):
                W(sc, "sLAM")
                inc(sc.activation(lamt[:, 130:132], lamt[:, 128:130], AF.Exp),
                    "sLAM")

            @blk.vector
            def _(v):
                W(v, "sLAM")
                fence(v, v.tensor_tensor(lamt[:, 132:133], lamt[:, 130:131],
                                         lamt[:, 131:132], op=ALU.subtract))
                fence(v, v.tensor_tensor(lamt[:, 134:135], lamt[:, 140:141],
                                         lamt[:, 141:142], op=ALU.add))
                fence(v, v.tensor_tensor(lamt[:, 135:136], lamt[:, 142:143],
                                         lamt[:, 143:144], op=ALU.add))
                fence(v, v.tensor_tensor(lamt[:, 133:134], lamt[:, 132:133],
                                         lamt[:, 134:135], op=ALU.add))
                inc(v.tensor_scalar(lamt[:, 133:134], lamt[:, 133:134], -1.0,
                                    None, op0=ALU.mult), "sLAM")

            @blk.tensor
            def _(t):
                W(t, "sLAM")
                t.matmul(psM[0][:, 0:1], ones_f[:], lamt[:, 133:134],
                         start=True, stop=True)
                inc(t.matmul(psM[0][:, 4:5], ones_f[:], lamt[:, 135:136],
                             start=True, stop=True), "sLAM")

            @blk.vector
            def _(v):
                W(v, "sLAM")
                fence(v, v.tensor_copy(neglam_b[:], psM[0][:, 0:1]))
                inc(v.tensor_copy(lnoml_b[:], psM[0][:, 4:5]), "sLAM")

            @blk.gpsimd
            def _(g):
                W(g, "sLAM")
                for i in range(4):
                    for qtr in range(4):
                        inc(g.dma_start(
                            AP(ltd[i].tensor, 512 * qtr,
                               [[2048, 128], [1, 512]]), zz[:]),
                            "sMISC", 16)
            ltd_init_done = C["sMISC"]

            pa = contextlib.ExitStack()
            paWo = contextlib.ExitStack()
            with pa:
                paW = contextlib.ExitStack()
                scr = paW.enter_context(nc.sbuf_tensor("scr", [128, 2048], F32))
                h_fm = paW.enter_context(
                    nc.sbuf_tensor("h_fm", [128, 16 * T], BF16))
                wbuf = [paW.enter_context(
                    nc.sbuf_tensor(f"wbuf{i}", [128, 8192], BF16))
                    for i in range(2)]

                a1 = contextlib.ExitStack()
                with a1:
                    x_sb = [a1.enter_context(
                        nc.sbuf_tensor(f"x{i}", [128, 2048], F32)) for i in range(2)]
                    h_sb = [a1.enter_context(
                        nc.sbuf_tensor(f"h{i}", [128, 2048], BF16)) for i in range(2)]

                    for j in range(nt):
                        b = j % 2
                        so = 8 * b

                        @blk.gpsimd
                        def _(g, j=j, b=b):
                            if j >= 2:
                                g.wait_ge(sems["sLN"], hget("hact", j - 2))
                            inc(g.dma_start(x_sb[b][:],
                                            x_d[128 * j:128 * j + 128, :]),
                                f"sXL{b}", 16)

                        @blk.vector
                        def _(v, j=j, b=b, so=so):
                            v.wait_ge(sems[f"sXL{b}"], 16 * (j // 2 + 1))
                            if j >= 2:
                                v.wait_ge(sems["sLN"], hget("hact", j - 2))
                            fence(v, v.reduce_sum(stat[:, so:so + 1], x_sb[b][:],
                                                  axis=AX.X))
                            fence(v, v.tensor_scalar(stat[:, so:so + 1],
                                                     stat[:, so:so + 1],
                                                     1.0 / DIM, None,
                                                     op0=ALU.mult))
                            inc(v.scalar_tensor_tensor(
                                scr[:], x_sb[b][:], stat[:, so:so + 1], x_sb[b][:],
                                op0=ALU.subtract, op1=ALU.mult), "sLN")
                            inc(v.reduce_sum(stat[:, so + 1:so + 2], scr[:],
                                             axis=AX.X), "sLN")
                        snap("ln_stats", j, "sLN")

                        @blk.scalar
                        def _(sc, j=j, so=so):
                            sc.wait_ge(sems["sLN"], hget("ln_stats", j))
                            fence(sc, sc.activation(stat[:, so + 2:so + 3],
                                                    stat[:, so + 1:so + 2], AF.Ln,
                                                    bias=EPS, scale=1.0 / DIM))
                            inc(sc.activation(stat[:, so + 2:so + 3],
                                              stat[:, so + 2:so + 3], AF.Exp,
                                              scale=-0.5), "sLN")
                        snap("ln_rstd", j, "sLN")

                        @blk.vector
                        def _(v, j=j, b=b, so=so):
                            v.wait_ge(sems["sLN"], hget("ln_rstd", j))
                            if j >= 2:
                                v.wait_ge(sems[f"sHF{j - 2}"], 16)
                            inc(v.tensor_scalar(h_sb[b][:], x_sb[b][:],
                                                stat[:, so:so + 1],
                                                stat[:, so + 2:so + 3],
                                                op0=ALU.subtract, op1=ALU.mult),
                                "sLN")
                        snap("hact", j, "sLN")

                        @blk.sync
                        def _(sy, j=j, b=b):
                            sy.wait_ge(sems["sLN"], hget("hact", j))
                            inc(sy.dma_start(h_dram[128 * j:128 * j + 128, :],
                                             h_sb[b][:]), f"sHF{j}", 16)

                        if (j + 1) % gsz == 0:
                            grp_ = (j + 1) // gsz - 1
                            r0 = 128 * gsz * grp_
                            nr = 128 * gsz
                            hsem = f"sHFM{grp_}"

                            @blk.sync
                            def _(sy, grp_=grp_, r0=r0, nr=nr, hsem=hsem):
                                for jd in range(grp_ * gsz,
                                                (grp_ + 1) * gsz):
                                    sy.wait_ge(sems[f"sHF{jd}"], 16)
                                if grp_ == 0:
                                    W(sy, "sMISC", ltd_init_done)
                                for kk in range(16):
                                    inc(sy.dma_start(
                                        h_fm[:, T * kk + r0:T * kk + r0 + nr],
                                        h_dram[r0:r0 + nr,
                                               128 * kk:128 * kk + 128],
                                        transpose=True), hsem, 16)

                        if j == min(2, nt - 1):
                            @blk.gpsimd
                            def _(g):
                                src = AP(wqT_d.tensor, 0,
                                         [[DIM, 128], [128 * DIM, 16],
                                          [1, 512]])
                                dst = AP(wbuf[0], 0,
                                         [[8192, 128], [512, 16], [1, 512]])
                                inc(g.dma_start(dst, src), "sW0", 16)
                        if j == min(6, nt - 1):
                            @blk.gpsimd
                            def _(g):
                                src = AP(wqT_d.tensor, 512,
                                         [[DIM, 128], [128 * DIM, 16],
                                          [1, 512]])
                                dst = AP(wbuf[1], 0,
                                         [[8192, 128], [512, 16], [1, 512]])
                                inc(g.dma_start(dst, src), "sW1", 16)

                wts = [wqT_d, wkT_d, wvT_d]
                drams = [q_dram, k_dram, v_dram]
                for wc in range(12):
                    tsr, ch = wc // 4, wc % 4

                    if wc >= 2:
                        @blk.gpsimd
                        def _(g, wc=wc, tsr=tsr, ch=ch):
                            g.wait_ge(sems["sQMM"], (wc - 1) * nt)
                            src = AP(wts[tsr].tensor, 512 * ch,
                                     [[DIM, 128], [128 * DIM, 16], [1, 512]])
                            dst = AP(wbuf[wc % 2], 0,
                                     [[8192, 128], [512, 16], [1, 512]])
                            inc(g.dma_start(dst, src), f"sW{wc % 2}", 16)

                    for j in range(nt):
                        cidx = wc * nt + j

                        @blk.tensor
                        def _(t, wc=wc, j=j, cidx=cidx):
                            t.wait_ge(sems[f"sHFM{min(j // gsz, ngrp - 1)}"],
                                      256)
                            t.wait_ge(sems[f"sW{wc % 2}"], 16 * (wc // 2 + 1))
                            if cidx >= 4:
                                t.wait_ge(sems["sEV"], cidx - 3)
                            pb = psb[cidx % 4]
                            mm = None
                            for kk in range(16):
                                mm = t.matmul(
                                    pb[:],
                                    h_fm[:, T * kk + 128 * j:T * kk + 128 * j + 128],
                                    wbuf[wc % 2][:, 512 * kk:512 * kk + 512],
                                    start=(kk == 0), stop=(kk == 15))
                            inc(mm, "sQMM")

                        @blk.vector
                        def _(v, cidx=cidx):
                            v.wait_ge(sems["sQMM"], cidx + 1)
                            if cidx >= 2:
                                v.wait_ge(sems[f"sQKVD{cidx % 2}"],
                                          16 * ((cidx - 2) // 2 + 1))
                            inc(v.tensor_copy(stg[cidx % 2][:], psb[cidx % 4][:]),
                                "sEV")

                        @blk.sync
                        def _(sy, tsr=tsr, ch=ch, j=j, cidx=cidx):
                            sy.wait_ge(sems["sEV"], cidx + 1)
                            inc(sy.dma_start(
                                drams[tsr][128 * j:128 * j + 128,
                                           512 * ch:512 * ch + 512],
                                stg[cidx % 2][:]), f"sQKVD{cidx % 2}", 16)

                paW.close()
                woT_sb = paWo.enter_context(
                    nc.sbuf_tensor("woT_sb", [128, 16 * DIM], BF16))
                bo_b = paWo.enter_context(nc.sbuf_tensor("bo_b", [1, DIM], BF16))

                lhsT1 = [[[pa.enter_context(
                    nc.sbuf_tensor(f"lhsT1_{p_}_{d}_{hh}", [128, 2048], BF16))
                    for hh in range(2)] for d in range(2)] for p_ in range(2)]
                rhs1 = [[pa.enter_context(
                    nc.sbuf_tensor(f"rhs1_{p_}_{d}", [128, 1024], BF16))
                    for d in range(2)] for p_ in range(2)]
                vst = [pa.enter_context(
                    nc.sbuf_tensor(f"vst{i}", [128, 2048], BF16))
                    for i in range(4)]
                lhsT2 = [pa.enter_context(
                    nc.sbuf_tensor(f"lhsT2_{i}", [128, 2048], BF16))
                    for i in range(4)]
                E1f = pa.enter_context(nc.sbuf_tensor("E1f", [128, 512], F32))
                E2f = pa.enter_context(nc.sbuf_tensor("E2f", [128, 512], F32))
                Ssum = pa.enter_context(nc.sbuf_tensor("Ssum", [128, 64], F32))
                D_sb = [pa.enter_context(
                    nc.sbuf_tensor(f"D_sb{i}", [128, 256], BF16))
                    for i in range(2)]
                o2_sb = pa.enter_context(nc.sbuf_tensor("o2_sb", [128, 2048], BF16))
                rmsd = pa.enter_context(nc.sbuf_tensor("rmsd", [128, 32], F32))
                Rb = pa.enter_context(nc.sbuf_tensor("Rb", [128, 1024], F32))
                QKVD_TOT = (C["sQKVD0"], C["sQKVD1"])
                psEsets = [(psE1, psE2), (psb[2], psb[3])]
                banks4 = [psM[0], psM[1], psb[0], psb[1]]

                @blk.vector
                def _(v):
                    v.wait_ge(sems["sQMM"], 12 * nt)
                    v.wait_ge(sems[f"sHF{nt - 1}"], 16)
                    for p_ in range(2):
                        for dd in range(2):
                            for hh in range(2):
                                inc(v.memset(lhsT1[p_][dd][hh][:], 0.0), "sLAM")
                            inc(v.memset(rhs1[p_][dd][:], 0.0), "sLAM")
                    for i in range(4):
                        inc(v.memset(vst[i][:], 0.0), "sLAM")
                        inc(v.memset(lhsT2[i][:], 0.0), "sLAM")
                lhsT1_zeroed = C["sLAM"]

                def em_repack(t):
                    tp = t % 2

                    @blk.sync
                    def _(sy, t=t, tp=tp):
                        sy.wait_ge(sems["sQKVD0"], QKVD_TOT[0])
                        sy.wait_ge(sems["sQKVD1"], QKVD_TOT[1])
                        if t < 2:
                            sy.wait_ge(sems["sLAM"], lhsT1_zeroed)
                        if t >= 2:
                            sy.wait_ge(sems["sATT"], t - 1)
                        tj = 128 * t
                        for dd in range(2):
                            for hh in range(2):
                                for s in range(4):
                                    dst = AP(lhsT1[tp][dd][hh],
                                             (32 * s) * 2048 + 16 * s,
                                             [[2048, 32], [64, 32], [1, 16]])
                                    src = AP(q_dram.tensor,
                                             (tj + s) * DIM + 1024 * dd + 16 * hh,
                                             [[32, 32], [4 * DIM, 32], [1, 16]])
                                    inc(sy.dma_start(dst, src), f"sRP{tp}", 16)
                    snap("rp", t, f"sRP{t % 2}")

                    @blk.scalar
                    def _(sc, t=t, tp=tp):
                        sc.wait_ge(sems["sQKVD0"], QKVD_TOT[0])
                        sc.wait_ge(sems["sQKVD1"], QKVD_TOT[1])
                        if t < 2:
                            sc.wait_ge(sems["sLAM"], lhsT1_zeroed)
                        if t >= 2:
                            sc.wait_ge(sems["sATT"], t - 1)
                        tj = 128 * t
                        for dd in range(2):
                            for s in range(4):
                                dst = AP(rhs1[tp][dd], (32 * s) * 1024,
                                         [[1024, 32], [32, 32], [1, 32]])
                                src = AP(k_dram.tensor,
                                         (tj + s) * DIM + 1024 * dd,
                                         [[32, 32], [4 * DIM, 32], [1, 32]])
                                inc(sc.dma_start(dst, src), f"sRPK{tp}", 16)
                    snap("rpk", t, f"sRPK{t % 2}")

                def em_front(j):
                    jp = j % 2
                    j4 = j % 4
                    tj = 128 * j

                    @blk.sync
                    def _(sy, j=j, tj=tj, j4=j4):
                        sy.wait_ge(sems["sQKVD0"], QKVD_TOT[0])
                        sy.wait_ge(sems["sQKVD1"], QKVD_TOT[1])
                        if j < 4:
                            sy.wait_ge(sems["sLAM"], lhsT1_zeroed)
                        for H in range(2):
                            for s in range(4):
                                for G in range(2):
                                    q2 = 8 * H + 2 * s + G
                                    dst = AP(v2_dram.tensor,
                                             128 * j * DIM + 128 * q2,
                                             [[16 * DIM, 8], [DIM, 16], [1, 128]])
                                    src = AP(v_dram.tensor,
                                             (tj + 64 * H + 32 * G + s) * DIM,
                                             [[4 * DIM, 8], [128, 16], [1, 128]])
                                    inc(sy.dma_start(dst, src), "sVST", 16)
                        sy.wait_ge(sems["sVST"], C["sVST"])
                        if j >= 4:
                            sy.wait_ge(sems["sM2"], 4 * (j - 3))
                        inc(sy.dma_start(vst[j4][:],
                                         v2_dram[128 * j:128 * j + 128, :]),
                            f"sVSTL{j4}", 16)
                    snap("vst", j, f"sVSTL{j % 4}")

                    @blk.tensor
                    def _(t, j=j, jp=jp):
                        t.wait_ge(sems[f"sRP{jp}"], hget("rp", j))
                        t.wait_ge(sems[f"sRPK{jp}"], hget("rpk", j))
                        if j >= 2:
                            t.wait_ge(sems["sEXP"], 2 * (j - 1))
                        eb1, eb2 = psEsets[jp]
                        mm = None
                        for p in range(32):
                            H, g16 = p // 16, p % 16
                            for hh, bank in ((0, eb1), (1, eb2)):
                                t.matmul(bank[64 * H:64 * H + 64,
                                              32 * g16:32 * g16 + 32],
                                         lhsT1[jp][0][hh][:, 64 * p:64 * p + 64],
                                         rhs1[jp][0][:, 32 * p:32 * p + 32],
                                         start=True, stop=False)
                                mm = t.matmul(
                                    bank[64 * H:64 * H + 64,
                                         32 * g16:32 * g16 + 32],
                                    lhsT1[jp][1][hh][:, 64 * p:64 * p + 64],
                                    rhs1[jp][1][:, 32 * p:32 * p + 32],
                                    start=False, stop=True)
                        inc(mm, "sATT")

                    @blk.scalar
                    def _(sc, j=j, jp=jp):
                        sc.wait_ge(sems["sATT"], j + 1)
                        if j >= 1:
                            sc.wait_ge(sems["sD"], j)
                        eb1, eb2 = psEsets[jp]
                        inc(sc.activation(E1f[:], eb1[:], AF.Exp, scale=0.125),
                            "sEXP")
                        inc(sc.activation(E2f[:], eb2[:], AF.Exp, scale=0.125),
                            "sEXP")

                    @blk.vector
                    def _(v, j=j, jp=jp):
                        v.wait_ge(sems["sEXP"], 2 * (j + 1))
                        if j >= 2:
                            v.wait_ge(sems[f"sDD{jp}"],
                                      256 * ((j - 2) // 2 + 1))
                        fence(v, v.reduce_sum(
                            Ssum[:, 0:16],
                            E1f[:].rearrange("p (g j) -> p g j", g=16),
                            axis=AX.X))
                        fence(v, v.reduce_sum(
                            Ssum[:, 32:48],
                            E2f[:].rearrange("p (g j) -> p g j", g=16),
                            axis=AX.X))
                        fence(v, v.reciprocal(Ssum[:, 16:32], Ssum[:, 0:16]))
                        fence(v, v.reciprocal(Ssum[:, 48:64], Ssum[:, 32:48]))
                        v.tensor_copy(
                            AP(Rb, 0, [[1024, 128], [32, 16], [1, 32]]),
                            AP(Ssum, 16, [[64, 128], [1, 16], [0, 32]]))
                        v.tensor_copy(
                            AP(Rb, 512, [[1024, 128], [32, 16], [1, 32]]),
                            AP(Ssum, 48, [[64, 128], [1, 16], [0, 32]]))
                        inc(v.tensor_tensor(E1f[:], E1f[:], Rb[:, 0:512],
                                            op=ALU.mult), "sSMX")
                        inc(v.tensor_tensor(E2f[:], E2f[:], Rb[:, 512:1024],
                                            op=ALU.mult), "sSMX")
                        inc(v.scalar_tensor_tensor(
                            AP(D_sb[jp], 0, [[256, 128], [16, 16], [1, 16]]),
                            AP(E2f, 16, [[512, 128], [32, 16], [1, 16]]),
                            neglam_b[:],
                            AP(E1f, 0, [[512, 128], [32, 16], [1, 16]]),
                            op0=ALU.mult, op1=ALU.add), "sD")

                    @blk.gpsimd
                    def _(g, j=j, jp=jp, j4=j4):
                        g.wait_ge(sems["sD"], j + 1)
                        if j >= 4:
                            g.wait_ge(sems[f"sLT2{j4}"],
                                      16 * ((j - 4) // 4 + 1))
                        if j < 4:
                            g.wait_ge(sems["sMISC"], ltd_init_done)
                        for H in range(2):
                            for s in range(4):
                                for G in range(2):
                                    q2 = 8 * H + 2 * s + G
                                    src = AP(D_sb[jp],
                                             (64 * H + 16 * s) * 256 + 128 * G,
                                             [[256, 16], [16, 8], [1, 16]])
                                    dst = AP(ltd[j4].tensor, 128 * 128 * q2,
                                             [[128, 16], [2064, 8], [1, 16]])
                                    inc(g.dma_start(dst, src), f"sDD{jp}", 16)

                def em_load(t):
                    t4 = t % 4
                    tp = t % 2

                    @blk.scalar
                    def _(sc, t=t, t4=t4, tp=tp):
                        sc.wait_ge(sems[f"sDD{tp}"], 256 * (t // 2 + 1))
                        if t < 4:
                            sc.wait_ge(sems["sLAM"], lhsT1_zeroed)
                        if t >= 4:
                            sc.wait_ge(sems["sM2"], 4 * (t - 3))
                        inc(sc.dma_start(lhsT2[t4][:], ltd[t4][:, :],
                                         transpose=True), f"sLT2{t4}", 16)

                def em_back(jj):
                    j4 = jj % 4
                    tjj = 128 * jj
                    for grp in range(4):
                        @blk.tensor
                        def _(t, jj=jj, j4=j4, grp=grp):
                            t.wait_ge(sems[f"sLT2{j4}"], 16 * (jj // 4 + 1))
                            t.wait_ge(sems[f"sVSTL{j4}"], hget("vst", jj))
                            if jj >= 1:
                                t.wait_ge(sems["sO2"],
                                          16 * (jj - 1) + 4 * (grp + 1))
                            bank = banks4[grp]
                            mm = None
                            for qq in range(4):
                                q2 = grp * 4 + qq
                                mm = t.matmul(
                                    bank[:, 128 * qq:128 * qq + 128],
                                    lhsT2[j4][:, 128 * q2:128 * q2 + 128],
                                    vst[j4][:, 128 * q2:128 * q2 + 128],
                                    start=True, stop=True)
                            inc(mm, "sM2")

                        @blk.scalar
                        def _(sc, jj=jj, grp=grp):
                            sc.wait_ge(sems["sM2"], jj * 4 + grp + 1)
                            W(sc, "sRMS")
                            inc(sc.activation(scrB[:], banks4[grp][:],
                                              AF.Square), "sRMS")

                        @blk.vector
                        def _(v, grp=grp):
                            W(v, "sRMS")
                            inc(v.reduce_sum(rmsd[:, 4 * grp:4 * grp + 4],
                                             scrB[:].rearrange(
                                                 "p (a b) -> p a b", a=4),
                                             axis=AX.X), "sRMS")

                    @blk.scalar
                    def _(sc):
                        W(sc, "sRMS")
                        fence(sc, sc.activation(rmsd[:, 16:32],
                                                rmsd[:, 0:16], AF.Ln,
                                                bias=EPS, scale=1.0 / 128))
                        inc(sc.activation(rmsd[:, 16:32],
                                          rmsd[:, 16:32], AF.Exp,
                                          bias=lnoml_b[:, 0:1], scale=-0.5),
                            "sRMS")

                    @blk.vector
                    def _(v, jj=jj):
                        W(v, "sRMS")
                        if jj >= 1:
                            v.wait_ge(sems["sO2D"], 256 * jj)
                        for q2 in range(16):
                            inc(v.tensor_scalar(
                                o2_sb[:, 128 * q2:128 * q2 + 128],
                                banks4[q2 // 4][:, 128 * (q2 % 4):
                                                128 * (q2 % 4) + 128],
                                rmsd[:, 16 + q2:17 + q2], None,
                                op0=ALU.mult), "sO2")

                    @blk.gpsimd
                    def _(g, jj=jj, tjj=tjj):
                        g.wait_ge(sems["sO2"], 16 * (jj + 1))
                        inc(g.dma_start(o2p_dram[128 * jj:128 * jj + 128, :],
                                        o2_sb[:]), "sO2P", 16)
                        g.wait_ge(sems["sO2P"], C["sO2P"])
                        for H in range(2):
                            for s in range(4):
                                for G in range(2):
                                    q2 = 8 * H + 2 * s + G
                                    src = AP(o2p_dram.tensor,
                                             128 * jj * DIM + 128 * q2,
                                             [[16 * DIM, 8], [DIM, 16], [1, 128]])
                                    dst = AP(o2_dram.tensor,
                                             (tjj + 64 * H + 32 * G + s) * DIM,
                                             [[4 * DIM, 8], [128, 16], [1, 128]])
                                    inc(g.dma_start(dst, src), "sO2D", 16)

                def em_wot_load(a0, a1, first):
                    @blk.gpsimd
                    def _(g, a0=a0, a1=a1, first=first):
                        g.wait_ge(sems["sLAM"], C["sLAM"])
                        if first:
                            inc(g.dma_start(bo_b[:], bo_d[:, :]), "sWOT", 16)
                        for a in range(a0, a1):
                            src = AP(woT_d.tensor, 128 * a * DIM,
                                     [[DIM, 128], [1, DIM]])
                            dst = AP(woT_sb, DIM * a,
                                     [[16 * DIM, 128], [1, DIM]])
                            inc(g.dma_start(dst, src), "sWOT", 16)

                for it in range(nt + 2):
                    if it == 0:
                        em_repack(0)
                        if nt > 1:
                            em_repack(1)
                    elif it + 1 <= nt - 1:
                        em_repack(it + 1)
                    if it < nt:
                        em_front(it)
                    if 1 <= it <= nt:
                        em_load(it - 1)
                    if it >= 2:
                        em_back(it - 2)
                    if nt >= 12:
                        if 2 <= it <= 9:
                            @blk.vector
                            def _(v, it=it):
                                if it == 2:
                                    v.wait_ge(sems["sQMM"], 12 * nt)
                                    inc(v.memset(bo_b[:], 0.0), "sLAM")
                                c0 = 4096 * (it - 2)
                                inc(v.memset(woT_sb[:, c0:c0 + 4096], 0.0),
                                    "sLAM")
                        if 3 <= it <= 10:
                            em_wot_load(2 * (it - 3), 2 * (it - 3) + 2,
                                        it == 3)
                    elif it == 2:
                        @blk.vector
                        def _(v):
                            v.wait_ge(sems["sQMM"], 12 * nt)
                            inc(v.memset(bo_b[:], 0.0), "sLAM")
                            inc(v.memset(woT_sb[:], 0.0), "sLAM")
                        em_wot_load(0, 16, True)

            A4_GATE = dict(sO2D=256 * nt, sM2=4 * nt, sATT=nt, sD=nt,
                           sO2=16 * nt, sEXP=2 * nt, sRMS=C["sRMS"],
                           sDD0=C["sDD0"], sDD1=C["sDD1"],
                           sLT20=C["sLT20"], sLT21=C["sLT21"],
                           sLT22=C["sLT22"], sLT23=C["sLT23"],
                           sVSTL0=C["sVSTL0"], sVSTL1=C["sVSTL1"],
                           sVSTL2=C["sVSTL2"], sVSTL3=C["sVSTL3"],
                           sRP0=C["sRP0"], sRP1=C["sRP1"],
                           sRPK0=C["sRPK0"], sRPK1=C["sRPK1"])

            def a4gate(eng):
                for k, v in A4_GATE.items():
                    eng.wait_ge(sems[k], v)

            pa5 = contextlib.ExitStack()
            with pa5:
                o2_fm = pa5.enter_context(
                    nc.sbuf_tensor("o2_fm", [128, 16 * T], BF16))
                x_sb5 = [pa5.enter_context(
                    nc.sbuf_tensor(f"x5{i}", [128, 2048], F32)) for i in range(2)]
                h2_sb = [pa5.enter_context(
                    nc.sbuf_tensor(f"h2{i}", [128, 2048], BF16)) for i in range(2)]
                scr5 = pa5.enter_context(nc.sbuf_tensor("scr5", [128, 2048], F32))

                @blk.vector
                def _(v):
                    a4gate(v)
                    for i in range(2):
                        inc(v.memset(x_sb5[i][:], 0.0), "sLAM")
                        inc(v.memset(h2_sb[i][:], 0.0), "sLAM")
                pa5_init = C["sLAM"]

                @blk.sync
                def _(sy):
                    a4gate(sy)
                    sy.wait_ge(sems["sO2D"], 256 * nt)
                    sy.wait_ge(sems["sLAM"], pa5_init)
                    for a in range(16):
                        inc(sy.dma_start(o2_fm[:, T * a:T * a + T],
                                         o2_dram[:, 128 * a:128 * a + 128],
                                         transpose=True), "sOFM", 16)

                base_qmm = C["sQMM"]
                for j in range(nt):
                    b = j % 2
                    so = 8 * b

                    @blk.gpsimd
                    def _(g, j=j, b=b):
                        a4gate(g)
                        if j < 2:
                            g.wait_ge(sems["sLAM"], pa5_init)
                        if j >= 2:
                            g.wait_ge(sems[f"sX1D{b}"], 16 * ((j - 2) // 2 + 1))
                        inc(g.dma_start(x_sb5[b][:], x_d[128 * j:128 * j + 128, :]),
                            f"sXL{b}", 16)
                    snap("x5", j, f"sXL{b}")

                    for ch in range(4):
                        cidx = j * 4 + ch

                        @blk.tensor
                        def _(t, j=j, ch=ch, cidx=cidx):
                            if cidx < 4:
                                t.wait_ge(sems["sEV"], 12 * nt)
                            t.wait_ge(sems["sWOT"], 272)
                            t.wait_ge(sems["sOFM"], 256)
                            if cidx >= 4:
                                t.wait_ge(sems["sX1"], cidx - 3)
                            pb = psb[cidx % 4]
                            for a in range(16):
                                t.matmul(
                                    pb[:],
                                    o2_fm[:, T * a + 128 * j:T * a + 128 * j + 128],
                                    woT_sb[:, DIM * a + 512 * ch:
                                           DIM * a + 512 * ch + 512],
                                    start=(a == 0), stop=False)
                            mm = t.matmul(pb[:], ones_b[:],
                                          bo_b[:, 512 * ch:512 * ch + 512],
                                          start=False, stop=True)
                            inc(mm, "sQMM")

                        @blk.vector
                        def _(v, j=j, ch=ch, b=b, cidx=cidx):
                            v.wait_ge(sems["sQMM"], base_qmm + cidx + 1)
                            v.wait_ge(sems[f"sXL{b}"], hget("x5", j))
                            inc(v.tensor_tensor(
                                x_sb5[b][:, 512 * ch:512 * ch + 512],
                                x_sb5[b][:, 512 * ch:512 * ch + 512],
                                psb[cidx % 4][:], op=ALU.add), "sX1")

                    @blk.vector
                    def _(v, j=j, b=b, so=so):
                        v.wait_ge(sems["sX1"], 4 * (j + 1))
                        if j >= 2:
                            v.wait_ge(sems["sH2"], j - 1)
                        fence(v, v.reduce_sum(stat[:, so:so + 1], x_sb5[b][:],
                                              axis=AX.X))
                        fence(v, v.tensor_scalar(stat[:, so:so + 1],
                                                 stat[:, so:so + 1], 1.0 / DIM,
                                                 None, op0=ALU.mult))
                        inc(v.scalar_tensor_tensor(
                            scr5[:], x_sb5[b][:], stat[:, so:so + 1], x_sb5[b][:],
                            op0=ALU.subtract, op1=ALU.mult), "sLN")
                        inc(v.reduce_sum(stat[:, so + 1:so + 2], scr5[:],
                                         axis=AX.X), "sLN")
                    snap("ln2_stats", j, "sLN")

                    @blk.scalar
                    def _(sc, j=j, so=so):
                        sc.wait_ge(sems["sLN"], hget("ln2_stats", j))
                        fence(sc, sc.activation(stat[:, so + 2:so + 3],
                                                stat[:, so + 1:so + 2], AF.Ln,
                                                bias=EPS, scale=1.0 / DIM))
                        inc(sc.activation(stat[:, so + 2:so + 3],
                                          stat[:, so + 2:so + 3], AF.Exp,
                                          scale=-0.5), "sLN")
                    snap("ln2_rstd", j, "sLN")

                    @blk.vector
                    def _(v, j=j, b=b, so=so):
                        v.wait_ge(sems["sLN"], hget("ln2_rstd", j))
                        if j >= 2:
                            v.wait_ge(sems[f"sH2D{b}"],
                                      16 * ((j - 2) // 2 + 1))
                        inc(v.tensor_scalar(h2_sb[b][:], x_sb5[b][:],
                                            stat[:, so:so + 1],
                                            stat[:, so + 2:so + 3],
                                            op0=ALU.subtract, op1=ALU.mult),
                            "sH2")

                    @blk.sync
                    def _(sy, j=j, b=b):
                        sy.wait_ge(sems["sH2"], j + 1)
                        inc(sy.dma_start(h2_dram[128 * j:128 * j + 128, :],
                                         h2_sb[b][:]), f"sH2D{b}", 16)
                        inc(sy.dma_start(x1_dram[128 * j:128 * j + 128, :],
                                         x_sb5[b][:]), f"sX1D{b}", 16)

            paWo.close()
            A5_GATE = dict(sQMM=C["sQMM"], sH2D0=C["sH2D0"], sH2D1=C["sH2D1"],
                           sX1D0=C["sX1D0"], sX1D1=C["sX1D1"])

            def a5gate(eng):
                for k, v in A5_GATE.items():
                    eng.wait_ge(sems[k], v)

            pbO = contextlib.ExitStack()
            with pbO:
                h2_fm = pbO.enter_context(
                    nc.sbuf_tensor("h2_fm", [128, 16 * T], BF16))
                gw_sb = pbO.enter_context(nc.sbuf_tensor("gw_sb", [128, 144], BF16))
                comb9 = pbO.enter_context(
                    nc.sbuf_tensor("comb9", [128, 16 * nt], F32))
                gstat = pbO.enter_context(nc.sbuf_tensor("gstat", [128, 32], F32))

                @blk.vector
                def _(v):
                    a5gate(v)
                    inc(v.memset(h2_fm[:], 0.0), "sLAM")
                    inc(v.memset(gw_sb[:], 0.0), "sLAM")
                pb0_init = C["sLAM"]

                @blk.sync
                def _(sy):
                    a5gate(sy)
                    sy.wait_ge(sems["sLAM"], pb0_init)
                    for kk in range(16):
                        inc(sy.dma_start(h2_fm[:, T * kk:T * kk + T],
                                         h2_dram[:, 128 * kk:128 * kk + 128],
                                         transpose=True), "sH2FM", 16)

                @blk.gpsimd
                def _(g):
                    a5gate(g)
                    g.wait_ge(sems["sLAM"], pb0_init)
                    src = AP(gwT_d.tensor, 0, [[9, 128], [128 * 9, 16], [1, 9]])
                    dst = AP(gw_sb, 0, [[144, 128], [9, 16], [1, 9]])
                    inc(g.dma_start(dst, src), "sGW", 16)

                for j in range(nt):
                    go = 16 * (j % 2)
                    sbo = 16 * (j % 2)

                    @blk.tensor
                    def _(t, j=j):
                        t.wait_ge(sems["sH2FM"], 256)
                        t.wait_ge(sems["sGW"], 16)
                        if j >= 2:
                            t.wait_ge(sems["sGTV"], hget("gt2", j - 2))
                        bank = psM[j % 2]
                        mm = None
                        for kk in range(16):
                            mm = t.matmul(bank[:, 0:9],
                                          h2_fm[:, T * kk + 128 * j:
                                                T * kk + 128 * j + 128],
                                          gw_sb[:, 9 * kk:9 * kk + 9],
                                          start=(kk == 0), stop=(kk == 15))
                        inc(mm, "sSCM")

                    @blk.vector
                    def _(v, j=j, go=go, sbo=sbo):
                        v.wait_ge(sems["sSCM"], j + 1)
                        if j >= 2:
                            v.wait_ge(sems["sCMB"], hget("cmb", j - 2))
                        sc_ = psM[j % 2][:, 0:8]
                        fence(v, v.reduce_max(gstat[:, go:go + 1], sc_,
                                              axis=AX.X))
                        fence(v, v.tensor_scalar(gstat[:, go + 2:go + 10], sc_,
                                                 gstat[:, go:go + 1], None,
                                                 op0=ALU.is_ge))
                        fence(v, v.scalar_tensor_tensor(
                            scrB[:, sbo:sbo + 8], gstat[:, go + 2:go + 10],
                            -1e30, sc_, op0=ALU.mult, op1=ALU.add))
                        fence(v, v.reduce_max(gstat[:, go + 1:go + 2],
                                              scrB[:, sbo:sbo + 8], axis=AX.X))
                        fence(v, v.tensor_tensor(gstat[:, go + 10:go + 11],
                                                 gstat[:, go:go + 1],
                                                 gstat[:, go + 1:go + 2],
                                                 op=ALU.subtract))
                        fence(v, v.tensor_scalar(scrB[:, sbo + 8:sbo + 16], sc_,
                                                 gstat[:, go + 1:go + 2], None,
                                                 op0=ALU.is_ge))
                        inc(v.tensor_tensor(scrB[:, sbo + 8:sbo + 16],
                                            scrB[:, sbo + 8:sbo + 16],
                                            gstat[:, go + 2:go + 10],
                                            op=ALU.subtract), "sGTV")
                    snap("gt1", j, "sGTV")

                    @blk.scalar
                    def _(sc, j=j, go=go):
                        sc.wait_ge(sems["sGTV"], hget("gt1", j))
                        sc.wait_ge(sems["sFEN"], C["sFEN"])
                        inc(sc.activation(gstat[:, go + 12:go + 13],
                                          gstat[:, go + 10:go + 11], AF.Sigmoid),
                            "sGTV")
                        inc(sc.activation(gstat[:, go + 13:go + 14],
                                          gstat[:, go + 10:go + 11], AF.Sigmoid,
                                          scale=-1.0), "sGTV")
                        inc(sc.activation(comb9[:, 16 * j + 8:16 * j + 9],
                                          psM[j % 2][:, 8:9], AF.Sigmoid), "sGTV")
                    snap("gt2", j, "sGTV")

                    @blk.vector
                    def _(v, j=j, go=go, sbo=sbo):
                        v.wait_ge(sems["sGTV"], hget("gt2", j))
                        fence(v, v.tensor_scalar(gstat[:, go + 2:go + 10],
                                                 gstat[:, go + 2:go + 10],
                                                 gstat[:, go + 12:go + 13], None,
                                                 op0=ALU.mult))
                        inc(v.scalar_tensor_tensor(
                            comb9[:, 16 * j:16 * j + 8],
                            scrB[:, sbo + 8:sbo + 16],
                            gstat[:, go + 13:go + 14],
                            gstat[:, go + 2:go + 10],
                            op0=ALU.mult, op1=ALU.add), "sCMB")
                    snap("cmb", j, "sCMB")

                pbW = contextlib.ExitStack()
                with pbW:
                    hid = [pbW.enter_context(
                        nc.sbuf_tensor(f"hid{i}", [128, 8 * T], BF16))
                        for i in range(2)]
                    wup = [[pbW.enter_context(
                        nc.sbuf_tensor(f"wup{gu}{i}", [128, 2048], BF16))
                        for i in range(2)] for gu in range(2)]
                    wdn = [pbW.enter_context(
                        nc.sbuf_tensor(f"wdn{i}", [128, 4096], BF16))
                        for i in range(4)]
                    comb_bc = pbW.enter_context(
                        nc.sbuf_tensor("comb_bc", [128, T], BF16))
                    sg_sb = pbW.enter_context(
                        nc.sbuf_tensor("sg_sb", [128, 512], BF16))
                    us_sb = pbW.enter_context(
                        nc.sbuf_tensor("us_sb", [128, 512], BF16))
                    stg_dn = [pbW.enter_context(
                        nc.sbuf_tensor(f"stgdn{i}", [128, 512], F32))
                        for i in range(2)]
                    comb_row = pbW.enter_context(
                        nc.sbuf_tensor("comb_row", [1, T], BF16))
                    @blk.vector
                    def _(v):
                        a5gate(v)
                        for gu in range(2):
                            for i in range(2):
                                inc(v.memset(wup[gu][i][:], 0.0), "sLAM")
                        for i in range(4):
                            inc(v.memset(wdn[i][:], 0.0), "sLAM")
                        inc(v.memset(comb_row[:], 0.0), "sLAM")
                    pbw_init = C["sLAM"]

                    up_emitted = set()

                    def em_upload(E, ch, gu):
                        up_emitted.add((E, ch, gu))
                        wsrc = upg_d if gu == 0 else upu_d

                        @blk.gpsimd
                        def _(g, E=E, ch=ch, gu=gu, wsrc=wsrc):
                            li = (E * UPC + ch) * 2 + gu
                            if li < 4:
                                a5gate(g)
                                g.wait_ge(sems["sLAM"], pbw_init)
                            if li >= 4:
                                g.wait_ge(sems["sUMM"],
                                          hget("upw_free", li - 4))
                            src = AP(wsrc.tensor,
                                     E * DIM * 1024 + 128 * ch,
                                     [[1024, 128], [128 * 1024, 16],
                                      [1, 128]])
                            dst = AP(wup[gu][ch % 2], 0,
                                     [[2048, 128], [128, 16], [1, 128]])
                            inc(g.dma_start(dst, src),
                                f"sUW{gu * 2 + ch % 2}", 16)
                        snap("uw", (E * UPC + ch) * 2 + gu,
                             f"sUW{gu * 2 + ch % 2}")

                    NE = 10
                    UPC = 8
                    base_x1 = C["sX1"]

                    up_chain = 0
                    dn_chain = 0
                    for E in range(NE):
                        eb = E % 2
                        rE = E if E < 8 else 8

                        for ci, (t0, tl) in enumerate(TCH):
                            ntile = tl // 128

                            @blk.tensor
                            def _(t, E=E, rE=rE, ci=ci, t0=t0, ntile=ntile):
                                t.wait_ge(sems["sCMB"], C["sCMB"])
                                t.wait_ge(sems["sGTV"], C["sGTV"])
                                if ci >= 2 or E >= 1:
                                    t.wait_ge(sems["sCBE"],
                                              max(0, C["sCBE"] - 1))
                                mm = None
                                for jj in range(ntile):
                                    jt = (t0 // 128) + jj
                                    mm = t.matmul(
                                        psM[ci % 2][0:1, 128 * jj:128 * jj + 128],
                                        comb9[:, 16 * jt + rE:16 * jt + rE + 1],
                                        ident[:], start=True, stop=True)
                                inc(mm, "sCB")

                            @blk.vector
                            def _(v, E=E, ci=ci, t0=t0, tl=tl):
                                W(v, "sCB")
                                if E >= 1:
                                    v.wait_ge(sems["sCB"], hget("bcast", E - 1))
                                inc(v.tensor_copy(comb_row[0:1, t0:t0 + tl],
                                                  psM[ci % 2][0:1, 0:tl]), "sCR")

                            @blk.tensor
                            def _(t, E=E, ci=ci, t0=t0, tl=tl):
                                t.wait_ge(sems["sCR"], C["sCR"])
                                inc(t.matmul(psM[ci % 2][:, 0:tl], ones_b[:],
                                             comb_row[0:1, t0:t0 + tl],
                                             start=True, stop=True), "sCB")

                            @blk.vector
                            def _(v, E=E, ci=ci, t0=t0, tl=tl):
                                W(v, "sCB")
                                inc(v.tensor_copy(comb_bc[:, t0:t0 + tl],
                                                  psM[ci % 2][:, 0:tl]), "sCBE")
                        snap("bcast", E, "sCB")

                        for ch in range(UPC):
                            for gu in range(2):
                                if (E, ch, gu) in up_emitted:
                                    continue
                                em_upload(E, ch, gu)

                            for ci, (t0, tl) in enumerate(TCH):
                                @blk.tensor
                                def _(t, E=E, ch=ch, ci=ci, t0=t0, tl=tl, eb=eb,
                                      up_chain=up_chain):
                                    li = (E * UPC + ch) * 2
                                    t.wait_ge(sems[f"sUW{0 * 2 + ch % 2}"],
                                              hget("uw", li))
                                    t.wait_ge(sems[f"sUW{1 * 2 + ch % 2}"],
                                              hget("uw", li + 1))
                                    t.wait_ge(sems["sCBE"], C["sCBE"])
                                    if E >= 2 and ch == 0 and ci == 0:
                                        t.wait_ge(sems["sDMM"],
                                                  hget("dn_done", E - 2))
                                    if up_chain == 0 and ci == 0:
                                        t.wait_ge(sems["sX1"], base_x1)
                                    if E >= 1 and E % 2 == 0 and ch == 0 and ci == 0:
                                        t.wait_ge(sems["sDNE"], C["sDNE"])
                                    for gu in range(2):
                                        cno = up_chain + (ci * 2 + gu)
                                        if cno >= 4:
                                            t.wait_ge(sems["sUPE"],
                                                      hget("up_cons", cno - 4))
                                        pb = psb[cno % 4]
                                        mm = None
                                        for kk in range(16):
                                            mm = t.matmul(
                                                pb[:, 0:tl],
                                                wup[gu][ch % 2][:,
                                                                128 * kk:128 * kk + 128],
                                                h2_fm[:, T * kk + t0:
                                                      T * kk + t0 + tl],
                                                start=(kk == 0), stop=(kk == 15))
                                        inc(mm, "sUMM")

                                @blk.scalar
                                def _(sc, E=E, ch=ch, ci=ci, tl=tl,
                                      up_chain=up_chain):
                                    sc.wait_ge(sems["sUMM"], up_chain + ci * 2 + 1)
                                    W(sc, "sUPE")
                                    gbank = psb[(up_chain + ci * 2) % 4]
                                    inc(sc.activation(sg_sb[:, 0:tl],
                                                      gbank[:, 0:tl], AF.Silu),
                                        "sUPE")
                                    snap("up_cons",
                                         up_chain + ci * 2, "sUPE")

                                @blk.vector
                                def _(v, E=E, ch=ch, ci=ci, t0=t0, tl=tl, eb=eb,
                                      up_chain=up_chain):
                                    v.wait_ge(sems["sUMM"], up_chain + ci * 2 + 2)
                                    W(v, "sUPE")
                                    ubank = psb[(up_chain + ci * 2 + 1) % 4]
                                    inc(v.tensor_tensor(us_sb[:, 0:tl],
                                                        ubank[:, 0:tl],
                                                        comb_bc[:, t0:t0 + tl],
                                                        op=ALU.mult), "sUPE")
                                    snap("up_cons",
                                         up_chain + ci * 2 + 1, "sUPE")
                                    inc(v.tensor_tensor(
                                        AP(hid[eb], T * ch + t0,
                                           [[8 * T, 128], [1, tl]]),
                                        sg_sb[:, 0:tl], us_sb[:, 0:tl],
                                        op=ALU.mult), "sHID")

                            up_chain += 2 * len(TCH)
                            snap("upw_free", (E * UPC + ch) * 2, "sUMM")
                            snap("upw_free", (E * UPC + ch) * 2 + 1, "sUMM")

                        if E % 2 == 1:
                            pair = E // 2

                            def em_dload(pair, cc, ei):
                                @blk.gpsimd
                                def _(g, pair=pair, cc=cc, ei=ei):
                                    li = (pair * 4 + cc) * 2 + ei
                                    if li < 4:
                                        g.wait_ge(sems["sLAM"], pbw_init)
                                    if li >= 4:
                                        g.wait_ge(sems["sDMM"],
                                                  hget("dnw_free", li - 4))
                                    Ex = 2 * pair + ei
                                    src = AP(dnw_d.tensor,
                                             Ex * 1024 * DIM + 512 * cc,
                                             [[DIM, 128], [128 * DIM, 8],
                                              [1, 512]])
                                    dst = AP(wdn[(cc % 2) * 2 + ei], 0,
                                             [[4096, 128], [512, 8], [1, 512]])
                                    inc(g.dma_start(dst, src),
                                        f"sDW{(cc % 2) * 2 + ei}", 16)
                                    snap("dw", (pair * 4 + cc) * 2 + ei,
                                         f"sDW{(cc % 2) * 2 + ei}")

                            for cc in range(4):
                                if cc == 0:
                                    for ei in range(2):
                                        em_dload(pair, 0, ei)
                                    for ei in range(2):
                                        em_dload(pair, 1, ei)
                                elif cc + 1 <= 3:
                                    for ei in range(2):
                                        em_dload(pair, cc + 1, ei)

                                if cc == 0 and E + 1 < NE:
                                    for ch_ in range(2):
                                        for gu_ in range(2):
                                            em_upload(E + 1, ch_, gu_)

                                for j in range(nt):
                                    @blk.tensor
                                    def _(t, pair=pair, cc=cc, j=j,
                                          dn_chain=dn_chain, upc=up_chain):
                                        li = (pair * 4 + cc) * 2
                                        t.wait_ge(sems[f"sDW{(cc % 2) * 2}"],
                                                  hget("dw", li))
                                        t.wait_ge(sems[f"sDW{(cc % 2) * 2 + 1}"],
                                                  hget("dw", li + 1))
                                        t.wait_ge(sems["sHID"], C["sHID"])
                                        cno = dn_chain + j
                                        if cno < dn_chain + 4 and cc == 0 and j < 4:
                                            t.wait_ge(sems["sUPE"], C["sUPE"])
                                        if cno >= 4:
                                            t.wait_ge(sems["sDNE"], cno - 3)
                                        pb = psb[cno % 4]
                                        mm = None
                                        for ei in range(2):
                                            for ht in range(8):
                                                mm = t.matmul(
                                                    pb[:],
                                                    AP(hid[ei], T * ht + 128 * j,
                                                       [[8 * T, 128], [1, 128]]),
                                                    wdn[(cc % 2) * 2 + ei][
                                                        :, 512 * ht:512 * ht + 512],
                                                    start=(ei == 0 and ht == 0),
                                                    stop=(ei == 1 and ht == 7))
                                        inc(mm, "sDMM")

                                    @blk.scalar
                                    def _(sc, cc=cc, j=j, dn_chain=dn_chain):
                                        cno = dn_chain + j
                                        sc.wait_ge(sems["sDMM"], cno + 1)
                                        if cno >= 2:
                                            sc.wait_ge(sems[f"sACC{cno % 2}"],
                                                       16 * ((cno - 2) // 2 + 1))
                                        inc(sc.copy(stg_dn[cno % 2][:],
                                                    psb[cno % 4][:]), "sDNE")

                                    @blk.gpsimd
                                    def _(g, pair=pair, cc=cc, j=j,
                                          dn_chain=dn_chain):
                                        cno = dn_chain + j
                                        g.wait_ge(sems["sDNE"], cno + 1)
                                        op = ALU.bypass if pair == 0 else ALU.add
                                        inc(g.dma_start(
                                            acc_dram[128 * j:128 * j + 128,
                                                     512 * cc:512 * cc + 512],
                                            stg_dn[cno % 2][:], accum_op=op),
                                            f"sACC{cno % 2}", 16)

                                dn_chain += nt
                                snap("dnw_free", (pair * 4 + cc) * 2, "sDMM")
                                snap("dnw_free", (pair * 4 + cc) * 2 + 1, "sDMM")
                            snap("dn_done", E, "sDMM")

                pb4 = contextlib.ExitStack()
                with pb4:
                    x1b = [pb4.enter_context(
                        nc.sbuf_tensor(f"x1b{i}", [128, 2048], F32))
                        for i in range(2)]
                    accb = [pb4.enter_context(
                        nc.sbuf_tensor(f"accb{i}", [128, 2048], F32))
                        for i in range(2)]
                    ACC_TOTAL = (C["sACC0"], C["sACC1"])

                    @blk.vector
                    def _(v):
                        v.wait_ge(sems["sACC0"], ACC_TOTAL[0])
                        v.wait_ge(sems["sACC1"], ACC_TOTAL[1])
                        for i in range(2):
                            inc(v.memset(x1b[i][:], 0.0), "sLAM")
                            inc(v.memset(accb[i][:], 0.0), "sLAM")
                    pb4_init = C["sLAM"]

                    for j in range(nt):
                        b = j % 2

                        @blk.gpsimd
                        def _(g, j=j, b=b):
                            g.wait_ge(sems["sACC0"], ACC_TOTAL[0])
                            g.wait_ge(sems["sACC1"], ACC_TOTAL[1])
                            g.wait_ge(sems["sLAM"], pb4_init)
                            if j >= 2:
                                g.wait_ge(sems[f"sY{b}"],
                                          16 * ((j - 2) // 2 + 1))
                            inc(g.dma_start(x1b[b][:],
                                            x1_dram[128 * j:128 * j + 128, :]),
                                f"sFB{b}", 16)
                            inc(g.dma_start(accb[b][:],
                                            acc_dram[128 * j:128 * j + 128, :]),
                                f"sFB{b}", 16)

                        @blk.vector
                        def _(v, j=j, b=b):
                            v.wait_ge(sems[f"sFB{b}"], 32 * (j // 2 + 1))
                            inc(v.tensor_tensor(x1b[b][:], x1b[b][:], accb[b][:],
                                                op=ALU.add), "sFIN")

                        @blk.sync
                        def _(sy, j=j, b=b):
                            sy.wait_ge(sems["sFIN"], j + 1)
                            inc(sy.dma_start(y_d[128 * j:128 * j + 128, :],
                                             x1b[b][:]), f"sY{b}", 16)

                    if taps:
                        @blk.sync
                        def _(sy):
                            sy.wait_ge(sems["sY0"], C["sY0"])
                            sy.wait_ge(sems["sY1"], C["sY1"])
                            for name, out in tap_outs.items():
                                src = scratch[name]
                                inc(sy.dma_start(out, src[:, :]), "sY0", 16)

                    @blk.gpsimd
                    def _(g):
                        g.wait_ge(sems["sY0"], C["sY0"])
                        g.wait_ge(sems["sY1"], C["sY1"])

    return nc


def _prep(inputs):
    f32 = np.float32
    g = lambda k: np.asarray(inputs[k], dtype=f32)
    cp = np.arange(DIM)
    i_ = cp % 32
    d_ = cp // 32
    pidx = (i_ % 16) * 128 + 2 * d_ + (i_ // 16)

    wq = g("L0_attn_Wq")[pidx, :]
    wk = g("L0_attn_Wk")[pidx, :]
    shg = g("L0_ffn_sh_gate")
    shu = g("L0_ffn_sh_up")
    shd = g("L0_ffn_sh_down")
    wg = g("L0_ffn_w_gate")
    wu = g("L0_ffn_w_up")
    wd = g("L0_ffn_w_down")

    upg = np.concatenate(
        [np.transpose(wg, (0, 2, 1)),
         np.transpose(shg, (1, 0)).reshape(DIM, 2, 1024).transpose(1, 0, 2)],
        axis=0)
    upu = np.concatenate(
        [np.transpose(wu, (0, 2, 1)),
         np.transpose(shu, (1, 0)).reshape(DIM, 2, 1024).transpose(1, 0, 2)],
        axis=0)
    dnw = np.concatenate(
        [np.transpose(wd, (0, 2, 1)),
         np.transpose(shd, (1, 0)).reshape(2, 1024, DIM)],
        axis=0)

    wm = {
        "wqT": np.ascontiguousarray(wq.T),
        "wkT": np.ascontiguousarray(wk.T),
        "wvT": np.ascontiguousarray(g("L0_attn_Wv").T),
        "woT": np.ascontiguousarray(g("L0_attn_Wo").T),
        "gwT": np.ascontiguousarray(
            np.concatenate([g("L0_ffn_gate_w"), g("L0_ffn_se_gate_w")], 0).T),
        "upg": np.ascontiguousarray(upg),
        "upu": np.ascontiguousarray(upu),
        "dnw": np.ascontiguousarray(dnw),
        "lama": np.ascontiguousarray(
            np.concatenate([g("L0_attn_lq1"), g("L0_attn_lq2")])[None, :]),
        "lamb": np.ascontiguousarray(
            np.concatenate([g("L0_attn_lk1"), g("L0_attn_lk2")])[None, :]),
        "bo": np.ascontiguousarray(g("L0_attn_bo")[None, :]),
        "ident": np.eye(128, dtype=f32),
    }
    c64 = np.zeros((1, 4), dtype=f32)
    for i, c in enumerate([LAM_INIT, LN_OML]):
        hi = np.float32(c)
        lo = np.float32(c - np.float64(hi))
        c64[0, 2 * i] = hi
        c64[0, 2 * i + 1] = lo
    wm["c64"] = c64
    x = np.ascontiguousarray(g("L0_attn_x"))
    return x, wm


def kernel(**inputs):
    x, wm = _prep(inputs)
    nc = build(nt=16)
    in_maps = []
    for c in range(NCORES):
        m = {"x": np.ascontiguousarray(x[c])}
        m.update(wm)
        in_maps.append(m)
    res = run_bass_kernel_spmd(nc, in_maps, core_ids=list(range(NCORES)))
    out = np.stack([np.asarray(r["y"], dtype=np.float32)
                    for r in res.results], axis=0)
    return out


# version 14 (final)
# speedup vs baseline: 1.0984x; 1.0252x over previous
import contextlib
import numpy as np
import ml_dtypes
import concourse.bass as bass
import concourse.mybir as mybir
from concourse.ap import AP
from concourse.bass_utils import run_bass_kernel_spmd

F32 = mybir.dt.float32
BF16 = mybir.dt.bfloat16
AF = mybir.ActivationFunctionType
ALU = mybir.AluOpType
AX = mybir.AxisListType
bfloat16 = ml_dtypes.bfloat16

DIM = 2048
NCORES = 8
LAM_INIT = 0.8 - 0.6 * float(np.exp(-3.6))
OML = 1.0 - LAM_INIT
LN_OML = float(np.log(OML))
EPS = 1e-5


def _tchunks(T):
    return [(c, min(512, T - c)) for c in range(0, T, 512)]


def build(nt=16, taps=()):
    T = nt * 128
    TCH = _tchunks(T)
    nc = bass.Bass("TRN2", target_bir_lowering=False, debug=False,
                   detect_race_conditions=False)

    x_d = nc.dram_tensor("x", [T, DIM], F32, kind="ExternalInput").ap()
    wqT_d = nc.dram_tensor("wqT", [DIM, DIM], F32, kind="ExternalInput").ap()
    wkT_d = nc.dram_tensor("wkT", [DIM, DIM], F32, kind="ExternalInput").ap()
    wvT_d = nc.dram_tensor("wvT", [DIM, DIM], F32, kind="ExternalInput").ap()
    woT_d = nc.dram_tensor("woT", [DIM, DIM], F32, kind="ExternalInput").ap()
    gwT_d = nc.dram_tensor("gwT", [DIM, 9], F32, kind="ExternalInput").ap()
    upg_d = nc.dram_tensor("upg", [10, DIM, 1024], F32, kind="ExternalInput").ap()
    upu_d = nc.dram_tensor("upu", [10, DIM, 1024], F32, kind="ExternalInput").ap()
    dnw_d = nc.dram_tensor("dnw", [10, 1024, DIM], F32, kind="ExternalInput").ap()
    lama_d = nc.dram_tensor("lama", [1, 128], F32, kind="ExternalInput").ap()
    lamb_d = nc.dram_tensor("lamb", [1, 128], F32, kind="ExternalInput").ap()
    bo_d = nc.dram_tensor("bo", [1, DIM], F32, kind="ExternalInput").ap()
    id_d = nc.dram_tensor("ident", [128, 128], F32, kind="ExternalInput").ap()
    c64_d = nc.dram_tensor("c64", [1, 4], F32, kind="ExternalInput").ap()
    y_d = nc.dram_tensor("y", [T, DIM], F32, kind="ExternalOutput").ap()

    h_dram = nc.dram_tensor("h_dram", [T, DIM], BF16, kind="Internal").ap()
    q_dram = nc.dram_tensor("q_dram", [T, DIM], BF16, kind="Internal").ap()
    k_dram = nc.dram_tensor("k_dram", [T, DIM], BF16, kind="Internal").ap()
    v_dram = nc.dram_tensor("v_dram", [T, DIM], BF16, kind="Internal").ap()
    ltd = [nc.dram_tensor(f"ltd{i}", [2048, 128], BF16, kind="Internal").ap()
           for i in range(4)]
    v2_dram = nc.dram_tensor("v2_dram", [T, DIM], BF16, kind="Internal").ap()
    o2p_dram = nc.dram_tensor("o2p_dram", [T, DIM], BF16, kind="Internal").ap()
    o2_dram = nc.dram_tensor("o2_dram", [T, DIM], BF16, kind="Internal").ap()
    x1_dram = nc.dram_tensor("x1_dram", [T, DIM], F32, kind="Internal").ap()
    h2_dram = nc.dram_tensor("h2_dram", [T, DIM], BF16, kind="Internal").ap()
    acc_dram = nc.dram_tensor("acc_dram", [T, DIM], F32, kind="Internal").ap()
    scratch = {"h_dram": h_dram, "q_dram": q_dram, "k_dram": k_dram,
               "v_dram": v_dram, "o2_dram": o2_dram, "x1_dram": x1_dram,
               "h2_dram": h2_dram, "acc_dram": acc_dram,
               "ltd0": ltd[0], "ltd1": ltd[1]}
    tap_outs = {}
    for name in taps:
        src = scratch[name]
        tap_outs[name] = nc.dram_tensor(
            "tap_" + name, list(src.shape), src.dtype, kind="ExternalOutput").ap()

    es = contextlib.ExitStack()
    sems = {}
    C = {}

    def mksem(name):
        sems[name] = es.enter_context(nc.semaphore(name))
        C[name] = 0

    def inc(instr, name, k=1):
        instr.then_inc(sems[name], k)
        C[name] += k

    def W(eng, name, val=None):
        eng.wait_ge(sems[name], C[name] if val is None else val)

    def fence(eng, instr):
        inc(instr, "sFEN")
        eng.wait_ge(sems["sFEN"], C["sFEN"])
        return instr

    for s in ["sXL0", "sXL1", "sLN", "sHD0", "sHD1", "sHFM", "sW0", "sW1",
              "sQMM", "sEV", "sQKVD0", "sQKVD1",
              "sRP0", "sRP1", "sRPK0", "sRPK1",
              "sATT", "sEXP", "sSMX", "sD", "sDD0", "sDD1",
              "sLT20", "sLT21", "sLT22", "sLT23",
              "sVST", "sVSC", "sM2", "sRMS", "sO2", "sO2D", "sO2P",
              "sVSTL0", "sVSTL1", "sVSTL2", "sVSTL3",
              "sWOT", "sOFM", "sX1", "sX1D0", "sX1D1",
              "sH2", "sH2D0", "sH2D1", "sH2FM", "sGW", "sSCM", "sGTV", "sCMB",
              "sCR", "sCB", "sCBE", "sUW0", "sUW1", "sUW2", "sUW3",
              "sUWF0", "sUWF1", "sUWF2", "sUWF3", "sUCP",
              "sUMM", "sUPE", "sHID", "sDW0", "sDW1", "sDW2", "sDW3", "sDMM",
              "sDNE", "sACC0", "sACC1", "sFB0", "sFB1", "sFIN", "sY0", "sY1",
              "sMISC", "sLAM", "sFEN"]:
        mksem(s)
    for g_ in range(4):
        mksem(f"sHFM{g_}")
    for j_ in range(nt):
        mksem(f"sHF{j_}")
    ngrp = 4 if nt % 4 == 0 and nt >= 8 else 2
    gsz = nt // ngrp

    hist = {}

    def snap(key, idx, semname):
        hist.setdefault(key, {})[idx] = C[semname]

    def hget(key, idx, default=0):
        return hist.get(key, {}).get(idx, default)

    with es:
        blk = es.enter_context(nc.Block())

        ones_f = es.enter_context(nc.sbuf_tensor("ones_f", [1, 128], F32))
        ones_b = es.enter_context(nc.sbuf_tensor("ones_b", [1, 128], BF16))
        ident = es.enter_context(nc.sbuf_tensor("ident_sb", [128, 128], F32))
        lam_sb = es.enter_context(nc.sbuf_tensor("lam_sb", [1, 256], F32))
        lamt = es.enter_context(nc.sbuf_tensor("lamt", [1, 256], F32))
        neglam_b = es.enter_context(nc.sbuf_tensor("neglam_b", [128, 1], F32))
        lnoml_b = es.enter_context(nc.sbuf_tensor("lnoml_b", [128, 1], F32))
        cst = es.enter_context(nc.sbuf_tensor("cst", [128, 2], F32))
        nc.const_aps.register(F32, EPS, cst[:, 0:1])
        nc.const_aps.register(F32, LN_OML, cst[:, 1:2])
        scrB = es.enter_context(nc.sbuf_tensor("scrB", [128, 512], F32))
        stat = es.enter_context(nc.sbuf_tensor("stat", [128, 16], F32))
        stg = [es.enter_context(nc.sbuf_tensor(f"stg{i}", [128, 512], BF16))
               for i in range(2)]
        psb = [es.enter_context(nc.psum_tensor(f"psb{i}", [128, 512], F32))
               for i in range(4)]
        psE1 = es.enter_context(nc.psum_tensor("psE1", [128, 512], F32))
        psE2 = es.enter_context(nc.psum_tensor("psE2", [128, 512], F32))
        psM = [es.enter_context(nc.psum_tensor(f"psM{i}", [128, 512], F32))
               for i in range(2)]

        z0 = contextlib.ExitStack()
        with z0:
            zz = z0.enter_context(nc.sbuf_tensor("zz", [128, 512], BF16))

            @blk.gpsimd
            def _(g):
                inc(g.dma_start(lam_sb[:, 0:128], lama_d[:, :]), "sMISC", 16)
                inc(g.dma_start(lam_sb[:, 128:256], lamb_d[:, :]), "sMISC", 16)
                inc(g.dma_start(ident[:], id_d[:, :]), "sMISC", 16)
                inc(g.dma_start(lamt[:, 140:144], c64_d[:, :]), "sMISC", 16)

            @blk.vector
            def _(v):
                inc(v.memset(ones_f[:], 1.0), "sLAM")
                inc(v.memset(ones_b[:], 1.0), "sLAM")
                inc(v.memset(zz[:], 0.0), "sLAM")
                inc(v.memset(cst[:, 0:1], EPS), "sLAM")
                inc(v.memset(cst[:, 1:2], LN_OML), "sLAM")
                W(v, "sMISC")
                fence(v, v.tensor_tensor(lamt[:, 0:128], lam_sb[:, 0:128],
                                         lam_sb[:, 128:256], op=ALU.mult))
                inc(v.reduce_sum(lamt[:, 128:130],
                                 lamt[:, 0:128].rearrange("p (a b) -> p a b", a=2),
                                 axis=AX.X), "sLAM")

            @blk.scalar
            def _(sc):
                W(sc, "sLAM")
                inc(sc.activation(lamt[:, 130:132], lamt[:, 128:130], AF.Exp),
                    "sLAM")

            @blk.vector
            def _(v):
                W(v, "sLAM")
                fence(v, v.tensor_tensor(lamt[:, 132:133], lamt[:, 130:131],
                                         lamt[:, 131:132], op=ALU.subtract))
                fence(v, v.tensor_tensor(lamt[:, 134:135], lamt[:, 140:141],
                                         lamt[:, 141:142], op=ALU.add))
                fence(v, v.tensor_tensor(lamt[:, 135:136], lamt[:, 142:143],
                                         lamt[:, 143:144], op=ALU.add))
                fence(v, v.tensor_tensor(lamt[:, 133:134], lamt[:, 132:133],
                                         lamt[:, 134:135], op=ALU.add))
                inc(v.tensor_scalar(lamt[:, 133:134], lamt[:, 133:134], -1.0,
                                    None, op0=ALU.mult), "sLAM")

            @blk.tensor
            def _(t):
                W(t, "sLAM")
                t.matmul(psM[0][:, 0:1], ones_f[:], lamt[:, 133:134],
                         start=True, stop=True)
                inc(t.matmul(psM[0][:, 4:5], ones_f[:], lamt[:, 135:136],
                             start=True, stop=True), "sLAM")

            @blk.vector
            def _(v):
                W(v, "sLAM")
                fence(v, v.tensor_copy(neglam_b[:], psM[0][:, 0:1]))
                inc(v.tensor_copy(lnoml_b[:], psM[0][:, 4:5]), "sLAM")

            @blk.gpsimd
            def _(g):
                W(g, "sLAM")
                for i in range(4):
                    for qtr in range(4):
                        inc(g.dma_start(
                            AP(ltd[i].tensor, 512 * qtr,
                               [[2048, 128], [1, 512]]), zz[:]),
                            "sMISC", 16)
            ltd_init_done = C["sMISC"]

            pa = contextlib.ExitStack()
            paWo = contextlib.ExitStack()
            with pa:
                paW = contextlib.ExitStack()
                scr = paW.enter_context(nc.sbuf_tensor("scr", [128, 2048], F32))
                h_fm = paW.enter_context(
                    nc.sbuf_tensor("h_fm", [128, 16 * T], BF16))
                wbuf = [paW.enter_context(
                    nc.sbuf_tensor(f"wbuf{i}", [128, 8192], BF16))
                    for i in range(2)]

                a1 = contextlib.ExitStack()
                with a1:
                    x_sb = [a1.enter_context(
                        nc.sbuf_tensor(f"x{i}", [128, 2048], F32)) for i in range(2)]
                    h_sb = [a1.enter_context(
                        nc.sbuf_tensor(f"h{i}", [128, 2048], BF16)) for i in range(2)]

                    for j in range(nt):
                        b = j % 2
                        so = 8 * b

                        @blk.gpsimd
                        def _(g, j=j, b=b):
                            if j >= 2:
                                g.wait_ge(sems["sLN"], hget("hact", j - 2))
                            inc(g.dma_start(x_sb[b][:],
                                            x_d[128 * j:128 * j + 128, :]),
                                f"sXL{b}", 16)

                        @blk.vector
                        def _(v, j=j, b=b, so=so):
                            v.wait_ge(sems[f"sXL{b}"], 16 * (j // 2 + 1))
                            if j >= 2:
                                v.wait_ge(sems["sLN"], hget("hact", j - 2))
                            fence(v, v.reduce_sum(stat[:, so:so + 1], x_sb[b][:],
                                                  axis=AX.X))
                            fence(v, v.tensor_scalar(stat[:, so:so + 1],
                                                     stat[:, so:so + 1],
                                                     1.0 / DIM, None,
                                                     op0=ALU.mult))
                            inc(v.scalar_tensor_tensor(
                                scr[:], x_sb[b][:], stat[:, so:so + 1], x_sb[b][:],
                                op0=ALU.subtract, op1=ALU.mult), "sLN")
                            inc(v.reduce_sum(stat[:, so + 1:so + 2], scr[:],
                                             axis=AX.X), "sLN")
                        snap("ln_stats", j, "sLN")

                        @blk.scalar
                        def _(sc, j=j, so=so):
                            sc.wait_ge(sems["sLN"], hget("ln_stats", j))
                            fence(sc, sc.activation(stat[:, so + 2:so + 3],
                                                    stat[:, so + 1:so + 2], AF.Ln,
                                                    bias=EPS, scale=1.0 / DIM))
                            inc(sc.activation(stat[:, so + 2:so + 3],
                                              stat[:, so + 2:so + 3], AF.Exp,
                                              scale=-0.5), "sLN")
                        snap("ln_rstd", j, "sLN")

                        @blk.vector
                        def _(v, j=j, b=b, so=so):
                            v.wait_ge(sems["sLN"], hget("ln_rstd", j))
                            if j >= 2:
                                v.wait_ge(sems[f"sHF{j - 2}"], 16)
                            inc(v.tensor_scalar(h_sb[b][:], x_sb[b][:],
                                                stat[:, so:so + 1],
                                                stat[:, so + 2:so + 3],
                                                op0=ALU.subtract, op1=ALU.mult),
                                "sLN")
                        snap("hact", j, "sLN")

                        @blk.sync
                        def _(sy, j=j, b=b):
                            sy.wait_ge(sems["sLN"], hget("hact", j))
                            inc(sy.dma_start(h_dram[128 * j:128 * j + 128, :],
                                             h_sb[b][:]), f"sHF{j}", 16)

                        if (j + 1) % gsz == 0:
                            grp_ = (j + 1) // gsz - 1
                            r0 = 128 * gsz * grp_
                            nr = 128 * gsz
                            hsem = f"sHFM{grp_}"

                            @blk.sync
                            def _(sy, grp_=grp_, r0=r0, nr=nr, hsem=hsem):
                                for jd in range(grp_ * gsz,
                                                (grp_ + 1) * gsz):
                                    sy.wait_ge(sems[f"sHF{jd}"], 16)
                                if grp_ == 0:
                                    W(sy, "sMISC", ltd_init_done)
                                for kk in range(16):
                                    inc(sy.dma_start(
                                        h_fm[:, T * kk + r0:T * kk + r0 + nr],
                                        h_dram[r0:r0 + nr,
                                               128 * kk:128 * kk + 128],
                                        transpose=True), hsem, 16)

                        if j == min(2, nt - 1):
                            @blk.gpsimd
                            def _(g):
                                src = AP(wqT_d.tensor, 0,
                                         [[DIM, 128], [128 * DIM, 16],
                                          [1, 512]])
                                dst = AP(wbuf[0], 0,
                                         [[8192, 128], [512, 16], [1, 512]])
                                inc(g.dma_start(dst, src), "sW0", 16)
                        if j == min(6, nt - 1):
                            @blk.gpsimd
                            def _(g):
                                src = AP(wqT_d.tensor, 512,
                                         [[DIM, 128], [128 * DIM, 16],
                                          [1, 512]])
                                dst = AP(wbuf[1], 0,
                                         [[8192, 128], [512, 16], [1, 512]])
                                inc(g.dma_start(dst, src), "sW1", 16)

                wts = [wqT_d, wkT_d, wvT_d]
                drams = [q_dram, k_dram, v_dram]
                for wc in range(12):
                    tsr, ch = wc // 4, wc % 4

                    if wc >= 2:
                        @blk.gpsimd
                        def _(g, wc=wc, tsr=tsr, ch=ch):
                            g.wait_ge(sems["sQMM"], (wc - 1) * nt)
                            src = AP(wts[tsr].tensor, 512 * ch,
                                     [[DIM, 128], [128 * DIM, 16], [1, 512]])
                            dst = AP(wbuf[wc % 2], 0,
                                     [[8192, 128], [512, 16], [1, 512]])
                            inc(g.dma_start(dst, src), f"sW{wc % 2}", 16)

                    for j in range(nt):
                        cidx = wc * nt + j

                        @blk.tensor
                        def _(t, wc=wc, j=j, cidx=cidx):
                            t.wait_ge(sems[f"sHFM{min(j // gsz, ngrp - 1)}"],
                                      256)
                            t.wait_ge(sems[f"sW{wc % 2}"], 16 * (wc // 2 + 1))
                            if cidx >= 4:
                                t.wait_ge(sems["sEV"], cidx - 3)
                            pb = psb[cidx % 4]
                            mm = None
                            for kk in range(16):
                                mm = t.matmul(
                                    pb[:],
                                    h_fm[:, T * kk + 128 * j:T * kk + 128 * j + 128],
                                    wbuf[wc % 2][:, 512 * kk:512 * kk + 512],
                                    start=(kk == 0), stop=(kk == 15))
                            inc(mm, "sQMM")

                        @blk.vector
                        def _(v, cidx=cidx):
                            v.wait_ge(sems["sQMM"], cidx + 1)
                            if cidx >= 2:
                                v.wait_ge(sems[f"sQKVD{cidx % 2}"],
                                          16 * ((cidx - 2) // 2 + 1))
                            inc(v.tensor_copy(stg[cidx % 2][:], psb[cidx % 4][:]),
                                "sEV")

                        @blk.sync
                        def _(sy, tsr=tsr, ch=ch, j=j, cidx=cidx):
                            sy.wait_ge(sems["sEV"], cidx + 1)
                            inc(sy.dma_start(
                                drams[tsr][128 * j:128 * j + 128,
                                           512 * ch:512 * ch + 512],
                                stg[cidx % 2][:]), f"sQKVD{cidx % 2}", 16)

                paW.close()
                woT_sb = paWo.enter_context(
                    nc.sbuf_tensor("woT_sb", [128, 16 * DIM], BF16))
                bo_b = paWo.enter_context(nc.sbuf_tensor("bo_b", [1, DIM], BF16))

                lhsT1 = [[[pa.enter_context(
                    nc.sbuf_tensor(f"lhsT1_{p_}_{d}_{hh}", [128, 2048], BF16))
                    for hh in range(2)] for d in range(2)] for p_ in range(2)]
                rhs1 = [[pa.enter_context(
                    nc.sbuf_tensor(f"rhs1_{p_}_{d}", [128, 1024], BF16))
                    for d in range(2)] for p_ in range(2)]
                vst = [pa.enter_context(
                    nc.sbuf_tensor(f"vst{i}", [128, 2048], BF16))
                    for i in range(4)]
                lhsT2 = [pa.enter_context(
                    nc.sbuf_tensor(f"lhsT2_{i}", [128, 2048], BF16))
                    for i in range(4)]
                E1f = pa.enter_context(nc.sbuf_tensor("E1f", [128, 512], F32))
                E2f = pa.enter_context(nc.sbuf_tensor("E2f", [128, 512], F32))
                Ssum = pa.enter_context(nc.sbuf_tensor("Ssum", [128, 64], F32))
                D_sb = [pa.enter_context(
                    nc.sbuf_tensor(f"D_sb{i}", [128, 256], BF16))
                    for i in range(2)]
                o2_sb = pa.enter_context(nc.sbuf_tensor("o2_sb", [128, 2048], BF16))
                rmsd = pa.enter_context(nc.sbuf_tensor("rmsd", [128, 32], F32))
                Rb = pa.enter_context(nc.sbuf_tensor("Rb", [128, 1024], F32))
                QKVD_TOT = (C["sQKVD0"], C["sQKVD1"])
                psEsets = [(psE1, psE2), (psb[2], psb[3])]
                banks4 = [psM[0], psM[1], psb[0], psb[1]]

                @blk.vector
                def _(v):
                    v.wait_ge(sems["sQMM"], 12 * nt)
                    v.wait_ge(sems[f"sHF{nt - 1}"], 16)
                    for p_ in range(2):
                        for dd in range(2):
                            for hh in range(2):
                                inc(v.memset(lhsT1[p_][dd][hh][:], 0.0), "sLAM")
                            inc(v.memset(rhs1[p_][dd][:], 0.0), "sLAM")
                    for i in range(4):
                        inc(v.memset(vst[i][:], 0.0), "sLAM")
                        inc(v.memset(lhsT2[i][:], 0.0), "sLAM")
                lhsT1_zeroed = C["sLAM"]

                def em_repack(t):
                    tp = t % 2

                    @blk.sync
                    def _(sy, t=t, tp=tp):
                        sy.wait_ge(sems["sQKVD0"], QKVD_TOT[0])
                        sy.wait_ge(sems["sQKVD1"], QKVD_TOT[1])
                        if t < 2:
                            sy.wait_ge(sems["sLAM"], lhsT1_zeroed)
                        if t >= 2:
                            sy.wait_ge(sems["sATT"], t - 1)
                        tj = 128 * t
                        for dd in range(2):
                            for hh in range(2):
                                for s in range(4):
                                    dst = AP(lhsT1[tp][dd][hh],
                                             (32 * s) * 2048 + 16 * s,
                                             [[2048, 32], [64, 32], [1, 16]])
                                    src = AP(q_dram.tensor,
                                             (tj + s) * DIM + 1024 * dd + 16 * hh,
                                             [[32, 32], [4 * DIM, 32], [1, 16]])
                                    inc(sy.dma_start(dst, src), f"sRP{tp}", 16)
                    snap("rp", t, f"sRP{t % 2}")

                    @blk.scalar
                    def _(sc, t=t, tp=tp):
                        sc.wait_ge(sems["sQKVD0"], QKVD_TOT[0])
                        sc.wait_ge(sems["sQKVD1"], QKVD_TOT[1])
                        if t < 2:
                            sc.wait_ge(sems["sLAM"], lhsT1_zeroed)
                        if t >= 2:
                            sc.wait_ge(sems["sATT"], t - 1)
                        tj = 128 * t
                        for dd in range(2):
                            for s in range(4):
                                dst = AP(rhs1[tp][dd], (32 * s) * 1024,
                                         [[1024, 32], [32, 32], [1, 32]])
                                src = AP(k_dram.tensor,
                                         (tj + s) * DIM + 1024 * dd,
                                         [[32, 32], [4 * DIM, 32], [1, 32]])
                                inc(sc.dma_start(dst, src), f"sRPK{tp}", 16)
                    snap("rpk", t, f"sRPK{t % 2}")

                def em_front(j):
                    jp = j % 2
                    j4 = j % 4
                    tj = 128 * j

                    @blk.scalar
                    def _(sc, j=j, tj=tj):
                        sc.wait_ge(sems["sQKVD0"], QKVD_TOT[0])
                        sc.wait_ge(sems["sQKVD1"], QKVD_TOT[1])
                        for H in range(2):
                            for s in range(2, 4):
                                for G in range(2):
                                    q2 = 8 * H + 2 * s + G
                                    dst = AP(v2_dram.tensor,
                                             128 * j * DIM + 128 * q2,
                                             [[16 * DIM, 8], [DIM, 16], [1, 128]])
                                    src = AP(v_dram.tensor,
                                             (tj + 64 * H + 32 * G + s) * DIM,
                                             [[4 * DIM, 8], [128, 16], [1, 128]])
                                    inc(sc.dma_start(dst, src), "sVSC", 16)

                    @blk.sync
                    def _(sy, j=j, tj=tj, j4=j4):
                        sy.wait_ge(sems["sQKVD0"], QKVD_TOT[0])
                        sy.wait_ge(sems["sQKVD1"], QKVD_TOT[1])
                        if j < 4:
                            sy.wait_ge(sems["sLAM"], lhsT1_zeroed)
                        for H in range(2):
                            for s in range(2):
                                for G in range(2):
                                    q2 = 8 * H + 2 * s + G
                                    dst = AP(v2_dram.tensor,
                                             128 * j * DIM + 128 * q2,
                                             [[16 * DIM, 8], [DIM, 16], [1, 128]])
                                    src = AP(v_dram.tensor,
                                             (tj + 64 * H + 32 * G + s) * DIM,
                                             [[4 * DIM, 8], [128, 16], [1, 128]])
                                    inc(sy.dma_start(dst, src), "sVST", 16)
                        sy.wait_ge(sems["sVST"], C["sVST"])
                        sy.wait_ge(sems["sVSC"], C["sVSC"])
                        if j >= 4:
                            sy.wait_ge(sems["sM2"], 4 * (j - 3))
                        inc(sy.dma_start(vst[j4][:],
                                         v2_dram[128 * j:128 * j + 128, :]),
                            f"sVSTL{j4}", 16)
                    snap("vst", j, f"sVSTL{j % 4}")

                    @blk.tensor
                    def _(t, j=j, jp=jp):
                        t.wait_ge(sems[f"sRP{jp}"], hget("rp", j))
                        t.wait_ge(sems[f"sRPK{jp}"], hget("rpk", j))
                        if j >= 2:
                            t.wait_ge(sems["sEXP"], 2 * (j - 1))
                        eb1, eb2 = psEsets[jp]
                        mm = None
                        for p in range(32):
                            H, g16 = p // 16, p % 16
                            for hh, bank in ((0, eb1), (1, eb2)):
                                t.matmul(bank[64 * H:64 * H + 64,
                                              32 * g16:32 * g16 + 32],
                                         lhsT1[jp][0][hh][:, 64 * p:64 * p + 64],
                                         rhs1[jp][0][:, 32 * p:32 * p + 32],
                                         start=True, stop=False)
                                mm = t.matmul(
                                    bank[64 * H:64 * H + 64,
                                         32 * g16:32 * g16 + 32],
                                    lhsT1[jp][1][hh][:, 64 * p:64 * p + 64],
                                    rhs1[jp][1][:, 32 * p:32 * p + 32],
                                    start=False, stop=True)
                        inc(mm, "sATT")

                    @blk.scalar
                    def _(sc, j=j, jp=jp):
                        sc.wait_ge(sems["sATT"], j + 1)
                        if j >= 1:
                            sc.wait_ge(sems["sD"], j)
                        eb1, eb2 = psEsets[jp]
                        inc(sc.activation(E1f[:], eb1[:], AF.Exp, scale=0.125),
                            "sEXP")
                        inc(sc.activation(E2f[:], eb2[:], AF.Exp, scale=0.125),
                            "sEXP")

                    @blk.vector
                    def _(v, j=j, jp=jp):
                        v.wait_ge(sems["sEXP"], 2 * (j + 1))
                        if j >= 2:
                            v.wait_ge(sems[f"sDD{jp}"],
                                      256 * ((j - 2) // 2 + 1))
                        fence(v, v.reduce_sum(
                            Ssum[:, 0:16],
                            E1f[:].rearrange("p (g j) -> p g j", g=16),
                            axis=AX.X))
                        fence(v, v.reduce_sum(
                            Ssum[:, 32:48],
                            E2f[:].rearrange("p (g j) -> p g j", g=16),
                            axis=AX.X))
                        fence(v, v.reciprocal(Ssum[:, 16:32], Ssum[:, 0:16]))
                        fence(v, v.reciprocal(Ssum[:, 48:64], Ssum[:, 32:48]))
                        v.tensor_copy(
                            AP(Rb, 0, [[1024, 128], [32, 16], [1, 32]]),
                            AP(Ssum, 16, [[64, 128], [1, 16], [0, 32]]))
                        v.tensor_copy(
                            AP(Rb, 512, [[1024, 128], [32, 16], [1, 32]]),
                            AP(Ssum, 48, [[64, 128], [1, 16], [0, 32]]))
                        inc(v.tensor_tensor(E1f[:], E1f[:], Rb[:, 0:512],
                                            op=ALU.mult), "sSMX")
                        inc(v.tensor_tensor(E2f[:], E2f[:], Rb[:, 512:1024],
                                            op=ALU.mult), "sSMX")
                        inc(v.scalar_tensor_tensor(
                            AP(D_sb[jp], 0, [[256, 128], [16, 16], [1, 16]]),
                            AP(E2f, 16, [[512, 128], [32, 16], [1, 16]]),
                            neglam_b[:],
                            AP(E1f, 0, [[512, 128], [32, 16], [1, 16]]),
                            op0=ALU.mult, op1=ALU.add), "sD")

                    @blk.gpsimd
                    def _(g, j=j, jp=jp, j4=j4):
                        g.wait_ge(sems["sD"], j + 1)
                        if j >= 4:
                            g.wait_ge(sems[f"sLT2{j4}"],
                                      16 * ((j - 4) // 4 + 1))
                        if j < 4:
                            g.wait_ge(sems["sMISC"], ltd_init_done)
                        for H in range(2):
                            for s in range(4):
                                for G in range(2):
                                    q2 = 8 * H + 2 * s + G
                                    src = AP(D_sb[jp],
                                             (64 * H + 16 * s) * 256 + 128 * G,
                                             [[256, 16], [16, 8], [1, 16]])
                                    dst = AP(ltd[j4].tensor, 128 * 128 * q2,
                                             [[128, 16], [2064, 8], [1, 16]])
                                    inc(g.dma_start(dst, src), f"sDD{jp}", 16)

                def em_load(t):
                    t4 = t % 4
                    tp = t % 2

                    @blk.scalar
                    def _(sc, t=t, t4=t4, tp=tp):
                        sc.wait_ge(sems[f"sDD{tp}"], 256 * (t // 2 + 1))
                        if t < 4:
                            sc.wait_ge(sems["sLAM"], lhsT1_zeroed)
                        if t >= 4:
                            sc.wait_ge(sems["sM2"], 4 * (t - 3))
                        inc(sc.dma_start(lhsT2[t4][:], ltd[t4][:, :],
                                         transpose=True), f"sLT2{t4}", 16)

                def em_back(jj):
                    j4 = jj % 4
                    tjj = 128 * jj
                    for grp in range(4):
                        @blk.tensor
                        def _(t, jj=jj, j4=j4, grp=grp):
                            t.wait_ge(sems[f"sLT2{j4}"], 16 * (jj // 4 + 1))
                            t.wait_ge(sems[f"sVSTL{j4}"], hget("vst", jj))
                            if jj >= 1:
                                t.wait_ge(sems["sO2"],
                                          16 * (jj - 1) + 4 * (grp + 1))
                            bank = banks4[grp]
                            mm = None
                            for qq in range(4):
                                q2 = grp * 4 + qq
                                mm = t.matmul(
                                    bank[:, 128 * qq:128 * qq + 128],
                                    lhsT2[j4][:, 128 * q2:128 * q2 + 128],
                                    vst[j4][:, 128 * q2:128 * q2 + 128],
                                    start=True, stop=True)
                            inc(mm, "sM2")

                        @blk.scalar
                        def _(sc, jj=jj, grp=grp):
                            sc.wait_ge(sems["sM2"], jj * 4 + grp + 1)
                            W(sc, "sRMS")
                            inc(sc.activation(scrB[:], banks4[grp][:],
                                              AF.Square), "sRMS")

                        @blk.vector
                        def _(v, grp=grp):
                            W(v, "sRMS")
                            inc(v.reduce_sum(rmsd[:, 4 * grp:4 * grp + 4],
                                             scrB[:].rearrange(
                                                 "p (a b) -> p a b", a=4),
                                             axis=AX.X), "sRMS")

                    @blk.scalar
                    def _(sc):
                        W(sc, "sRMS")
                        fence(sc, sc.activation(rmsd[:, 16:32],
                                                rmsd[:, 0:16], AF.Ln,
                                                bias=EPS, scale=1.0 / 128))
                        inc(sc.activation(rmsd[:, 16:32],
                                          rmsd[:, 16:32], AF.Exp,
                                          bias=lnoml_b[:, 0:1], scale=-0.5),
                            "sRMS")

                    @blk.vector
                    def _(v, jj=jj):
                        W(v, "sRMS")
                        if jj >= 1:
                            v.wait_ge(sems["sO2D"], 256 * jj)
                        for q2 in range(16):
                            inc(v.tensor_scalar(
                                o2_sb[:, 128 * q2:128 * q2 + 128],
                                banks4[q2 // 4][:, 128 * (q2 % 4):
                                                128 * (q2 % 4) + 128],
                                rmsd[:, 16 + q2:17 + q2], None,
                                op0=ALU.mult), "sO2")

                    @blk.gpsimd
                    def _(g, jj=jj, tjj=tjj):
                        g.wait_ge(sems["sO2"], 16 * (jj + 1))
                        inc(g.dma_start(o2p_dram[128 * jj:128 * jj + 128, :],
                                        o2_sb[:]), "sO2P", 16)
                        g.wait_ge(sems["sO2P"], C["sO2P"])
                        for H in range(2):
                            for s in range(4):
                                for G in range(2):
                                    q2 = 8 * H + 2 * s + G
                                    src = AP(o2p_dram.tensor,
                                             128 * jj * DIM + 128 * q2,
                                             [[16 * DIM, 8], [DIM, 16], [1, 128]])
                                    dst = AP(o2_dram.tensor,
                                             (tjj + 64 * H + 32 * G + s) * DIM,
                                             [[4 * DIM, 8], [128, 16], [1, 128]])
                                    inc(g.dma_start(dst, src), "sO2D", 16)

                def em_wot_load(a0, a1, first):
                    @blk.gpsimd
                    def _(g, a0=a0, a1=a1, first=first):
                        g.wait_ge(sems["sLAM"], C["sLAM"])
                        if first:
                            inc(g.dma_start(bo_b[:], bo_d[:, :]), "sWOT", 16)
                        for a in range(a0, a1):
                            src = AP(woT_d.tensor, 128 * a * DIM,
                                     [[DIM, 128], [1, DIM]])
                            dst = AP(woT_sb, DIM * a,
                                     [[16 * DIM, 128], [1, DIM]])
                            inc(g.dma_start(dst, src), "sWOT", 16)

                for it in range(nt + 2):
                    if it == 0:
                        em_repack(0)
                        if nt > 1:
                            em_repack(1)
                    elif it + 1 <= nt - 1:
                        em_repack(it + 1)
                    if it < nt:
                        em_front(it)
                    if 1 <= it <= nt:
                        em_load(it - 1)
                    if it >= 2:
                        em_back(it - 2)
                    if nt >= 12:
                        if 2 <= it <= 9:
                            @blk.vector
                            def _(v, it=it):
                                if it == 2:
                                    v.wait_ge(sems["sQMM"], 12 * nt)
                                    inc(v.memset(bo_b[:], 0.0), "sLAM")
                                c0 = 4096 * (it - 2)
                                inc(v.memset(woT_sb[:, c0:c0 + 4096], 0.0),
                                    "sLAM")
                        if 3 <= it <= 10:
                            em_wot_load(2 * (it - 3), 2 * (it - 3) + 2,
                                        it == 3)
                    elif it == 2:
                        @blk.vector
                        def _(v):
                            v.wait_ge(sems["sQMM"], 12 * nt)
                            inc(v.memset(bo_b[:], 0.0), "sLAM")
                            inc(v.memset(woT_sb[:], 0.0), "sLAM")
                        em_wot_load(0, 16, True)

            A4_GATE = dict(sO2D=256 * nt, sM2=4 * nt, sATT=nt, sD=nt,
                           sO2=16 * nt, sEXP=2 * nt, sRMS=C["sRMS"],
                           sDD0=C["sDD0"], sDD1=C["sDD1"],
                           sLT20=C["sLT20"], sLT21=C["sLT21"],
                           sLT22=C["sLT22"], sLT23=C["sLT23"],
                           sVSTL0=C["sVSTL0"], sVSTL1=C["sVSTL1"],
                           sVSTL2=C["sVSTL2"], sVSTL3=C["sVSTL3"],
                           sRP0=C["sRP0"], sRP1=C["sRP1"],
                           sRPK0=C["sRPK0"], sRPK1=C["sRPK1"])

            def a4gate(eng):
                for k, v in A4_GATE.items():
                    eng.wait_ge(sems[k], v)

            pa5 = contextlib.ExitStack()
            with pa5:
                o2_fm = pa5.enter_context(
                    nc.sbuf_tensor("o2_fm", [128, 16 * T], BF16))
                x_sb5 = [pa5.enter_context(
                    nc.sbuf_tensor(f"x5{i}", [128, 2048], F32)) for i in range(2)]
                h2_sb = [pa5.enter_context(
                    nc.sbuf_tensor(f"h2{i}", [128, 2048], BF16)) for i in range(2)]
                scr5 = pa5.enter_context(nc.sbuf_tensor("scr5", [128, 2048], F32))

                @blk.vector
                def _(v):
                    a4gate(v)
                    for i in range(2):
                        inc(v.memset(x_sb5[i][:], 0.0), "sLAM")
                        inc(v.memset(h2_sb[i][:], 0.0), "sLAM")
                pa5_init = C["sLAM"]

                @blk.sync
                def _(sy):
                    a4gate(sy)
                    sy.wait_ge(sems["sO2D"], 256 * nt)
                    sy.wait_ge(sems["sLAM"], pa5_init)
                    for a in range(16):
                        inc(sy.dma_start(o2_fm[:, T * a:T * a + T],
                                         o2_dram[:, 128 * a:128 * a + 128],
                                         transpose=True), "sOFM", 16)

                base_qmm = C["sQMM"]
                for j in range(nt):
                    b = j % 2
                    so = 8 * b

                    @blk.gpsimd
                    def _(g, j=j, b=b):
                        a4gate(g)
                        if j < 2:
                            g.wait_ge(sems["sLAM"], pa5_init)
                        if j >= 2:
                            g.wait_ge(sems[f"sX1D{b}"], 16 * ((j - 2) // 2 + 1))
                        inc(g.dma_start(x_sb5[b][:], x_d[128 * j:128 * j + 128, :]),
                            f"sXL{b}", 16)
                    snap("x5", j, f"sXL{b}")

                    for ch in range(4):
                        cidx = j * 4 + ch

                        @blk.tensor
                        def _(t, j=j, ch=ch, cidx=cidx):
                            if cidx < 4:
                                t.wait_ge(sems["sEV"], 12 * nt)
                            t.wait_ge(sems["sWOT"], 272)
                            t.wait_ge(sems["sOFM"], 256)
                            if cidx >= 4:
                                t.wait_ge(sems["sX1"], cidx - 3)
                            pb = psb[cidx % 4]
                            for a in range(16):
                                t.matmul(
                                    pb[:],
                                    o2_fm[:, T * a + 128 * j:T * a + 128 * j + 128],
                                    woT_sb[:, DIM * a + 512 * ch:
                                           DIM * a + 512 * ch + 512],
                                    start=(a == 0), stop=False)
                            mm = t.matmul(pb[:], ones_b[:],
                                          bo_b[:, 512 * ch:512 * ch + 512],
                                          start=False, stop=True)
                            inc(mm, "sQMM")

                        @blk.vector
                        def _(v, j=j, ch=ch, b=b, cidx=cidx):
                            v.wait_ge(sems["sQMM"], base_qmm + cidx + 1)
                            v.wait_ge(sems[f"sXL{b}"], hget("x5", j))
                            inc(v.tensor_tensor(
                                x_sb5[b][:, 512 * ch:512 * ch + 512],
                                x_sb5[b][:, 512 * ch:512 * ch + 512],
                                psb[cidx % 4][:], op=ALU.add), "sX1")

                    @blk.vector
                    def _(v, j=j, b=b, so=so):
                        v.wait_ge(sems["sX1"], 4 * (j + 1))
                        if j >= 2:
                            v.wait_ge(sems["sH2"], j - 1)
                        fence(v, v.reduce_sum(stat[:, so:so + 1], x_sb5[b][:],
                                              axis=AX.X))
                        fence(v, v.tensor_scalar(stat[:, so:so + 1],
                                                 stat[:, so:so + 1], 1.0 / DIM,
                                                 None, op0=ALU.mult))
                        inc(v.scalar_tensor_tensor(
                            scr5[:], x_sb5[b][:], stat[:, so:so + 1], x_sb5[b][:],
                            op0=ALU.subtract, op1=ALU.mult), "sLN")
                        inc(v.reduce_sum(stat[:, so + 1:so + 2], scr5[:],
                                         axis=AX.X), "sLN")
                    snap("ln2_stats", j, "sLN")

                    @blk.scalar
                    def _(sc, j=j, so=so):
                        sc.wait_ge(sems["sLN"], hget("ln2_stats", j))
                        fence(sc, sc.activation(stat[:, so + 2:so + 3],
                                                stat[:, so + 1:so + 2], AF.Ln,
                                                bias=EPS, scale=1.0 / DIM))
                        inc(sc.activation(stat[:, so + 2:so + 3],
                                          stat[:, so + 2:so + 3], AF.Exp,
                                          scale=-0.5), "sLN")
                    snap("ln2_rstd", j, "sLN")

                    @blk.vector
                    def _(v, j=j, b=b, so=so):
                        v.wait_ge(sems["sLN"], hget("ln2_rstd", j))
                        if j >= 2:
                            v.wait_ge(sems[f"sH2D{b}"],
                                      16 * ((j - 2) // 2 + 1))
                        inc(v.tensor_scalar(h2_sb[b][:], x_sb5[b][:],
                                            stat[:, so:so + 1],
                                            stat[:, so + 2:so + 3],
                                            op0=ALU.subtract, op1=ALU.mult),
                            "sH2")

                    @blk.sync
                    def _(sy, j=j, b=b):
                        sy.wait_ge(sems["sH2"], j + 1)
                        inc(sy.dma_start(h2_dram[128 * j:128 * j + 128, :],
                                         h2_sb[b][:]), f"sH2D{b}", 16)
                        inc(sy.dma_start(x1_dram[128 * j:128 * j + 128, :],
                                         x_sb5[b][:]), f"sX1D{b}", 16)

            paWo.close()
            A5_GATE = dict(sQMM=C["sQMM"], sH2D0=C["sH2D0"], sH2D1=C["sH2D1"],
                           sX1D0=C["sX1D0"], sX1D1=C["sX1D1"])

            def a5gate(eng):
                for k, v in A5_GATE.items():
                    eng.wait_ge(sems[k], v)

            pbO = contextlib.ExitStack()
            with pbO:
                h2_fm = pbO.enter_context(
                    nc.sbuf_tensor("h2_fm", [128, 16 * T], BF16))
                gw_sb = pbO.enter_context(nc.sbuf_tensor("gw_sb", [128, 144], BF16))
                comb9 = pbO.enter_context(
                    nc.sbuf_tensor("comb9", [128, 16 * nt], F32))
                gstat = pbO.enter_context(nc.sbuf_tensor("gstat", [128, 32], F32))

                @blk.vector
                def _(v):
                    a5gate(v)
                    inc(v.memset(h2_fm[:], 0.0), "sLAM")
                    inc(v.memset(gw_sb[:], 0.0), "sLAM")
                pb0_init = C["sLAM"]

                @blk.sync
                def _(sy):
                    a5gate(sy)
                    sy.wait_ge(sems["sLAM"], pb0_init)
                    for kk in range(16):
                        inc(sy.dma_start(h2_fm[:, T * kk:T * kk + T],
                                         h2_dram[:, 128 * kk:128 * kk + 128],
                                         transpose=True), "sH2FM", 16)

                @blk.gpsimd
                def _(g):
                    a5gate(g)
                    g.wait_ge(sems["sLAM"], pb0_init)
                    src = AP(gwT_d.tensor, 0, [[9, 128], [128 * 9, 16], [1, 9]])
                    dst = AP(gw_sb, 0, [[144, 128], [9, 16], [1, 9]])
                    inc(g.dma_start(dst, src), "sGW", 16)

                for j in range(nt):
                    go = 16 * (j % 2)
                    sbo = 16 * (j % 2)

                    @blk.tensor
                    def _(t, j=j):
                        t.wait_ge(sems["sH2FM"], 256)
                        t.wait_ge(sems["sGW"], 16)
                        if j >= 2:
                            t.wait_ge(sems["sGTV"], hget("gt2", j - 2))
                        bank = psM[j % 2]
                        mm = None
                        for kk in range(16):
                            mm = t.matmul(bank[:, 0:9],
                                          h2_fm[:, T * kk + 128 * j:
                                                T * kk + 128 * j + 128],
                                          gw_sb[:, 9 * kk:9 * kk + 9],
                                          start=(kk == 0), stop=(kk == 15))
                        inc(mm, "sSCM")

                    @blk.vector
                    def _(v, j=j, go=go, sbo=sbo):
                        v.wait_ge(sems["sSCM"], j + 1)
                        if j >= 2:
                            v.wait_ge(sems["sCMB"], hget("cmb", j - 2))
                        sc_ = psM[j % 2][:, 0:8]
                        fence(v, v.reduce_max(gstat[:, go:go + 1], sc_,
                                              axis=AX.X))
                        fence(v, v.tensor_scalar(gstat[:, go + 2:go + 10], sc_,
                                                 gstat[:, go:go + 1], None,
                                                 op0=ALU.is_ge))
                        fence(v, v.scalar_tensor_tensor(
                            scrB[:, sbo:sbo + 8], gstat[:, go + 2:go + 10],
                            -1e30, sc_, op0=ALU.mult, op1=ALU.add))
                        fence(v, v.reduce_max(gstat[:, go + 1:go + 2],
                                              scrB[:, sbo:sbo + 8], axis=AX.X))
                        fence(v, v.tensor_tensor(gstat[:, go + 10:go + 11],
                                                 gstat[:, go:go + 1],
                                                 gstat[:, go + 1:go + 2],
                                                 op=ALU.subtract))
                        fence(v, v.tensor_scalar(scrB[:, sbo + 8:sbo + 16], sc_,
                                                 gstat[:, go + 1:go + 2], None,
                                                 op0=ALU.is_ge))
                        inc(v.tensor_tensor(scrB[:, sbo + 8:sbo + 16],
                                            scrB[:, sbo + 8:sbo + 16],
                                            gstat[:, go + 2:go + 10],
                                            op=ALU.subtract), "sGTV")
                    snap("gt1", j, "sGTV")

                    @blk.scalar
                    def _(sc, j=j, go=go):
                        sc.wait_ge(sems["sGTV"], hget("gt1", j))
                        sc.wait_ge(sems["sFEN"], C["sFEN"])
                        inc(sc.activation(gstat[:, go + 12:go + 13],
                                          gstat[:, go + 10:go + 11], AF.Sigmoid),
                            "sGTV")
                        inc(sc.activation(gstat[:, go + 13:go + 14],
                                          gstat[:, go + 10:go + 11], AF.Sigmoid,
                                          scale=-1.0), "sGTV")
                        inc(sc.activation(comb9[:, 16 * j + 8:16 * j + 9],
                                          psM[j % 2][:, 8:9], AF.Sigmoid), "sGTV")
                    snap("gt2", j, "sGTV")

                    @blk.vector
                    def _(v, j=j, go=go, sbo=sbo):
                        v.wait_ge(sems["sGTV"], hget("gt2", j))
                        fence(v, v.tensor_scalar(gstat[:, go + 2:go + 10],
                                                 gstat[:, go + 2:go + 10],
                                                 gstat[:, go + 12:go + 13], None,
                                                 op0=ALU.mult))
                        inc(v.scalar_tensor_tensor(
                            comb9[:, 16 * j:16 * j + 8],
                            scrB[:, sbo + 8:sbo + 16],
                            gstat[:, go + 13:go + 14],
                            gstat[:, go + 2:go + 10],
                            op0=ALU.mult, op1=ALU.add), "sCMB")
                    snap("cmb", j, "sCMB")

                pbW = contextlib.ExitStack()
                with pbW:
                    hid = [pbW.enter_context(
                        nc.sbuf_tensor(f"hid{i}", [128, 8 * T], BF16))
                        for i in range(2)]
                    wup = [[pbW.enter_context(
                        nc.sbuf_tensor(f"wup{gu}{i}", [128, 2048], BF16))
                        for i in range(2)] for gu in range(2)]
                    wdn = [pbW.enter_context(
                        nc.sbuf_tensor(f"wdn{i}", [128, 4096], BF16))
                        for i in range(4)]
                    comb_bc = pbW.enter_context(
                        nc.sbuf_tensor("comb_bc", [128, T], BF16))
                    sg_sb = pbW.enter_context(
                        nc.sbuf_tensor("sg_sb", [128, 512], BF16))
                    us_sb = pbW.enter_context(
                        nc.sbuf_tensor("us_sb", [128, 512], BF16))
                    stg_dn = [pbW.enter_context(
                        nc.sbuf_tensor(f"stgdn{i}", [128, 512], F32))
                        for i in range(2)]
                    comb_row = pbW.enter_context(
                        nc.sbuf_tensor("comb_row", [1, T], BF16))
                    @blk.vector
                    def _(v):
                        a5gate(v)
                        for gu in range(2):
                            for i in range(2):
                                inc(v.memset(wup[gu][i][:], 0.0), "sLAM")
                        for i in range(4):
                            inc(v.memset(wdn[i][:], 0.0), "sLAM")
                        inc(v.memset(comb_row[:], 0.0), "sLAM")
                    pbw_init = C["sLAM"]

                    up_emitted = set()

                    def em_upload(E, ch, gu):
                        up_emitted.add((E, ch, gu))
                        wsrc = upg_d if gu == 0 else upu_d

                        @blk.gpsimd
                        def _(g, E=E, ch=ch, gu=gu, wsrc=wsrc):
                            li = (E * UPC + ch) * 2 + gu
                            if li < 4:
                                a5gate(g)
                                g.wait_ge(sems["sLAM"], pbw_init)
                            if li >= 4:
                                g.wait_ge(sems["sUMM"],
                                          hget("upw_free", li - 4))
                            src = AP(wsrc.tensor,
                                     E * DIM * 1024 + 128 * ch,
                                     [[1024, 128], [128 * 1024, 16],
                                      [1, 128]])
                            dst = AP(wup[gu][ch % 2], 0,
                                     [[2048, 128], [128, 16], [1, 128]])
                            inc(g.dma_start(dst, src),
                                f"sUW{gu * 2 + ch % 2}", 16)
                        snap("uw", (E * UPC + ch) * 2 + gu,
                             f"sUW{gu * 2 + ch % 2}")

                    NE = 10
                    UPC = 8
                    base_x1 = C["sX1"]

                    up_chain = 0
                    dn_chain = 0
                    for E in range(NE):
                        eb = E % 2
                        rE = E if E < 8 else 8

                        for ci, (t0, tl) in enumerate(TCH):
                            ntile = tl // 128

                            @blk.tensor
                            def _(t, E=E, rE=rE, ci=ci, t0=t0, ntile=ntile):
                                t.wait_ge(sems["sCMB"], C["sCMB"])
                                t.wait_ge(sems["sGTV"], C["sGTV"])
                                if ci >= 2 or E >= 1:
                                    t.wait_ge(sems["sCBE"],
                                              max(0, C["sCBE"] - 1))
                                mm = None
                                for jj in range(ntile):
                                    jt = (t0 // 128) + jj
                                    mm = t.matmul(
                                        psM[ci % 2][0:1, 128 * jj:128 * jj + 128],
                                        comb9[:, 16 * jt + rE:16 * jt + rE + 1],
                                        ident[:], start=True, stop=True)
                                inc(mm, "sCB")

                            @blk.vector
                            def _(v, E=E, ci=ci, t0=t0, tl=tl):
                                W(v, "sCB")
                                if E >= 1:
                                    v.wait_ge(sems["sCB"], hget("bcast", E - 1))
                                inc(v.tensor_copy(comb_row[0:1, t0:t0 + tl],
                                                  psM[ci % 2][0:1, 0:tl]), "sCR")

                            @blk.tensor
                            def _(t, E=E, ci=ci, t0=t0, tl=tl):
                                t.wait_ge(sems["sCR"], C["sCR"])
                                inc(t.matmul(psM[ci % 2][:, 0:tl], ones_b[:],
                                             comb_row[0:1, t0:t0 + tl],
                                             start=True, stop=True), "sCB")

                            @blk.vector
                            def _(v, E=E, ci=ci, t0=t0, tl=tl):
                                W(v, "sCB")
                                inc(v.tensor_copy(comb_bc[:, t0:t0 + tl],
                                                  psM[ci % 2][:, 0:tl]), "sCBE")
                        snap("bcast", E, "sCB")

                        for ch in range(UPC):
                            for gu in range(2):
                                if (E, ch, gu) in up_emitted:
                                    continue
                                em_upload(E, ch, gu)

                            for ci, (t0, tl) in enumerate(TCH):
                                @blk.tensor
                                def _(t, E=E, ch=ch, ci=ci, t0=t0, tl=tl, eb=eb,
                                      up_chain=up_chain):
                                    li = (E * UPC + ch) * 2
                                    t.wait_ge(sems[f"sUW{0 * 2 + ch % 2}"],
                                              hget("uw", li))
                                    t.wait_ge(sems[f"sUW{1 * 2 + ch % 2}"],
                                              hget("uw", li + 1))
                                    t.wait_ge(sems["sCBE"], C["sCBE"])
                                    if E >= 2 and ch == 0 and ci == 0:
                                        t.wait_ge(sems["sDMM"],
                                                  hget("dn_done", E - 2))
                                    if up_chain == 0 and ci == 0:
                                        t.wait_ge(sems["sX1"], base_x1)
                                    if E >= 1 and E % 2 == 0 and ch == 0 and ci == 0:
                                        t.wait_ge(sems["sDNE"], C["sDNE"])
                                    for gu in range(2):
                                        cno = up_chain + (ci * 2 + gu)
                                        if cno >= 4:
                                            t.wait_ge(sems["sUPE"],
                                                      hget("up_cons", cno - 4))
                                        pb = psb[cno % 4]
                                        mm = None
                                        for kk in range(16):
                                            mm = t.matmul(
                                                pb[:, 0:tl],
                                                wup[gu][ch % 2][:,
                                                                128 * kk:128 * kk + 128],
                                                h2_fm[:, T * kk + t0:
                                                      T * kk + t0 + tl],
                                                start=(kk == 0), stop=(kk == 15))
                                        inc(mm, "sUMM")

                                @blk.scalar
                                def _(sc, E=E, ch=ch, ci=ci, tl=tl,
                                      up_chain=up_chain):
                                    sc.wait_ge(sems["sUMM"], up_chain + ci * 2 + 1)
                                    W(sc, "sUPE")
                                    gbank = psb[(up_chain + ci * 2) % 4]
                                    inc(sc.activation(sg_sb[:, 0:tl],
                                                      gbank[:, 0:tl], AF.Silu),
                                        "sUPE")
                                    snap("up_cons",
                                         up_chain + ci * 2, "sUPE")

                                @blk.vector
                                def _(v, E=E, ch=ch, ci=ci, t0=t0, tl=tl, eb=eb,
                                      up_chain=up_chain):
                                    v.wait_ge(sems["sUMM"], up_chain + ci * 2 + 2)
                                    W(v, "sUPE")
                                    ubank = psb[(up_chain + ci * 2 + 1) % 4]
                                    inc(v.tensor_tensor(us_sb[:, 0:tl],
                                                        ubank[:, 0:tl],
                                                        comb_bc[:, t0:t0 + tl],
                                                        op=ALU.mult), "sUPE")
                                    snap("up_cons",
                                         up_chain + ci * 2 + 1, "sUPE")
                                    inc(v.tensor_tensor(
                                        AP(hid[eb], T * ch + t0,
                                           [[8 * T, 128], [1, tl]]),
                                        sg_sb[:, 0:tl], us_sb[:, 0:tl],
                                        op=ALU.mult), "sHID")

                            up_chain += 2 * len(TCH)
                            snap("upw_free", (E * UPC + ch) * 2, "sUMM")
                            snap("upw_free", (E * UPC + ch) * 2 + 1, "sUMM")

                        if E % 2 == 1:
                            pair = E // 2

                            def em_dload(pair, cc, ei):
                                @blk.gpsimd
                                def _(g, pair=pair, cc=cc, ei=ei):
                                    li = (pair * 4 + cc) * 2 + ei
                                    if li < 4:
                                        g.wait_ge(sems["sLAM"], pbw_init)
                                    if li >= 4:
                                        g.wait_ge(sems["sDMM"],
                                                  hget("dnw_free", li - 4))
                                    Ex = 2 * pair + ei
                                    src = AP(dnw_d.tensor,
                                             Ex * 1024 * DIM + 512 * cc,
                                             [[DIM, 128], [128 * DIM, 8],
                                              [1, 512]])
                                    dst = AP(wdn[(cc % 2) * 2 + ei], 0,
                                             [[4096, 128], [512, 8], [1, 512]])
                                    inc(g.dma_start(dst, src),
                                        f"sDW{(cc % 2) * 2 + ei}", 16)
                                    snap("dw", (pair * 4 + cc) * 2 + ei,
                                         f"sDW{(cc % 2) * 2 + ei}")

                            for cc in range(4):
                                if cc == 0:
                                    for ei in range(2):
                                        em_dload(pair, 0, ei)
                                    for ei in range(2):
                                        em_dload(pair, 1, ei)
                                elif cc + 1 <= 3:
                                    for ei in range(2):
                                        em_dload(pair, cc + 1, ei)

                                if cc == 0 and E + 1 < NE:
                                    for ch_ in range(2):
                                        for gu_ in range(2):
                                            em_upload(E + 1, ch_, gu_)

                                for j in range(nt):
                                    @blk.tensor
                                    def _(t, pair=pair, cc=cc, j=j,
                                          dn_chain=dn_chain, upc=up_chain):
                                        li = (pair * 4 + cc) * 2
                                        t.wait_ge(sems[f"sDW{(cc % 2) * 2}"],
                                                  hget("dw", li))
                                        t.wait_ge(sems[f"sDW{(cc % 2) * 2 + 1}"],
                                                  hget("dw", li + 1))
                                        t.wait_ge(sems["sHID"], C["sHID"])
                                        cno = dn_chain + j
                                        if cno < dn_chain + 4 and cc == 0 and j < 4:
                                            t.wait_ge(sems["sUPE"], C["sUPE"])
                                        if cno >= 4:
                                            t.wait_ge(sems["sDNE"], cno - 3)
                                        pb = psb[cno % 4]
                                        mm = None
                                        for ei in range(2):
                                            for ht in range(8):
                                                mm = t.matmul(
                                                    pb[:],
                                                    AP(hid[ei], T * ht + 128 * j,
                                                       [[8 * T, 128], [1, 128]]),
                                                    wdn[(cc % 2) * 2 + ei][
                                                        :, 512 * ht:512 * ht + 512],
                                                    start=(ei == 0 and ht == 0),
                                                    stop=(ei == 1 and ht == 7))
                                        inc(mm, "sDMM")

                                    @blk.scalar
                                    def _(sc, cc=cc, j=j, dn_chain=dn_chain):
                                        cno = dn_chain + j
                                        sc.wait_ge(sems["sDMM"], cno + 1)
                                        if cno >= 2:
                                            sc.wait_ge(sems[f"sACC{cno % 2}"],
                                                       16 * ((cno - 2) // 2 + 1))
                                        inc(sc.copy(stg_dn[cno % 2][:],
                                                    psb[cno % 4][:]), "sDNE")

                                    @blk.gpsimd
                                    def _(g, pair=pair, cc=cc, j=j,
                                          dn_chain=dn_chain):
                                        cno = dn_chain + j
                                        g.wait_ge(sems["sDNE"], cno + 1)
                                        op = ALU.bypass if pair == 0 else ALU.add
                                        inc(g.dma_start(
                                            acc_dram[128 * j:128 * j + 128,
                                                     512 * cc:512 * cc + 512],
                                            stg_dn[cno % 2][:], accum_op=op),
                                            f"sACC{cno % 2}", 16)

                                dn_chain += nt
                                snap("dnw_free", (pair * 4 + cc) * 2, "sDMM")
                                snap("dnw_free", (pair * 4 + cc) * 2 + 1, "sDMM")
                            snap("dn_done", E, "sDMM")

                pb4 = contextlib.ExitStack()
                with pb4:
                    x1b = [pb4.enter_context(
                        nc.sbuf_tensor(f"x1b{i}", [128, 2048], F32))
                        for i in range(2)]
                    accb = [pb4.enter_context(
                        nc.sbuf_tensor(f"accb{i}", [128, 2048], F32))
                        for i in range(2)]
                    ACC_TOTAL = (C["sACC0"], C["sACC1"])

                    @blk.vector
                    def _(v):
                        v.wait_ge(sems["sACC0"], ACC_TOTAL[0])
                        v.wait_ge(sems["sACC1"], ACC_TOTAL[1])
                        for i in range(2):
                            inc(v.memset(x1b[i][:], 0.0), "sLAM")
                            inc(v.memset(accb[i][:], 0.0), "sLAM")
                    pb4_init = C["sLAM"]

                    for j in range(nt):
                        b = j % 2

                        @blk.gpsimd
                        def _(g, j=j, b=b):
                            g.wait_ge(sems["sACC0"], ACC_TOTAL[0])
                            g.wait_ge(sems["sACC1"], ACC_TOTAL[1])
                            g.wait_ge(sems["sLAM"], pb4_init)
                            if j >= 2:
                                g.wait_ge(sems[f"sY{b}"],
                                          16 * ((j - 2) // 2 + 1))
                            inc(g.dma_start(x1b[b][:],
                                            x1_dram[128 * j:128 * j + 128, :]),
                                f"sFB{b}", 16)
                            inc(g.dma_start(accb[b][:],
                                            acc_dram[128 * j:128 * j + 128, :]),
                                f"sFB{b}", 16)

                        @blk.vector
                        def _(v, j=j, b=b):
                            v.wait_ge(sems[f"sFB{b}"], 32 * (j // 2 + 1))
                            inc(v.tensor_tensor(x1b[b][:], x1b[b][:], accb[b][:],
                                                op=ALU.add), "sFIN")

                        @blk.sync
                        def _(sy, j=j, b=b):
                            sy.wait_ge(sems["sFIN"], j + 1)
                            inc(sy.dma_start(y_d[128 * j:128 * j + 128, :],
                                             x1b[b][:]), f"sY{b}", 16)

                    if taps:
                        @blk.sync
                        def _(sy):
                            sy.wait_ge(sems["sY0"], C["sY0"])
                            sy.wait_ge(sems["sY1"], C["sY1"])
                            for name, out in tap_outs.items():
                                src = scratch[name]
                                inc(sy.dma_start(out, src[:, :]), "sY0", 16)

                    @blk.gpsimd
                    def _(g):
                        g.wait_ge(sems["sY0"], C["sY0"])
                        g.wait_ge(sems["sY1"], C["sY1"])

    return nc


def _prep(inputs):
    f32 = np.float32
    g = lambda k: np.asarray(inputs[k], dtype=f32)
    cp = np.arange(DIM)
    i_ = cp % 32
    d_ = cp // 32
    pidx = (i_ % 16) * 128 + 2 * d_ + (i_ // 16)

    wq = g("L0_attn_Wq")[pidx, :]
    wk = g("L0_attn_Wk")[pidx, :]
    shg = g("L0_ffn_sh_gate")
    shu = g("L0_ffn_sh_up")
    shd = g("L0_ffn_sh_down")
    wg = g("L0_ffn_w_gate")
    wu = g("L0_ffn_w_up")
    wd = g("L0_ffn_w_down")

    upg = np.concatenate(
        [np.transpose(wg, (0, 2, 1)),
         np.transpose(shg, (1, 0)).reshape(DIM, 2, 1024).transpose(1, 0, 2)],
        axis=0)
    upu = np.concatenate(
        [np.transpose(wu, (0, 2, 1)),
         np.transpose(shu, (1, 0)).reshape(DIM, 2, 1024).transpose(1, 0, 2)],
        axis=0)
    dnw = np.concatenate(
        [np.transpose(wd, (0, 2, 1)),
         np.transpose(shd, (1, 0)).reshape(2, 1024, DIM)],
        axis=0)

    wm = {
        "wqT": np.ascontiguousarray(wq.T),
        "wkT": np.ascontiguousarray(wk.T),
        "wvT": np.ascontiguousarray(g("L0_attn_Wv").T),
        "woT": np.ascontiguousarray(g("L0_attn_Wo").T),
        "gwT": np.ascontiguousarray(
            np.concatenate([g("L0_ffn_gate_w"), g("L0_ffn_se_gate_w")], 0).T),
        "upg": np.ascontiguousarray(upg),
        "upu": np.ascontiguousarray(upu),
        "dnw": np.ascontiguousarray(dnw),
        "lama": np.ascontiguousarray(
            np.concatenate([g("L0_attn_lq1"), g("L0_attn_lq2")])[None, :]),
        "lamb": np.ascontiguousarray(
            np.concatenate([g("L0_attn_lk1"), g("L0_attn_lk2")])[None, :]),
        "bo": np.ascontiguousarray(g("L0_attn_bo")[None, :]),
        "ident": np.eye(128, dtype=f32),
    }
    c64 = np.zeros((1, 4), dtype=f32)
    for i, c in enumerate([LAM_INIT, LN_OML]):
        hi = np.float32(c)
        lo = np.float32(c - np.float64(hi))
        c64[0, 2 * i] = hi
        c64[0, 2 * i + 1] = lo
    wm["c64"] = c64
    x = np.ascontiguousarray(g("L0_attn_x"))
    return x, wm


def kernel(**inputs):
    x, wm = _prep(inputs)
    nc = build(nt=16)
    in_maps = []
    for c in range(NCORES):
        m = {"x": np.ascontiguousarray(x[c])}
        m.update(wm)
        in_maps.append(m)
    res = run_bass_kernel_spmd(nc, in_maps, core_ids=list(range(NCORES)))
    out = np.stack([np.asarray(r["y"], dtype=np.float32)
                    for r in res.results], axis=0)
    return out


# version 16
# speedup vs baseline: 1.1036x; 1.0047x over previous
import contextlib
import numpy as np
import ml_dtypes
import concourse.bass as bass
import concourse.mybir as mybir
from concourse.ap import AP
from concourse.bass_utils import run_bass_kernel_spmd

F32 = mybir.dt.float32
BF16 = mybir.dt.bfloat16
AF = mybir.ActivationFunctionType
ALU = mybir.AluOpType
AX = mybir.AxisListType
bfloat16 = ml_dtypes.bfloat16

DIM = 2048
NCORES = 8
LAM_INIT = 0.8 - 0.6 * float(np.exp(-3.6))
OML = 1.0 - LAM_INIT
LN_OML = float(np.log(OML))
EPS = 1e-5


def _tchunks(T):
    return [(c, min(512, T - c)) for c in range(0, T, 512)]


def build(nt=16, taps=()):
    T = nt * 128
    TCH = _tchunks(T)
    nc = bass.Bass("TRN2", target_bir_lowering=False, debug=False,
                   detect_race_conditions=False)

    x_d = nc.dram_tensor("x", [T, DIM], F32, kind="ExternalInput").ap()
    wqT_d = nc.dram_tensor("wqT", [DIM, DIM], F32, kind="ExternalInput").ap()
    wkT_d = nc.dram_tensor("wkT", [DIM, DIM], F32, kind="ExternalInput").ap()
    wvT_d = nc.dram_tensor("wvT", [DIM, DIM], F32, kind="ExternalInput").ap()
    woT_d = nc.dram_tensor("woT", [DIM, DIM], F32, kind="ExternalInput").ap()
    gwT_d = nc.dram_tensor("gwT", [DIM, 9], F32, kind="ExternalInput").ap()
    upg_d = nc.dram_tensor("upg", [10, DIM, 1024], F32, kind="ExternalInput").ap()
    upu_d = nc.dram_tensor("upu", [10, DIM, 1024], F32, kind="ExternalInput").ap()
    dnw_d = nc.dram_tensor("dnw", [10, 1024, DIM], F32, kind="ExternalInput").ap()
    lama_d = nc.dram_tensor("lama", [1, 128], F32, kind="ExternalInput").ap()
    lamb_d = nc.dram_tensor("lamb", [1, 128], F32, kind="ExternalInput").ap()
    bo_d = nc.dram_tensor("bo", [1, DIM], F32, kind="ExternalInput").ap()
    id_d = nc.dram_tensor("ident", [128, 128], F32, kind="ExternalInput").ap()
    c64_d = nc.dram_tensor("c64", [1, 4], F32, kind="ExternalInput").ap()
    y_d = nc.dram_tensor("y", [T, DIM], F32, kind="ExternalOutput").ap()

    h_dram = nc.dram_tensor("h_dram", [T, DIM], BF16, kind="Internal").ap()
    q_dram = nc.dram_tensor("q_dram", [T, DIM], BF16, kind="Internal").ap()
    k_dram = nc.dram_tensor("k_dram", [T, DIM], BF16, kind="Internal").ap()
    v_dram = nc.dram_tensor("v_dram", [T, DIM], BF16, kind="Internal").ap()
    ltd = [nc.dram_tensor(f"ltd{i}", [2048, 128], BF16, kind="Internal").ap()
           for i in range(4)]
    v2_dram = nc.dram_tensor("v2_dram", [T, DIM], BF16, kind="Internal").ap()
    o2p_dram = nc.dram_tensor("o2p_dram", [T, DIM], BF16, kind="Internal").ap()
    o2_dram = nc.dram_tensor("o2_dram", [T, DIM], BF16, kind="Internal").ap()
    x1_dram = nc.dram_tensor("x1_dram", [T, DIM], F32, kind="Internal").ap()
    h2_dram = nc.dram_tensor("h2_dram", [T, DIM], BF16, kind="Internal").ap()
    acc_dram = nc.dram_tensor("acc_dram", [T, DIM], F32, kind="Internal").ap()
    scratch = {"h_dram": h_dram, "q_dram": q_dram, "k_dram": k_dram,
               "v_dram": v_dram, "o2_dram": o2_dram, "x1_dram": x1_dram,
               "h2_dram": h2_dram, "acc_dram": acc_dram,
               "ltd0": ltd[0], "ltd1": ltd[1]}
    tap_outs = {}
    for name in taps:
        src = scratch[name]
        tap_outs[name] = nc.dram_tensor(
            "tap_" + name, list(src.shape), src.dtype, kind="ExternalOutput").ap()

    es = contextlib.ExitStack()
    sems = {}
    C = {}

    def mksem(name):
        sems[name] = es.enter_context(nc.semaphore(name))
        C[name] = 0

    def inc(instr, name, k=1):
        instr.then_inc(sems[name], k)
        C[name] += k

    def W(eng, name, val=None):
        eng.wait_ge(sems[name], C[name] if val is None else val)

    def fence(eng, instr):
        inc(instr, "sFEN")
        eng.wait_ge(sems["sFEN"], C["sFEN"])
        return instr

    for s in ["sXL0", "sXL1", "sLN", "sHD0", "sHD1", "sHFM", "sW0", "sW1",
              "sQMM", "sEV", "sQKVD0", "sQKVD1",
              "sRP0", "sRP1", "sRPK0", "sRPK1",
              "sATT", "sEXP", "sSMX", "sD", "sDD0", "sDD1",
              "sLT20", "sLT21", "sLT22", "sLT23",
              "sVST", "sVSC", "sM2", "sRMS", "sO2", "sO2D", "sO2P",
              "sVSTL0", "sVSTL1", "sVSTL2", "sVSTL3",
              "sWOT", "sOFM", "sX1", "sX1D0", "sX1D1",
              "sH2", "sH2D0", "sH2D1", "sH2FM", "sGW", "sSCM", "sGTV", "sCMB",
              "sCR", "sCB", "sCBE", "sUW0", "sUW1", "sUW2", "sUW3",
              "sUWF0", "sUWF1", "sUWF2", "sUWF3", "sUCP",
              "sUMM", "sUPE", "sHID", "sDW0", "sDW1", "sDW2", "sDW3", "sDMM",
              "sDNE", "sACC0", "sACC1", "sFB0", "sFB1", "sFIN", "sY0", "sY1",
              "sMISC", "sLAM", "sFEN"]:
        mksem(s)
    for g_ in range(4):
        mksem(f"sHFM{g_}")
    for j_ in range(nt):
        mksem(f"sHF{j_}")
    ngrp = 4 if nt % 4 == 0 and nt >= 8 else 2
    gsz = nt // ngrp

    hist = {}

    def snap(key, idx, semname):
        hist.setdefault(key, {})[idx] = C[semname]

    def hget(key, idx, default=0):
        return hist.get(key, {}).get(idx, default)

    with es:
        blk = es.enter_context(nc.Block())

        ones_f = es.enter_context(nc.sbuf_tensor("ones_f", [1, 128], F32))
        ones_b = es.enter_context(nc.sbuf_tensor("ones_b", [1, 128], BF16))
        ident = es.enter_context(nc.sbuf_tensor("ident_sb", [128, 128], F32))
        lam_sb = es.enter_context(nc.sbuf_tensor("lam_sb", [1, 256], F32))
        lamt = es.enter_context(nc.sbuf_tensor("lamt", [1, 256], F32))
        neglam_b = es.enter_context(nc.sbuf_tensor("neglam_b", [128, 1], F32))
        lnoml_b = es.enter_context(nc.sbuf_tensor("lnoml_b", [128, 1], F32))
        cst = es.enter_context(nc.sbuf_tensor("cst", [128, 2], F32))
        nc.const_aps.register(F32, EPS, cst[:, 0:1])
        nc.const_aps.register(F32, LN_OML, cst[:, 1:2])
        scrB = es.enter_context(nc.sbuf_tensor("scrB", [128, 512], F32))
        stat = es.enter_context(nc.sbuf_tensor("stat", [128, 16], F32))
        stg = [es.enter_context(nc.sbuf_tensor(f"stg{i}", [128, 512], BF16))
               for i in range(2)]
        psb = [es.enter_context(nc.psum_tensor(f"psb{i}", [128, 512], F32))
               for i in range(4)]
        psE1 = es.enter_context(nc.psum_tensor("psE1", [128, 512], F32))
        psE2 = es.enter_context(nc.psum_tensor("psE2", [128, 512], F32))
        psM = [es.enter_context(nc.psum_tensor(f"psM{i}", [128, 512], F32))
               for i in range(2)]

        z0 = contextlib.ExitStack()
        with z0:
            zz = z0.enter_context(nc.sbuf_tensor("zz", [128, 512], BF16))

            @blk.gpsimd
            def _(g):
                inc(g.dma_start(lam_sb[:, 0:128], lama_d[:, :]), "sMISC", 16)
                inc(g.dma_start(lam_sb[:, 128:256], lamb_d[:, :]), "sMISC", 16)
                inc(g.dma_start(ident[:], id_d[:, :]), "sMISC", 16)
                inc(g.dma_start(lamt[:, 140:144], c64_d[:, :]), "sMISC", 16)

            @blk.vector
            def _(v):
                inc(v.memset(ones_f[:], 1.0), "sLAM")
                inc(v.memset(ones_b[:], 1.0), "sLAM")
                inc(v.memset(zz[:], 0.0), "sLAM")
                inc(v.memset(cst[:, 0:1], EPS), "sLAM")
                inc(v.memset(cst[:, 1:2], LN_OML), "sLAM")
                W(v, "sMISC")
                fence(v, v.tensor_tensor(lamt[:, 0:128], lam_sb[:, 0:128],
                                         lam_sb[:, 128:256], op=ALU.mult))
                inc(v.reduce_sum(lamt[:, 128:130],
                                 lamt[:, 0:128].rearrange("p (a b) -> p a b", a=2),
                                 axis=AX.X), "sLAM")

            @blk.scalar
            def _(sc):
                W(sc, "sLAM")
                inc(sc.activation(lamt[:, 130:132], lamt[:, 128:130], AF.Exp),
                    "sLAM")

            @blk.vector
            def _(v):
                W(v, "sLAM")
                fence(v, v.tensor_tensor(lamt[:, 132:133], lamt[:, 130:131],
                                         lamt[:, 131:132], op=ALU.subtract))
                fence(v, v.tensor_tensor(lamt[:, 134:135], lamt[:, 140:141],
                                         lamt[:, 141:142], op=ALU.add))
                fence(v, v.tensor_tensor(lamt[:, 135:136], lamt[:, 142:143],
                                         lamt[:, 143:144], op=ALU.add))
                fence(v, v.tensor_tensor(lamt[:, 133:134], lamt[:, 132:133],
                                         lamt[:, 134:135], op=ALU.add))
                inc(v.tensor_scalar(lamt[:, 133:134], lamt[:, 133:134], -1.0,
                                    None, op0=ALU.mult), "sLAM")

            @blk.tensor
            def _(t):
                W(t, "sLAM")
                t.matmul(psM[0][:, 0:1], ones_f[:], lamt[:, 133:134],
                         start=True, stop=True)
                inc(t.matmul(psM[0][:, 4:5], ones_f[:], lamt[:, 135:136],
                             start=True, stop=True), "sLAM")

            @blk.vector
            def _(v):
                W(v, "sLAM")
                fence(v, v.tensor_copy(neglam_b[:], psM[0][:, 0:1]))
                inc(v.tensor_copy(lnoml_b[:], psM[0][:, 4:5]), "sLAM")

            @blk.gpsimd
            def _(g):
                W(g, "sLAM")
                for i in range(4):
                    for qtr in range(4):
                        inc(g.dma_start(
                            AP(ltd[i].tensor, 512 * qtr,
                               [[2048, 128], [1, 512]]), zz[:]),
                            "sMISC", 16)
            ltd_init_done = C["sMISC"]

            pa = contextlib.ExitStack()
            paWo = contextlib.ExitStack()
            with pa:
                paW = contextlib.ExitStack()
                scr = paW.enter_context(nc.sbuf_tensor("scr", [128, 2048], F32))
                h_fm = paW.enter_context(
                    nc.sbuf_tensor("h_fm", [128, 16 * T], BF16))
                wbuf = [paW.enter_context(
                    nc.sbuf_tensor(f"wbuf{i}", [128, 8192], BF16))
                    for i in range(2)]

                a1 = contextlib.ExitStack()
                with a1:
                    x_sb = [a1.enter_context(
                        nc.sbuf_tensor(f"x{i}", [128, 2048], F32)) for i in range(2)]
                    h_sb = [a1.enter_context(
                        nc.sbuf_tensor(f"h{i}", [128, 2048], BF16)) for i in range(2)]

                    for j in range(nt):
                        b = j % 2
                        so = 8 * b

                        @blk.gpsimd
                        def _(g, j=j, b=b):
                            if j >= 2:
                                g.wait_ge(sems["sLN"], hget("hact", j - 2))
                            inc(g.dma_start(x_sb[b][:],
                                            x_d[128 * j:128 * j + 128, :]),
                                f"sXL{b}", 16)

                        @blk.vector
                        def _(v, j=j, b=b, so=so):
                            v.wait_ge(sems[f"sXL{b}"], 16 * (j // 2 + 1))
                            if j >= 2:
                                v.wait_ge(sems["sLN"], hget("hact", j - 2))
                            fence(v, v.reduce_sum(stat[:, so:so + 1], x_sb[b][:],
                                                  axis=AX.X))
                            fence(v, v.tensor_scalar(stat[:, so:so + 1],
                                                     stat[:, so:so + 1],
                                                     1.0 / DIM, None,
                                                     op0=ALU.mult))
                            inc(v.scalar_tensor_tensor(
                                scr[:], x_sb[b][:], stat[:, so:so + 1], x_sb[b][:],
                                op0=ALU.subtract, op1=ALU.mult), "sLN")
                            inc(v.reduce_sum(stat[:, so + 1:so + 2], scr[:],
                                             axis=AX.X), "sLN")
                        snap("ln_stats", j, "sLN")

                        @blk.scalar
                        def _(sc, j=j, so=so):
                            sc.wait_ge(sems["sLN"], hget("ln_stats", j))
                            fence(sc, sc.activation(stat[:, so + 2:so + 3],
                                                    stat[:, so + 1:so + 2], AF.Ln,
                                                    bias=EPS, scale=1.0 / DIM))
                            inc(sc.activation(stat[:, so + 2:so + 3],
                                              stat[:, so + 2:so + 3], AF.Exp,
                                              scale=-0.5), "sLN")
                        snap("ln_rstd", j, "sLN")

                        @blk.vector
                        def _(v, j=j, b=b, so=so):
                            v.wait_ge(sems["sLN"], hget("ln_rstd", j))
                            if j >= 2:
                                v.wait_ge(sems[f"sHF{j - 2}"], 16)
                            inc(v.tensor_scalar(h_sb[b][:], x_sb[b][:],
                                                stat[:, so:so + 1],
                                                stat[:, so + 2:so + 3],
                                                op0=ALU.subtract, op1=ALU.mult),
                                "sLN")
                        snap("hact", j, "sLN")

                        @blk.sync
                        def _(sy, j=j, b=b):
                            sy.wait_ge(sems["sLN"], hget("hact", j))
                            inc(sy.dma_start(h_dram[128 * j:128 * j + 128, :],
                                             h_sb[b][:]), f"sHF{j}", 16)

                        if (j + 1) % gsz == 0:
                            grp_ = (j + 1) // gsz - 1
                            r0 = 128 * gsz * grp_
                            nr = 128 * gsz
                            hsem = f"sHFM{grp_}"

                            @blk.sync
                            def _(sy, grp_=grp_, r0=r0, nr=nr, hsem=hsem):
                                for jd in range(grp_ * gsz,
                                                (grp_ + 1) * gsz):
                                    sy.wait_ge(sems[f"sHF{jd}"], 16)
                                if grp_ == 0:
                                    W(sy, "sMISC", ltd_init_done)
                                for kk in range(16):
                                    inc(sy.dma_start(
                                        h_fm[:, T * kk + r0:T * kk + r0 + nr],
                                        h_dram[r0:r0 + nr,
                                               128 * kk:128 * kk + 128],
                                        transpose=True), hsem, 16)

                        if j == min(2, nt - 1):
                            @blk.gpsimd
                            def _(g):
                                src = AP(wqT_d.tensor, 0,
                                         [[DIM, 128], [128 * DIM, 16],
                                          [1, 512]])
                                dst = AP(wbuf[0], 0,
                                         [[8192, 128], [512, 16], [1, 512]])
                                inc(g.dma_start(dst, src), "sW0", 16)
                        if j == min(6, nt - 1):
                            @blk.gpsimd
                            def _(g):
                                src = AP(wqT_d.tensor, 512,
                                         [[DIM, 128], [128 * DIM, 16],
                                          [1, 512]])
                                dst = AP(wbuf[1], 0,
                                         [[8192, 128], [512, 16], [1, 512]])
                                inc(g.dma_start(dst, src), "sW1", 16)

                wts = [wqT_d, wkT_d, wvT_d]
                drams = [q_dram, k_dram, v_dram]
                for wc in range(12):
                    tsr, ch = wc // 4, wc % 4

                    if wc >= 2:
                        @blk.gpsimd
                        def _(g, wc=wc, tsr=tsr, ch=ch):
                            g.wait_ge(sems["sQMM"], (wc - 1) * nt)
                            src = AP(wts[tsr].tensor, 512 * ch,
                                     [[DIM, 128], [128 * DIM, 16], [1, 512]])
                            dst = AP(wbuf[wc % 2], 0,
                                     [[8192, 128], [512, 16], [1, 512]])
                            inc(g.dma_start(dst, src), f"sW{wc % 2}", 16)

                    for j in range(nt):
                        cidx = wc * nt + j

                        @blk.tensor
                        def _(t, wc=wc, j=j, cidx=cidx):
                            t.wait_ge(sems[f"sHFM{min(j // gsz, ngrp - 1)}"],
                                      256)
                            t.wait_ge(sems[f"sW{wc % 2}"], 16 * (wc // 2 + 1))
                            if cidx >= 4:
                                t.wait_ge(sems["sEV"], cidx - 3)
                            pb = psb[cidx % 4]
                            mm = None
                            for kk in range(16):
                                mm = t.matmul(
                                    pb[:],
                                    h_fm[:, T * kk + 128 * j:T * kk + 128 * j + 128],
                                    wbuf[wc % 2][:, 512 * kk:512 * kk + 512],
                                    start=(kk == 0), stop=(kk == 15))
                            inc(mm, "sQMM")

                        @blk.vector
                        def _(v, cidx=cidx):
                            v.wait_ge(sems["sQMM"], cidx + 1)
                            if cidx >= 2:
                                v.wait_ge(sems[f"sQKVD{cidx % 2}"],
                                          16 * ((cidx - 2) // 2 + 1))
                            inc(v.tensor_copy(stg[cidx % 2][:], psb[cidx % 4][:]),
                                "sEV")

                        @blk.sync
                        def _(sy, tsr=tsr, ch=ch, j=j, cidx=cidx):
                            sy.wait_ge(sems["sEV"], cidx + 1)
                            inc(sy.dma_start(
                                drams[tsr][128 * j:128 * j + 128,
                                           512 * ch:512 * ch + 512],
                                stg[cidx % 2][:]), f"sQKVD{cidx % 2}", 16)

                paW.close()
                woT_sb = paWo.enter_context(
                    nc.sbuf_tensor("woT_sb", [128, 16 * DIM], BF16))
                bo_b = paWo.enter_context(nc.sbuf_tensor("bo_b", [1, DIM], BF16))

                lhsT1 = [[[pa.enter_context(
                    nc.sbuf_tensor(f"lhsT1_{p_}_{d}_{hh}", [128, 2048], BF16))
                    for hh in range(2)] for d in range(2)] for p_ in range(2)]
                rhs1 = [[pa.enter_context(
                    nc.sbuf_tensor(f"rhs1_{p_}_{d}", [128, 1024], BF16))
                    for d in range(2)] for p_ in range(2)]
                vst = [pa.enter_context(
                    nc.sbuf_tensor(f"vst{i}", [128, 2048], BF16))
                    for i in range(4)]
                lhsT2 = [pa.enter_context(
                    nc.sbuf_tensor(f"lhsT2_{i}", [128, 2048], BF16))
                    for i in range(4)]
                E1f = pa.enter_context(nc.sbuf_tensor("E1f", [128, 512], F32))
                E2f = pa.enter_context(nc.sbuf_tensor("E2f", [128, 512], F32))
                Ssum = pa.enter_context(nc.sbuf_tensor("Ssum", [128, 64], F32))
                D_sb = [pa.enter_context(
                    nc.sbuf_tensor(f"D_sb{i}", [128, 256], BF16))
                    for i in range(2)]
                o2_sb = pa.enter_context(nc.sbuf_tensor("o2_sb", [128, 2048], BF16))
                rmsd = pa.enter_context(nc.sbuf_tensor("rmsd", [128, 32], F32))
                Rb = pa.enter_context(nc.sbuf_tensor("Rb", [128, 1024], F32))
                QKVD_TOT = (C["sQKVD0"], C["sQKVD1"])
                psEsets = [(psE1, psE2), (psb[2], psb[3])]
                banks4 = [psM[0], psM[1], psb[0], psb[1]]

                @blk.vector
                def _(v):
                    v.wait_ge(sems["sQMM"], 12 * nt)
                    v.wait_ge(sems[f"sHF{nt - 1}"], 16)
                    for p_ in range(2):
                        for dd in range(2):
                            for hh in range(2):
                                inc(v.memset(lhsT1[p_][dd][hh][:], 0.0), "sLAM")
                            inc(v.memset(rhs1[p_][dd][:], 0.0), "sLAM")
                    for i in range(4):
                        inc(v.memset(vst[i][:], 0.0), "sLAM")
                        inc(v.memset(lhsT2[i][:], 0.0), "sLAM")
                lhsT1_zeroed = C["sLAM"]

                def em_repack(t):
                    tp = t % 2

                    @blk.sync
                    def _(sy, t=t, tp=tp):
                        sy.wait_ge(sems["sQKVD0"], QKVD_TOT[0])
                        sy.wait_ge(sems["sQKVD1"], QKVD_TOT[1])
                        if t < 2:
                            sy.wait_ge(sems["sLAM"], lhsT1_zeroed)
                        if t >= 2:
                            sy.wait_ge(sems["sATT"], t - 1)
                        tj = 128 * t
                        for dd in range(2):
                            for hh in range(2):
                                for s in range(4):
                                    dst = AP(lhsT1[tp][dd][hh],
                                             (32 * s) * 2048 + 16 * s,
                                             [[2048, 32], [64, 32], [1, 16]])
                                    src = AP(q_dram.tensor,
                                             (tj + s) * DIM + 1024 * dd + 16 * hh,
                                             [[32, 32], [4 * DIM, 32], [1, 16]])
                                    inc(sy.dma_start(dst, src), f"sRP{tp}", 16)
                    snap("rp", t, f"sRP{t % 2}")

                    @blk.scalar
                    def _(sc, t=t, tp=tp):
                        sc.wait_ge(sems["sQKVD0"], QKVD_TOT[0])
                        sc.wait_ge(sems["sQKVD1"], QKVD_TOT[1])
                        if t < 2:
                            sc.wait_ge(sems["sLAM"], lhsT1_zeroed)
                        if t >= 2:
                            sc.wait_ge(sems["sATT"], t - 1)
                        tj = 128 * t
                        for dd in range(2):
                            for s in range(4):
                                dst = AP(rhs1[tp][dd], (32 * s) * 1024,
                                         [[1024, 32], [32, 32], [1, 32]])
                                src = AP(k_dram.tensor,
                                         (tj + s) * DIM + 1024 * dd,
                                         [[32, 32], [4 * DIM, 32], [1, 32]])
                                inc(sc.dma_start(dst, src), f"sRPK{tp}", 16)
                    snap("rpk", t, f"sRPK{t % 2}")

                def em_front(j):
                    jp = j % 2
                    j4 = j % 4
                    tj = 128 * j

                    @blk.scalar
                    def _(sc, j=j, tj=tj):
                        sc.wait_ge(sems["sQKVD0"], QKVD_TOT[0])
                        sc.wait_ge(sems["sQKVD1"], QKVD_TOT[1])
                        for H in range(2):
                            for s in range(2, 4):
                                for G in range(2):
                                    q2 = 8 * H + 2 * s + G
                                    dst = AP(v2_dram.tensor,
                                             128 * j * DIM + 128 * q2,
                                             [[16 * DIM, 8], [DIM, 16], [1, 128]])
                                    src = AP(v_dram.tensor,
                                             (tj + 64 * H + 32 * G + s) * DIM,
                                             [[4 * DIM, 8], [128, 16], [1, 128]])
                                    inc(sc.dma_start(dst, src), "sVSC", 16)

                    @blk.sync
                    def _(sy, j=j, tj=tj, j4=j4):
                        sy.wait_ge(sems["sQKVD0"], QKVD_TOT[0])
                        sy.wait_ge(sems["sQKVD1"], QKVD_TOT[1])
                        if j < 4:
                            sy.wait_ge(sems["sLAM"], lhsT1_zeroed)
                        for H in range(2):
                            for s in range(2):
                                for G in range(2):
                                    q2 = 8 * H + 2 * s + G
                                    dst = AP(v2_dram.tensor,
                                             128 * j * DIM + 128 * q2,
                                             [[16 * DIM, 8], [DIM, 16], [1, 128]])
                                    src = AP(v_dram.tensor,
                                             (tj + 64 * H + 32 * G + s) * DIM,
                                             [[4 * DIM, 8], [128, 16], [1, 128]])
                                    inc(sy.dma_start(dst, src), "sVST", 16)
                        sy.wait_ge(sems["sVST"], C["sVST"])
                        sy.wait_ge(sems["sVSC"], C["sVSC"])
                        if j >= 4:
                            sy.wait_ge(sems["sM2"], 4 * (j - 3))
                        inc(sy.dma_start(vst[j4][:],
                                         v2_dram[128 * j:128 * j + 128, :]),
                            f"sVSTL{j4}", 16)
                    snap("vst", j, f"sVSTL{j % 4}")

                    @blk.tensor
                    def _(t, j=j, jp=jp):
                        t.wait_ge(sems[f"sRP{jp}"], hget("rp", j))
                        t.wait_ge(sems[f"sRPK{jp}"], hget("rpk", j))
                        if j >= 2:
                            t.wait_ge(sems["sEXP"], 2 * (j - 1))
                        eb1, eb2 = psEsets[jp]
                        mm = None
                        for p in range(32):
                            H, g16 = p // 16, p % 16
                            for hh, bank in ((0, eb1), (1, eb2)):
                                t.matmul(bank[64 * H:64 * H + 64,
                                              32 * g16:32 * g16 + 32],
                                         lhsT1[jp][0][hh][:, 64 * p:64 * p + 64],
                                         rhs1[jp][0][:, 32 * p:32 * p + 32],
                                         start=True, stop=False)
                                mm = t.matmul(
                                    bank[64 * H:64 * H + 64,
                                         32 * g16:32 * g16 + 32],
                                    lhsT1[jp][1][hh][:, 64 * p:64 * p + 64],
                                    rhs1[jp][1][:, 32 * p:32 * p + 32],
                                    start=False, stop=True)
                        inc(mm, "sATT")

                    @blk.scalar
                    def _(sc, j=j, jp=jp):
                        sc.wait_ge(sems["sATT"], j + 1)
                        if j >= 1:
                            sc.wait_ge(sems["sD"], j)
                        eb1, eb2 = psEsets[jp]
                        inc(sc.activation(E1f[:], eb1[:], AF.Exp, scale=0.125),
                            "sEXP")
                        inc(sc.activation(E2f[:], eb2[:], AF.Exp, scale=0.125),
                            "sEXP")

                    @blk.vector
                    def _(v, j=j, jp=jp):
                        v.wait_ge(sems["sEXP"], 2 * (j + 1))
                        if j >= 2:
                            v.wait_ge(sems[f"sDD{jp}"],
                                      256 * ((j - 2) // 2 + 1))
                        fence(v, v.reduce_sum(
                            Ssum[:, 0:16],
                            E1f[:].rearrange("p (g j) -> p g j", g=16),
                            axis=AX.X))
                        fence(v, v.reduce_sum(
                            Ssum[:, 32:48],
                            E2f[:].rearrange("p (g j) -> p g j", g=16),
                            axis=AX.X))
                        fence(v, v.reciprocal(Ssum[:, 16:32], Ssum[:, 0:16]))
                        fence(v, v.reciprocal(Ssum[:, 48:64], Ssum[:, 32:48]))
                        v.tensor_copy(
                            AP(Rb, 0, [[1024, 128], [32, 16], [1, 32]]),
                            AP(Ssum, 16, [[64, 128], [1, 16], [0, 32]]))
                        v.tensor_copy(
                            AP(Rb, 512, [[1024, 128], [32, 16], [1, 32]]),
                            AP(Ssum, 48, [[64, 128], [1, 16], [0, 32]]))
                        inc(v.tensor_tensor(E1f[:], E1f[:], Rb[:, 0:512],
                                            op=ALU.mult), "sSMX")
                        inc(v.tensor_tensor(E2f[:], E2f[:], Rb[:, 512:1024],
                                            op=ALU.mult), "sSMX")
                        inc(v.scalar_tensor_tensor(
                            AP(D_sb[jp], 0, [[256, 128], [16, 16], [1, 16]]),
                            AP(E2f, 16, [[512, 128], [32, 16], [1, 16]]),
                            neglam_b[:],
                            AP(E1f, 0, [[512, 128], [32, 16], [1, 16]]),
                            op0=ALU.mult, op1=ALU.add), "sD")

                    @blk.gpsimd
                    def _(g, j=j, jp=jp, j4=j4):
                        g.wait_ge(sems["sD"], j + 1)
                        if j >= 4:
                            g.wait_ge(sems[f"sLT2{j4}"],
                                      16 * ((j - 4) // 4 + 1))
                        if j < 4:
                            g.wait_ge(sems["sMISC"], ltd_init_done)
                        for H in range(2):
                            for s in range(4):
                                for G in range(2):
                                    q2 = 8 * H + 2 * s + G
                                    src = AP(D_sb[jp],
                                             (64 * H + 16 * s) * 256 + 128 * G,
                                             [[256, 16], [16, 8], [1, 16]])
                                    dst = AP(ltd[j4].tensor, 128 * 128 * q2,
                                             [[128, 16], [2064, 8], [1, 16]])
                                    inc(g.dma_start(dst, src), f"sDD{jp}", 16)

                def em_load(t):
                    t4 = t % 4
                    tp = t % 2

                    @blk.scalar
                    def _(sc, t=t, t4=t4, tp=tp):
                        sc.wait_ge(sems[f"sDD{tp}"], 256 * (t // 2 + 1))
                        if t < 4:
                            sc.wait_ge(sems["sLAM"], lhsT1_zeroed)
                        if t >= 4:
                            sc.wait_ge(sems["sM2"], 4 * (t - 3))
                        inc(sc.dma_start(lhsT2[t4][:], ltd[t4][:, :],
                                         transpose=True), f"sLT2{t4}", 16)

                def em_back(jj):
                    j4 = jj % 4
                    tjj = 128 * jj
                    for grp in range(4):
                        @blk.tensor
                        def _(t, jj=jj, j4=j4, grp=grp):
                            t.wait_ge(sems[f"sLT2{j4}"], 16 * (jj // 4 + 1))
                            t.wait_ge(sems[f"sVSTL{j4}"], hget("vst", jj))
                            if jj >= 1:
                                t.wait_ge(sems["sO2"],
                                          16 * (jj - 1) + 4 * (grp + 1))
                            bank = banks4[grp]
                            mm = None
                            for qq in range(4):
                                q2 = grp * 4 + qq
                                mm = t.matmul(
                                    bank[:, 128 * qq:128 * qq + 128],
                                    lhsT2[j4][:, 128 * q2:128 * q2 + 128],
                                    vst[j4][:, 128 * q2:128 * q2 + 128],
                                    start=True, stop=True)
                            inc(mm, "sM2")

                        @blk.scalar
                        def _(sc, jj=jj, grp=grp):
                            sc.wait_ge(sems["sM2"], jj * 4 + grp + 1)
                            W(sc, "sRMS")
                            inc(sc.activation(scrB[:], banks4[grp][:],
                                              AF.Square), "sRMS")

                        @blk.vector
                        def _(v, grp=grp):
                            W(v, "sRMS")
                            inc(v.reduce_sum(rmsd[:, 4 * grp:4 * grp + 4],
                                             scrB[:].rearrange(
                                                 "p (a b) -> p a b", a=4),
                                             axis=AX.X), "sRMS")

                    @blk.scalar
                    def _(sc):
                        W(sc, "sRMS")
                        fence(sc, sc.activation(rmsd[:, 16:32],
                                                rmsd[:, 0:16], AF.Ln,
                                                bias=EPS, scale=1.0 / 128))
                        inc(sc.activation(rmsd[:, 16:32],
                                          rmsd[:, 16:32], AF.Exp,
                                          bias=lnoml_b[:, 0:1], scale=-0.5),
                            "sRMS")

                    @blk.vector
                    def _(v, jj=jj):
                        W(v, "sRMS")
                        if jj >= 1:
                            v.wait_ge(sems["sO2D"], 256 * jj)
                        for q2 in range(16):
                            inc(v.tensor_scalar(
                                o2_sb[:, 128 * q2:128 * q2 + 128],
                                banks4[q2 // 4][:, 128 * (q2 % 4):
                                                128 * (q2 % 4) + 128],
                                rmsd[:, 16 + q2:17 + q2], None,
                                op0=ALU.mult), "sO2")

                    @blk.gpsimd
                    def _(g, jj=jj, tjj=tjj):
                        g.wait_ge(sems["sO2"], 16 * (jj + 1))
                        inc(g.dma_start(o2p_dram[128 * jj:128 * jj + 128, :],
                                        o2_sb[:]), "sO2P", 16)
                        g.wait_ge(sems["sO2P"], C["sO2P"])
                        for H in range(2):
                            for s in range(4):
                                for G in range(2):
                                    q2 = 8 * H + 2 * s + G
                                    src = AP(o2p_dram.tensor,
                                             128 * jj * DIM + 128 * q2,
                                             [[16 * DIM, 8], [DIM, 16], [1, 128]])
                                    dst = AP(o2_dram.tensor,
                                             (tjj + 64 * H + 32 * G + s) * DIM,
                                             [[4 * DIM, 8], [128, 16], [1, 128]])
                                    inc(g.dma_start(dst, src), "sO2D", 16)

                def em_wot_load(a0, a1, first):
                    @blk.gpsimd
                    def _(g, a0=a0, a1=a1, first=first):
                        g.wait_ge(sems["sLAM"], C["sLAM"])
                        if first:
                            inc(g.dma_start(bo_b[:], bo_d[:, :]), "sWOT", 16)
                        for a in range(a0, a1):
                            src = AP(woT_d.tensor, 128 * a * DIM,
                                     [[DIM, 128], [1, DIM]])
                            dst = AP(woT_sb, DIM * a,
                                     [[16 * DIM, 128], [1, DIM]])
                            inc(g.dma_start(dst, src), "sWOT", 16)

                for it in range(nt + 3):
                    if it == 0:
                        em_repack(0)
                        if nt > 1:
                            em_repack(1)
                    elif it + 1 <= nt - 1:
                        em_repack(it + 1)
                    if it < nt:
                        em_front(it)
                    if 1 <= it <= nt:
                        em_load(it - 1)
                    if it >= 3:
                        em_back(it - 3)
                    if nt >= 12:
                        if 2 <= it <= 9:
                            @blk.vector
                            def _(v, it=it):
                                if it == 2:
                                    v.wait_ge(sems["sQMM"], 12 * nt)
                                    inc(v.memset(bo_b[:], 0.0), "sLAM")
                                c0 = 4096 * (it - 2)
                                inc(v.memset(woT_sb[:, c0:c0 + 4096], 0.0),
                                    "sLAM")
                        if 3 <= it <= 10:
                            em_wot_load(2 * (it - 3), 2 * (it - 3) + 2,
                                        it == 3)
                    elif it == 2:
                        @blk.vector
                        def _(v):
                            v.wait_ge(sems["sQMM"], 12 * nt)
                            inc(v.memset(bo_b[:], 0.0), "sLAM")
                            inc(v.memset(woT_sb[:], 0.0), "sLAM")
                        em_wot_load(0, 16, True)

            A4_GATE = dict(sO2D=256 * nt, sM2=4 * nt, sATT=nt, sD=nt,
                           sO2=16 * nt, sEXP=2 * nt, sRMS=C["sRMS"],
                           sDD0=C["sDD0"], sDD1=C["sDD1"],
                           sLT20=C["sLT20"], sLT21=C["sLT21"],
                           sLT22=C["sLT22"], sLT23=C["sLT23"],
                           sVSTL0=C["sVSTL0"], sVSTL1=C["sVSTL1"],
                           sVSTL2=C["sVSTL2"], sVSTL3=C["sVSTL3"],
                           sRP0=C["sRP0"], sRP1=C["sRP1"],
                           sRPK0=C["sRPK0"], sRPK1=C["sRPK1"])

            def a4gate(eng):
                for k, v in A4_GATE.items():
                    eng.wait_ge(sems[k], v)

            pa5 = contextlib.ExitStack()
            with pa5:
                o2_fm = pa5.enter_context(
                    nc.sbuf_tensor("o2_fm", [128, 16 * T], BF16))
                x_sb5 = [pa5.enter_context(
                    nc.sbuf_tensor(f"x5{i}", [128, 2048], F32)) for i in range(2)]
                h2_sb = [pa5.enter_context(
                    nc.sbuf_tensor(f"h2{i}", [128, 2048], BF16)) for i in range(2)]
                scr5 = pa5.enter_context(nc.sbuf_tensor("scr5", [128, 2048], F32))

                @blk.vector
                def _(v):
                    a4gate(v)
                    for i in range(2):
                        inc(v.memset(x_sb5[i][:], 0.0), "sLAM")
                        inc(v.memset(h2_sb[i][:], 0.0), "sLAM")
                pa5_init = C["sLAM"]

                @blk.sync
                def _(sy):
                    a4gate(sy)
                    sy.wait_ge(sems["sO2D"], 256 * nt)
                    sy.wait_ge(sems["sLAM"], pa5_init)
                    for a in range(16):
                        inc(sy.dma_start(o2_fm[:, T * a:T * a + T],
                                         o2_dram[:, 128 * a:128 * a + 128],
                                         transpose=True), "sOFM", 16)

                base_qmm = C["sQMM"]
                for j in range(nt):
                    b = j % 2
                    so = 8 * b

                    @blk.gpsimd
                    def _(g, j=j, b=b):
                        a4gate(g)
                        if j < 2:
                            g.wait_ge(sems["sLAM"], pa5_init)
                        if j >= 2:
                            g.wait_ge(sems[f"sX1D{b}"], 16 * ((j - 2) // 2 + 1))
                        inc(g.dma_start(x_sb5[b][:], x_d[128 * j:128 * j + 128, :]),
                            f"sXL{b}", 16)
                    snap("x5", j, f"sXL{b}")

                    for ch in range(4):
                        cidx = j * 4 + ch

                        @blk.tensor
                        def _(t, j=j, ch=ch, cidx=cidx):
                            if cidx < 4:
                                t.wait_ge(sems["sEV"], 12 * nt)
                            t.wait_ge(sems["sWOT"], 272)
                            t.wait_ge(sems["sOFM"], 256)
                            if cidx >= 4:
                                t.wait_ge(sems["sX1"], cidx - 3)
                            pb = psb[cidx % 4]
                            for a in range(16):
                                t.matmul(
                                    pb[:],
                                    o2_fm[:, T * a + 128 * j:T * a + 128 * j + 128],
                                    woT_sb[:, DIM * a + 512 * ch:
                                           DIM * a + 512 * ch + 512],
                                    start=(a == 0), stop=False)
                            mm = t.matmul(pb[:], ones_b[:],
                                          bo_b[:, 512 * ch:512 * ch + 512],
                                          start=False, stop=True)
                            inc(mm, "sQMM")

                        @blk.vector
                        def _(v, j=j, ch=ch, b=b, cidx=cidx):
                            v.wait_ge(sems["sQMM"], base_qmm + cidx + 1)
                            v.wait_ge(sems[f"sXL{b}"], hget("x5", j))
                            inc(v.tensor_tensor(
                                x_sb5[b][:, 512 * ch:512 * ch + 512],
                                x_sb5[b][:, 512 * ch:512 * ch + 512],
                                psb[cidx % 4][:], op=ALU.add), "sX1")

                    @blk.vector
                    def _(v, j=j, b=b, so=so):
                        v.wait_ge(sems["sX1"], 4 * (j + 1))
                        if j >= 2:
                            v.wait_ge(sems["sH2"], j - 1)
                        fence(v, v.reduce_sum(stat[:, so:so + 1], x_sb5[b][:],
                                              axis=AX.X))
                        fence(v, v.tensor_scalar(stat[:, so:so + 1],
                                                 stat[:, so:so + 1], 1.0 / DIM,
                                                 None, op0=ALU.mult))
                        inc(v.scalar_tensor_tensor(
                            scr5[:], x_sb5[b][:], stat[:, so:so + 1], x_sb5[b][:],
                            op0=ALU.subtract, op1=ALU.mult), "sLN")
                        inc(v.reduce_sum(stat[:, so + 1:so + 2], scr5[:],
                                         axis=AX.X), "sLN")
                    snap("ln2_stats", j, "sLN")

                    @blk.scalar
                    def _(sc, j=j, so=so):
                        sc.wait_ge(sems["sLN"], hget("ln2_stats", j))
                        fence(sc, sc.activation(stat[:, so + 2:so + 3],
                                                stat[:, so + 1:so + 2], AF.Ln,
                                                bias=EPS, scale=1.0 / DIM))
                        inc(sc.activation(stat[:, so + 2:so + 3],
                                          stat[:, so + 2:so + 3], AF.Exp,
                                          scale=-0.5), "sLN")
                    snap("ln2_rstd", j, "sLN")

                    @blk.vector
                    def _(v, j=j, b=b, so=so):
                        v.wait_ge(sems["sLN"], hget("ln2_rstd", j))
                        if j >= 2:
                            v.wait_ge(sems[f"sH2D{b}"],
                                      16 * ((j - 2) // 2 + 1))
                        inc(v.tensor_scalar(h2_sb[b][:], x_sb5[b][:],
                                            stat[:, so:so + 1],
                                            stat[:, so + 2:so + 3],
                                            op0=ALU.subtract, op1=ALU.mult),
                            "sH2")

                    @blk.sync
                    def _(sy, j=j, b=b):
                        sy.wait_ge(sems["sH2"], j + 1)
                        inc(sy.dma_start(h2_dram[128 * j:128 * j + 128, :],
                                         h2_sb[b][:]), f"sH2D{b}", 16)
                        inc(sy.dma_start(x1_dram[128 * j:128 * j + 128, :],
                                         x_sb5[b][:]), f"sX1D{b}", 16)

            paWo.close()
            A5_GATE = dict(sQMM=C["sQMM"], sH2D0=C["sH2D0"], sH2D1=C["sH2D1"],
                           sX1D0=C["sX1D0"], sX1D1=C["sX1D1"])

            def a5gate(eng):
                for k, v in A5_GATE.items():
                    eng.wait_ge(sems[k], v)

            pbO = contextlib.ExitStack()
            with pbO:
                h2_fm = pbO.enter_context(
                    nc.sbuf_tensor("h2_fm", [128, 16 * T], BF16))
                gw_sb = pbO.enter_context(nc.sbuf_tensor("gw_sb", [128, 144], BF16))
                comb9 = pbO.enter_context(
                    nc.sbuf_tensor("comb9", [128, 16 * nt], F32))
                gstat = pbO.enter_context(nc.sbuf_tensor("gstat", [128, 32], F32))

                @blk.vector
                def _(v):
                    a5gate(v)
                    inc(v.memset(h2_fm[:], 0.0), "sLAM")
                    inc(v.memset(gw_sb[:], 0.0), "sLAM")
                pb0_init = C["sLAM"]

                @blk.sync
                def _(sy):
                    a5gate(sy)
                    sy.wait_ge(sems["sLAM"], pb0_init)
                    for kk in range(16):
                        inc(sy.dma_start(h2_fm[:, T * kk:T * kk + T],
                                         h2_dram[:, 128 * kk:128 * kk + 128],
                                         transpose=True), "sH2FM", 16)

                @blk.gpsimd
                def _(g):
                    a5gate(g)
                    g.wait_ge(sems["sLAM"], pb0_init)
                    src = AP(gwT_d.tensor, 0, [[9, 128], [128 * 9, 16], [1, 9]])
                    dst = AP(gw_sb, 0, [[144, 128], [9, 16], [1, 9]])
                    inc(g.dma_start(dst, src), "sGW", 16)

                for j in range(nt):
                    go = 16 * (j % 2)
                    sbo = 16 * (j % 2)

                    @blk.tensor
                    def _(t, j=j):
                        t.wait_ge(sems["sH2FM"], 256)
                        t.wait_ge(sems["sGW"], 16)
                        if j >= 2:
                            t.wait_ge(sems["sGTV"], hget("gt2", j - 2))
                        bank = psM[j % 2]
                        mm = None
                        for kk in range(16):
                            mm = t.matmul(bank[:, 0:9],
                                          h2_fm[:, T * kk + 128 * j:
                                                T * kk + 128 * j + 128],
                                          gw_sb[:, 9 * kk:9 * kk + 9],
                                          start=(kk == 0), stop=(kk == 15))
                        inc(mm, "sSCM")

                    @blk.vector
                    def _(v, j=j, go=go, sbo=sbo):
                        v.wait_ge(sems["sSCM"], j + 1)
                        if j >= 2:
                            v.wait_ge(sems["sCMB"], hget("cmb", j - 2))
                        sc_ = psM[j % 2][:, 0:8]
                        fence(v, v.reduce_max(gstat[:, go:go + 1], sc_,
                                              axis=AX.X))
                        fence(v, v.tensor_scalar(gstat[:, go + 2:go + 10], sc_,
                                                 gstat[:, go:go + 1], None,
                                                 op0=ALU.is_ge))
                        fence(v, v.scalar_tensor_tensor(
                            scrB[:, sbo:sbo + 8], gstat[:, go + 2:go + 10],
                            -1e30, sc_, op0=ALU.mult, op1=ALU.add))
                        fence(v, v.reduce_max(gstat[:, go + 1:go + 2],
                                              scrB[:, sbo:sbo + 8], axis=AX.X))
                        fence(v, v.tensor_tensor(gstat[:, go + 10:go + 11],
                                                 gstat[:, go:go + 1],
                                                 gstat[:, go + 1:go + 2],
                                                 op=ALU.subtract))
                        fence(v, v.tensor_scalar(scrB[:, sbo + 8:sbo + 16], sc_,
                                                 gstat[:, go + 1:go + 2], None,
                                                 op0=ALU.is_ge))
                        inc(v.tensor_tensor(scrB[:, sbo + 8:sbo + 16],
                                            scrB[:, sbo + 8:sbo + 16],
                                            gstat[:, go + 2:go + 10],
                                            op=ALU.subtract), "sGTV")
                    snap("gt1", j, "sGTV")

                    @blk.scalar
                    def _(sc, j=j, go=go):
                        sc.wait_ge(sems["sGTV"], hget("gt1", j))
                        sc.wait_ge(sems["sFEN"], C["sFEN"])
                        inc(sc.activation(gstat[:, go + 12:go + 13],
                                          gstat[:, go + 10:go + 11], AF.Sigmoid),
                            "sGTV")
                        inc(sc.activation(gstat[:, go + 13:go + 14],
                                          gstat[:, go + 10:go + 11], AF.Sigmoid,
                                          scale=-1.0), "sGTV")
                        inc(sc.activation(comb9[:, 16 * j + 8:16 * j + 9],
                                          psM[j % 2][:, 8:9], AF.Sigmoid), "sGTV")
                    snap("gt2", j, "sGTV")

                    @blk.vector
                    def _(v, j=j, go=go, sbo=sbo):
                        v.wait_ge(sems["sGTV"], hget("gt2", j))
                        fence(v, v.tensor_scalar(gstat[:, go + 2:go + 10],
                                                 gstat[:, go + 2:go + 10],
                                                 gstat[:, go + 12:go + 13], None,
                                                 op0=ALU.mult))
                        inc(v.scalar_tensor_tensor(
                            comb9[:, 16 * j:16 * j + 8],
                            scrB[:, sbo + 8:sbo + 16],
                            gstat[:, go + 13:go + 14],
                            gstat[:, go + 2:go + 10],
                            op0=ALU.mult, op1=ALU.add), "sCMB")
                    snap("cmb", j, "sCMB")

                pbW = contextlib.ExitStack()
                with pbW:
                    hid = [pbW.enter_context(
                        nc.sbuf_tensor(f"hid{i}", [128, 8 * T], BF16))
                        for i in range(2)]
                    wup = [[pbW.enter_context(
                        nc.sbuf_tensor(f"wup{gu}{i}", [128, 2048], BF16))
                        for i in range(2)] for gu in range(2)]
                    wdn = [pbW.enter_context(
                        nc.sbuf_tensor(f"wdn{i}", [128, 4096], BF16))
                        for i in range(4)]
                    comb_bc = pbW.enter_context(
                        nc.sbuf_tensor("comb_bc", [128, T], BF16))
                    sg_sb = pbW.enter_context(
                        nc.sbuf_tensor("sg_sb", [128, 512], BF16))
                    us_sb = pbW.enter_context(
                        nc.sbuf_tensor("us_sb", [128, 512], BF16))
                    stg_dn = [pbW.enter_context(
                        nc.sbuf_tensor(f"stgdn{i}", [128, 512], F32))
                        for i in range(2)]
                    comb_row = pbW.enter_context(
                        nc.sbuf_tensor("comb_row", [1, T], BF16))
                    @blk.vector
                    def _(v):
                        a5gate(v)
                        for gu in range(2):
                            for i in range(2):
                                inc(v.memset(wup[gu][i][:], 0.0), "sLAM")
                        for i in range(4):
                            inc(v.memset(wdn[i][:], 0.0), "sLAM")
                        inc(v.memset(comb_row[:], 0.0), "sLAM")
                    pbw_init = C["sLAM"]

                    up_emitted = set()

                    def em_upload(E, ch, gu):
                        up_emitted.add((E, ch, gu))
                        wsrc = upg_d if gu == 0 else upu_d

                        @blk.gpsimd
                        def _(g, E=E, ch=ch, gu=gu, wsrc=wsrc):
                            li = (E * UPC + ch) * 2 + gu
                            if li < 4:
                                a5gate(g)
                                g.wait_ge(sems["sLAM"], pbw_init)
                            if li >= 4:
                                g.wait_ge(sems["sUMM"],
                                          hget("upw_free", li - 4))
                            src = AP(wsrc.tensor,
                                     E * DIM * 1024 + 128 * ch,
                                     [[1024, 128], [128 * 1024, 16],
                                      [1, 128]])
                            dst = AP(wup[gu][ch % 2], 0,
                                     [[2048, 128], [128, 16], [1, 128]])
                            inc(g.dma_start(dst, src),
                                f"sUW{gu * 2 + ch % 2}", 16)
                        snap("uw", (E * UPC + ch) * 2 + gu,
                             f"sUW{gu * 2 + ch % 2}")

                    NE = 10
                    UPC = 8
                    base_x1 = C["sX1"]

                    up_chain = 0
                    dn_chain = 0
                    for E in range(NE):
                        eb = E % 2
                        rE = E if E < 8 else 8

                        for ci, (t0, tl) in enumerate(TCH):
                            ntile = tl // 128

                            @blk.tensor
                            def _(t, E=E, rE=rE, ci=ci, t0=t0, ntile=ntile):
                                t.wait_ge(sems["sCMB"], C["sCMB"])
                                t.wait_ge(sems["sGTV"], C["sGTV"])
                                if ci >= 2 or E >= 1:
                                    t.wait_ge(sems["sCBE"],
                                              max(0, C["sCBE"] - 1))
                                mm = None
                                for jj in range(ntile):
                                    jt = (t0 // 128) + jj
                                    mm = t.matmul(
                                        psM[ci % 2][0:1, 128 * jj:128 * jj + 128],
                                        comb9[:, 16 * jt + rE:16 * jt + rE + 1],
                                        ident[:], start=True, stop=True)
                                inc(mm, "sCB")

                            @blk.vector
                            def _(v, E=E, ci=ci, t0=t0, tl=tl):
                                W(v, "sCB")
                                if E >= 1:
                                    v.wait_ge(sems["sCB"], hget("bcast", E - 1))
                                inc(v.tensor_copy(comb_row[0:1, t0:t0 + tl],
                                                  psM[ci % 2][0:1, 0:tl]), "sCR")

                            @blk.tensor
                            def _(t, E=E, ci=ci, t0=t0, tl=tl):
                                t.wait_ge(sems["sCR"], C["sCR"])
                                inc(t.matmul(psM[ci % 2][:, 0:tl], ones_b[:],
                                             comb_row[0:1, t0:t0 + tl],
                                             start=True, stop=True), "sCB")

                            @blk.vector
                            def _(v, E=E, ci=ci, t0=t0, tl=tl):
                                W(v, "sCB")
                                inc(v.tensor_copy(comb_bc[:, t0:t0 + tl],
                                                  psM[ci % 2][:, 0:tl]), "sCBE")
                        snap("bcast", E, "sCB")

                        for ch in range(UPC):
                            for gu in range(2):
                                if (E, ch, gu) in up_emitted:
                                    continue
                                em_upload(E, ch, gu)

                            for ci, (t0, tl) in enumerate(TCH):
                                @blk.tensor
                                def _(t, E=E, ch=ch, ci=ci, t0=t0, tl=tl, eb=eb,
                                      up_chain=up_chain):
                                    li = (E * UPC + ch) * 2
                                    t.wait_ge(sems[f"sUW{0 * 2 + ch % 2}"],
                                              hget("uw", li))
                                    t.wait_ge(sems[f"sUW{1 * 2 + ch % 2}"],
                                              hget("uw", li + 1))
                                    t.wait_ge(sems["sCBE"], C["sCBE"])
                                    if E >= 2 and ch == 0 and ci == 0:
                                        t.wait_ge(sems["sDMM"],
                                                  hget("dn_done", E - 2))
                                    if up_chain == 0 and ci == 0:
                                        t.wait_ge(sems["sX1"], base_x1)
                                    if E >= 1 and E % 2 == 0 and ch == 0 and ci == 0:
                                        t.wait_ge(sems["sDNE"], C["sDNE"])
                                    for gu in range(2):
                                        cno = up_chain + (ci * 2 + gu)
                                        if cno >= 4:
                                            t.wait_ge(sems["sUPE"],
                                                      hget("up_cons", cno - 4))
                                        pb = psb[cno % 4]
                                        mm = None
                                        for kk in range(16):
                                            mm = t.matmul(
                                                pb[:, 0:tl],
                                                wup[gu][ch % 2][:,
                                                                128 * kk:128 * kk + 128],
                                                h2_fm[:, T * kk + t0:
                                                      T * kk + t0 + tl],
                                                start=(kk == 0), stop=(kk == 15))
                                        inc(mm, "sUMM")

                                @blk.scalar
                                def _(sc, E=E, ch=ch, ci=ci, tl=tl,
                                      up_chain=up_chain):
                                    sc.wait_ge(sems["sUMM"], up_chain + ci * 2 + 1)
                                    W(sc, "sUPE")
                                    gbank = psb[(up_chain + ci * 2) % 4]
                                    inc(sc.activation(sg_sb[:, 0:tl],
                                                      gbank[:, 0:tl], AF.Silu),
                                        "sUPE")
                                    snap("up_cons",
                                         up_chain + ci * 2, "sUPE")

                                @blk.vector
                                def _(v, E=E, ch=ch, ci=ci, t0=t0, tl=tl, eb=eb,
                                      up_chain=up_chain):
                                    v.wait_ge(sems["sUMM"], up_chain + ci * 2 + 2)
                                    W(v, "sUPE")
                                    ubank = psb[(up_chain + ci * 2 + 1) % 4]
                                    inc(v.tensor_tensor(us_sb[:, 0:tl],
                                                        ubank[:, 0:tl],
                                                        comb_bc[:, t0:t0 + tl],
                                                        op=ALU.mult), "sUPE")
                                    snap("up_cons",
                                         up_chain + ci * 2 + 1, "sUPE")
                                    inc(v.tensor_tensor(
                                        AP(hid[eb], T * ch + t0,
                                           [[8 * T, 128], [1, tl]]),
                                        sg_sb[:, 0:tl], us_sb[:, 0:tl],
                                        op=ALU.mult), "sHID")

                            up_chain += 2 * len(TCH)
                            snap("upw_free", (E * UPC + ch) * 2, "sUMM")
                            snap("upw_free", (E * UPC + ch) * 2 + 1, "sUMM")

                        if E % 2 == 1:
                            pair = E // 2

                            def em_dload(pair, cc, ei):
                                @blk.gpsimd
                                def _(g, pair=pair, cc=cc, ei=ei):
                                    li = (pair * 4 + cc) * 2 + ei
                                    if li < 4:
                                        g.wait_ge(sems["sLAM"], pbw_init)
                                    if li >= 4:
                                        g.wait_ge(sems["sDMM"],
                                                  hget("dnw_free", li - 4))
                                    Ex = 2 * pair + ei
                                    src = AP(dnw_d.tensor,
                                             Ex * 1024 * DIM + 512 * cc,
                                             [[DIM, 128], [128 * DIM, 8],
                                              [1, 512]])
                                    dst = AP(wdn[(cc % 2) * 2 + ei], 0,
                                             [[4096, 128], [512, 8], [1, 512]])
                                    inc(g.dma_start(dst, src),
                                        f"sDW{(cc % 2) * 2 + ei}", 16)
                                    snap("dw", (pair * 4 + cc) * 2 + ei,
                                         f"sDW{(cc % 2) * 2 + ei}")

                            for cc in range(4):
                                if cc == 0:
                                    for ei in range(2):
                                        em_dload(pair, 0, ei)
                                    for ei in range(2):
                                        em_dload(pair, 1, ei)
                                elif cc + 1 <= 3:
                                    for ei in range(2):
                                        em_dload(pair, cc + 1, ei)

                                if cc == 0 and E + 1 < NE:
                                    for ch_ in range(2):
                                        for gu_ in range(2):
                                            em_upload(E + 1, ch_, gu_)

                                for j in range(nt):
                                    @blk.tensor
                                    def _(t, pair=pair, cc=cc, j=j,
                                          dn_chain=dn_chain, upc=up_chain):
                                        li = (pair * 4 + cc) * 2
                                        t.wait_ge(sems[f"sDW{(cc % 2) * 2}"],
                                                  hget("dw", li))
                                        t.wait_ge(sems[f"sDW{(cc % 2) * 2 + 1}"],
                                                  hget("dw", li + 1))
                                        t.wait_ge(sems["sHID"], C["sHID"])
                                        cno = dn_chain + j
                                        if cno < dn_chain + 4 and cc == 0 and j < 4:
                                            t.wait_ge(sems["sUPE"], C["sUPE"])
                                        if cno >= 4:
                                            t.wait_ge(sems["sDNE"], cno - 3)
                                        pb = psb[cno % 4]
                                        mm = None
                                        for ei in range(2):
                                            for ht in range(8):
                                                mm = t.matmul(
                                                    pb[:],
                                                    AP(hid[ei], T * ht + 128 * j,
                                                       [[8 * T, 128], [1, 128]]),
                                                    wdn[(cc % 2) * 2 + ei][
                                                        :, 512 * ht:512 * ht + 512],
                                                    start=(ei == 0 and ht == 0),
                                                    stop=(ei == 1 and ht == 7))
                                        inc(mm, "sDMM")

                                    @blk.scalar
                                    def _(sc, cc=cc, j=j, dn_chain=dn_chain):
                                        cno = dn_chain + j
                                        sc.wait_ge(sems["sDMM"], cno + 1)
                                        if cno >= 2:
                                            sc.wait_ge(sems[f"sACC{cno % 2}"],
                                                       16 * ((cno - 2) // 2 + 1))
                                        inc(sc.copy(stg_dn[cno % 2][:],
                                                    psb[cno % 4][:]), "sDNE")

                                    @blk.gpsimd
                                    def _(g, pair=pair, cc=cc, j=j,
                                          dn_chain=dn_chain):
                                        cno = dn_chain + j
                                        g.wait_ge(sems["sDNE"], cno + 1)
                                        op = ALU.bypass if pair == 0 else ALU.add
                                        inc(g.dma_start(
                                            acc_dram[128 * j:128 * j + 128,
                                                     512 * cc:512 * cc + 512],
                                            stg_dn[cno % 2][:], accum_op=op),
                                            f"sACC{cno % 2}", 16)

                                dn_chain += nt
                                snap("dnw_free", (pair * 4 + cc) * 2, "sDMM")
                                snap("dnw_free", (pair * 4 + cc) * 2 + 1, "sDMM")
                            snap("dn_done", E, "sDMM")

                pb4 = contextlib.ExitStack()
                with pb4:
                    x1b = [pb4.enter_context(
                        nc.sbuf_tensor(f"x1b{i}", [128, 2048], F32))
                        for i in range(2)]
                    accb = [pb4.enter_context(
                        nc.sbuf_tensor(f"accb{i}", [128, 2048], F32))
                        for i in range(2)]
                    ACC_TOTAL = (C["sACC0"], C["sACC1"])

                    @blk.vector
                    def _(v):
                        v.wait_ge(sems["sACC0"], ACC_TOTAL[0])
                        v.wait_ge(sems["sACC1"], ACC_TOTAL[1])
                        for i in range(2):
                            inc(v.memset(x1b[i][:], 0.0), "sLAM")
                            inc(v.memset(accb[i][:], 0.0), "sLAM")
                    pb4_init = C["sLAM"]

                    for j in range(nt):
                        b = j % 2

                        @blk.gpsimd
                        def _(g, j=j, b=b):
                            g.wait_ge(sems["sACC0"], ACC_TOTAL[0])
                            g.wait_ge(sems["sACC1"], ACC_TOTAL[1])
                            g.wait_ge(sems["sLAM"], pb4_init)
                            if j >= 2:
                                g.wait_ge(sems[f"sY{b}"],
                                          16 * ((j - 2) // 2 + 1))
                            inc(g.dma_start(x1b[b][:],
                                            x1_dram[128 * j:128 * j + 128, :]),
                                f"sFB{b}", 16)
                            inc(g.dma_start(accb[b][:],
                                            acc_dram[128 * j:128 * j + 128, :]),
                                f"sFB{b}", 16)

                        @blk.vector
                        def _(v, j=j, b=b):
                            v.wait_ge(sems[f"sFB{b}"], 32 * (j // 2 + 1))
                            inc(v.tensor_tensor(x1b[b][:], x1b[b][:], accb[b][:],
                                                op=ALU.add), "sFIN")

                        @blk.sync
                        def _(sy, j=j, b=b):
                            sy.wait_ge(sems["sFIN"], j + 1)
                            inc(sy.dma_start(y_d[128 * j:128 * j + 128, :],
                                             x1b[b][:]), f"sY{b}", 16)

                    if taps:
                        @blk.sync
                        def _(sy):
                            sy.wait_ge(sems["sY0"], C["sY0"])
                            sy.wait_ge(sems["sY1"], C["sY1"])
                            for name, out in tap_outs.items():
                                src = scratch[name]
                                inc(sy.dma_start(out, src[:, :]), "sY0", 16)

                    @blk.gpsimd
                    def _(g):
                        g.wait_ge(sems["sY0"], C["sY0"])
                        g.wait_ge(sems["sY1"], C["sY1"])

    return nc


def _prep(inputs):
    f32 = np.float32
    g = lambda k: np.asarray(inputs[k], dtype=f32)
    cp = np.arange(DIM)
    i_ = cp % 32
    d_ = cp // 32
    pidx = (i_ % 16) * 128 + 2 * d_ + (i_ // 16)

    wq = g("L0_attn_Wq")[pidx, :]
    wk = g("L0_attn_Wk")[pidx, :]
    shg = g("L0_ffn_sh_gate")
    shu = g("L0_ffn_sh_up")
    shd = g("L0_ffn_sh_down")
    wg = g("L0_ffn_w_gate")
    wu = g("L0_ffn_w_up")
    wd = g("L0_ffn_w_down")

    upg = np.concatenate(
        [np.transpose(wg, (0, 2, 1)),
         np.transpose(shg, (1, 0)).reshape(DIM, 2, 1024).transpose(1, 0, 2)],
        axis=0)
    upu = np.concatenate(
        [np.transpose(wu, (0, 2, 1)),
         np.transpose(shu, (1, 0)).reshape(DIM, 2, 1024).transpose(1, 0, 2)],
        axis=0)
    dnw = np.concatenate(
        [np.transpose(wd, (0, 2, 1)),
         np.transpose(shd, (1, 0)).reshape(2, 1024, DIM)],
        axis=0)

    wm = {
        "wqT": np.ascontiguousarray(wq.T),
        "wkT": np.ascontiguousarray(wk.T),
        "wvT": np.ascontiguousarray(g("L0_attn_Wv").T),
        "woT": np.ascontiguousarray(g("L0_attn_Wo").T),
        "gwT": np.ascontiguousarray(
            np.concatenate([g("L0_ffn_gate_w"), g("L0_ffn_se_gate_w")], 0).T),
        "upg": np.ascontiguousarray(upg),
        "upu": np.ascontiguousarray(upu),
        "dnw": np.ascontiguousarray(dnw),
        "lama": np.ascontiguousarray(
            np.concatenate([g("L0_attn_lq1"), g("L0_attn_lq2")])[None, :]),
        "lamb": np.ascontiguousarray(
            np.concatenate([g("L0_attn_lk1"), g("L0_attn_lk2")])[None, :]),
        "bo": np.ascontiguousarray(g("L0_attn_bo")[None, :]),
        "ident": np.eye(128, dtype=f32),
    }
    c64 = np.zeros((1, 4), dtype=f32)
    for i, c in enumerate([LAM_INIT, LN_OML]):
        hi = np.float32(c)
        lo = np.float32(c - np.float64(hi))
        c64[0, 2 * i] = hi
        c64[0, 2 * i + 1] = lo
    wm["c64"] = c64
    x = np.ascontiguousarray(g("L0_attn_x"))
    return x, wm


def kernel(**inputs):
    x, wm = _prep(inputs)
    nc = build(nt=16)
    in_maps = []
    for c in range(NCORES):
        m = {"x": np.ascontiguousarray(x[c])}
        m.update(wm)
        in_maps.append(m)
    res = run_bass_kernel_spmd(nc, in_maps, core_ids=list(range(NCORES)))
    out = np.stack([np.asarray(r["y"], dtype=np.float32)
                    for r in res.results], axis=0)
    return out
